# version 07
# baseline (speedup 1.0000x reference)
import contextlib
import numpy as np
import ml_dtypes
import concourse.bass as bass
import concourse.mybir as mybir
from concourse.bass_utils import run_bass_kernel_spmd

F32 = mybir.dt.float32
BF16 = mybir.dt.bfloat16
AF = mybir.ActivationFunctionType
ALU = mybir.AluOpType

NCORES = 8
B, L, D = 4, 4096, 2048
T = 2048
NTT = T // 128
HC = D // 128
NE, I = 16, 1792
NIM = I // 128
NHB = D // 512
HTOK = T // 2
NTM = HTOK // 128
DIN = 4096
DXB = 1024
CONVD = 6144
NFC = CONVD // 128
FIN = 10304
NFB = 81
NH = 64
CH, CL = 8, 256
NG = 32
EPS = 1e-5

STAGE = 3
RGROUPS = [[0, 1], [2, 3], [4, 5], [6, 7]]



def _prep_core_inputs(inputs):
    f32 = np.float32
    gate = np.asarray(inputs["L0_ffn_gate_proj"], f32)
    up = np.asarray(inputs["L0_ffn_up_proj"], f32)
    down = np.asarray(inputs["L0_ffn_down_proj"], f32)
    W_in = np.asarray(inputs["L0_attn_W_in"], f32)
    W_out = np.asarray(inputs["L0_attn_W_out"], f32)
    router = np.asarray(inputs["L0_ffn_router_w"], f32)
    ebias = np.asarray(inputs["L0_ffn_expert_bias"], f32)
    conv_w = np.asarray(inputs["L0_attn_conv_w"], f32)
    conv_b = np.asarray(inputs["L0_attn_conv_b"], f32)
    dt_bias = np.asarray(inputs["L0_attn_dt_bias"], f32)
    A_log = np.asarray(inputs["L0_attn_A_log"], f32)
    Dv = np.asarray(inputs["L0_attn_D"], f32)
    nw = np.asarray(inputs["L0_attn_norm_weight"], f32)
    u = np.asarray(inputs["L0_attn_u"], f32).reshape(B * L, D)

    sh = {}
    sh["gate_blk"] = np.ascontiguousarray(
        gate.reshape(NE, HC, 128, NIM, 128).transpose(0, 3, 2, 1, 4).reshape(NE * NIM, 128, HC * 128))
    sh["up_blk"] = np.ascontiguousarray(
        up.reshape(NE, HC, 128, NIM, 128).transpose(0, 3, 2, 1, 4).reshape(NE * NIM, 128, HC * 128))
    sh["down_nat"] = np.ascontiguousarray(down.reshape(NE * NIM, 128, D))
    WinT = np.zeros((D, NFB * 128), f32)
    WinT[:, :FIN] = W_in.T
    sh["win_blk"] = np.ascontiguousarray(WinT.reshape(HC, 128, NFB, 128).transpose(2, 1, 0, 3))
    sh["wout_nat"] = np.ascontiguousarray(W_out.T.reshape(NG, 128, D))
    sh["router_t"] = np.ascontiguousarray(router.T.reshape(HC, 128, NE).transpose(1, 0, 2))
    sh["bias_row"] = np.ascontiguousarray(ebias.reshape(1, NE))
    sh["convw_t"] = np.ascontiguousarray(conv_w.reshape(NFC, 128, 4).transpose(1, 0, 2))
    sh["convb_t"] = np.ascontiguousarray(conv_b.reshape(NFC, 128).T)
    sh["dtb_col"] = np.ascontiguousarray(dt_bias.reshape(NH, 1))
    sh["alog_col"] = np.ascontiguousarray(A_log.reshape(NH, 1))
    sh["d_rep"] = np.ascontiguousarray(np.broadcast_to(Dv[None, :], (128, NH)).copy())
    sh["norm_t"] = np.ascontiguousarray(nw.reshape(NG, 128).T)
    sh["ident32"] = np.eye(128, dtype=f32)
    sh["identb"] = np.eye(128, dtype=f32).astype(ml_dtypes.bfloat16)
    sh["ones_row"] = np.ones((1, 128), f32)
    sh["ones_col"] = np.ones((128, 1), f32)
    sh["eps_col"] = np.full((128, 1), EPS, f32)
    tidx = np.arange(CL)[None, None, :]
    sidx = np.arange(128)[:, None, None] + 128 * np.arange(2)[None, :, None]
    sh["negbig"] = np.where(tidx >= sidx, 0.0, -1e9).astype(f32)

    per_core = []
    for c in range(NCORES):
        d = dict(sh)
        d["u_slice"] = np.ascontiguousarray(u[c * T:(c + 1) * T, :])
        half = c % 2
        d["fsend"] = np.full((128, 1), 1.0 if half == 0 else 0.0, f32)
        d["frecv"] = np.full((128, 1), 1.0 if half == 1 else 0.0, f32)
        per_core.append(d)
    return per_core



class Sem:
    __slots__ = ("h", "v", "name")

    def __init__(self, es, nc, name):
        self.h = es.enter_context(nc.semaphore(name))
        self.v = 0
        self.name = name

    def inc(self, ins, n=1):
        ins.then_inc(self.h, n)
        self.v += n
        return self.v


class Arena:
    def __init__(self, ap_u8, nbytes):
        self.ap = ap_u8
        self.nbytes = nbytes
        self.off = 0
        self.marks = []

    def push(self):
        self.marks.append(self.off)

    def pop(self):
        self.off = self.marks.pop()

    def _view(self, off, shape, dtype):
        esz = 2 if dtype == BF16 else 4
        n = 1
        for s in shape[1:]:
            n *= s
        nb = n * esz
        assert off + nb <= self.nbytes, f"arena overflow: {off}+{nb}>{self.nbytes}"
        v = self.ap[:, off:off + nb].bitcast(dtype)
        if len(shape) > 2:
            dims = shape[1:]
            pat_in = "p (" + " ".join(f"d{i}" for i in range(len(dims))) + ")"
            pat_out = "p " + " ".join(f"d{i}" for i in range(len(dims)))
            v = v.rearrange(pat_in + " -> " + pat_out, **{f"d{i}": dims[i] for i in range(len(dims))})
        if shape[0] != 128:
            v = v[0:shape[0]]
        return v, off + nb

    def alloc(self, shape, dtype):
        off = (self.off + 3) & ~3
        v, end = self._view(off, shape, dtype)
        self.off = end
        self.last_off = off
        return v

    def at(self, off_bytes, shape, dtype):
        v, _ = self._view(off_bytes, shape, dtype)
        return v


class Ctx:
    pass


def fence(cx, engines=("sync", "tensor", "vector", "scalar", "gpsimd")):
    b = cx.block
    sems = ([cx.s_pe, cx.s_ev, cx.s_sc, cx.s_ldc, cx.s_wd, cx.s_cc] + cx.s_st + cx.s_ld + cx.s_ld2 +
            cx.s_wg + cx.s_wu + cx.s_gx + getattr(cx, 's_ldx', []) + getattr(cx, 's_ldw', []) +
            getattr(cx, 's_stc', []) + getattr(cx, 's_ldcg', []) + getattr(cx, 's_lda', []) +
            getattr(cx, 's_cin', []) + getattr(cx, 's_wcv', []) + [x for x in [getattr(cx, 's_dt', None), getattr(cx, 's_po', None)] if x])
    vals = [(s.h, s.v) for s in sems if s.v > 0]

    for name in engines:
        @getattr(b, name)
        def _(e, vals=vals):
            for h, v in vals:
                e.wait_ge(h, v)



def build_program(stage=None):
    if stage is None:
        stage = STAGE
    nc = bass.Bass("TRN2", target_bir_lowering=False)
    cx = Ctx()
    cx.nc = nc
    cx.stage = stage

    def din(name, shape, dtype=F32):
        return nc.dram_tensor(name, shape, dtype, kind="ExternalInput").ap()

    cx.u = din("u_slice", [T, D])
    cx.gate_blkd = din("gate_blk", [NE * NIM, 128, HC * 128])
    cx.up_blkd = din("up_blk", [NE * NIM, 128, HC * 128])
    cx.down_natd = din("down_nat", [NE * NIM, 128, D])
    cx.win_blkd = din("win_blk", [NFB, 128, HC, 128])
    cx.wout_natd = din("wout_nat", [NG, 128, D])
    cx.router_td = din("router_t", [128, HC, NE])
    cx.bias_rowd = din("bias_row", [1, NE])
    cx.convw_td = din("convw_t", [128, NFC, 4])
    cx.convb_td = din("convb_t", [128, NFC])
    cx.dtb_cold = din("dtb_col", [NH, 1])
    cx.alog_cold = din("alog_col", [NH, 1])
    cx.d_repd = din("d_rep", [128, NH])
    cx.norm_td = din("norm_t", [128, NG])
    cx.ident32d = din("ident32", [128, 128])
    cx.identbd = din("identb", [128, 128], BF16)
    cx.ones_rowd = din("ones_row", [1, 128])
    cx.ones_cold = din("ones_col", [128, 1])
    cx.eps_cold = din("eps_col", [128, 1])
    cx.negbigd = din("negbig", [128, 2, CL])
    cx.fsendd = din("fsend", [128, 1])
    cx.frecvd = din("frecv", [128, 1])

    cx.out = nc.dram_tensor("out", [T, D], F32, kind="ExternalOutput").ap()

    def dsc(name, shape, dtype):
        return nc.dram_tensor(name, shape, dtype).ap()

    cx.xln1 = dsc("xln1", [T, D], BF16)
    cx.x1 = dsc("x1", [T, D], F32)
    cx.xln2 = dsc("xln2", [T, D], BF16)
    cx.x2 = dsc("x2", [T, D], F32)
    cx.xbc_raw = dsc("xbc_raw", [NFC, 128, T], BF16)
    cx.z_raw = dsc("z_raw", [NG, 128, T], BF16)
    cx.xbc_conv = dsc("xbc_conv", [NFC, 128, T], BF16)
    cx.acs_dram = dsc("acs_d", [NH, T], F32)
    cx.xbt_dram = dsc("xbt_dram2", [16, 128, 2048], BF16)
    cx.expacs_dram = dsc("expacs_d", [NH, T], BF16)
    cx.dall_dram = dsc("dall_d", [1, NH * CH], F32)
    cx.wout_bfd = dsc("wout_bf", [NG, 128, D], BF16)
    cx.ex1_in = dsc("ex1_in", [128, NFC * 3], F32)
    cx.ex1_out = dsc("ex1_out", [128, NFC * 3], F32)
    cx.ex2_in = dsc("ex2_in", [128, DIN], F32)
    cx.ex2_out = dsc("ex2_out", [128, DIN], F32)

    es = contextlib.ExitStack()
    cx.es = es
    es.__enter__()

    sbt = lambda name, shape, dt=F32: es.enter_context(nc.sbuf_tensor(name, shape, dt))
    cx.ident32 = sbt("c_id32", [128, 128])
    cx.identb = sbt("c_idb", [128, 128], BF16)
    cx.ones_row = sbt("c_ones_r", [1, 128])
    cx.ones_col = sbt("c_ones_c", [128, 1])
    cx.eps_col = sbt("c_eps_c", [128, 1])
    cx.router_t = sbt("c_router", [128, HC, NE])
    cx.biasr = sbt("c_biasr", [1, NE])
    cx.alpha = sbt("alpha", [128, NTT, NE])
    cx.dtbc = sbt("c_dtbc", [NH, 1])
    cx.alogc = sbt("c_alogc", [NH, 1])
    cx.d_rep = sbt("c_drep", [128, NH])
    cx.norm_sb = sbt("c_norm", [128, NG])
    cx.flags = sbt("c_flags", [128, 2])
    cx.statw = sbt("statw", [128, 8])
    cx.acol = sbt("acolA", [NH, 1])

    AR_BYTES = 206848
    cx.arena_t = es.enter_context(nc.sbuf_tensor("arena", [128, AR_BYTES], mybir.dt.uint8))
    cx.ar = Arena(cx.arena_t.ap(), AR_BYTES)

    cx.psum_t = es.enter_context(nc.psum_tensor("psum", [128, 4096], F32))
    cx.psum = cx.psum_t.ap()

    S = lambda n: Sem(es, nc, n)
    cx.s_ldc = S("s_ldc")
    cx.s_ld = [S("s_ld0"), S("s_ld1")]
    cx.s_st = [S("s_st0"), S("s_st1")]
    cx.s_ld2 = [S("s_ld2a"), S("s_ld2b")]
    cx.s_wg = [S("s_wg0"), S("s_wg1")]
    cx.s_wu = [S("s_wu0"), S("s_wu1")]
    cx.s_wd = S("s_wd")
    cx.s_gx = [S("s_gx0"), S("s_gx1")]
    cx.s_pe = S("s_pe")
    cx.s_ev = S("s_ev")
    cx.s_sc = S("s_sc")
    cx.s_po = S("s_po")
    cx.s_cc = S("s_cc")
    cx.s_ldx = [S("s_ldx0"), S("s_ldx1")]
    cx.s_ldw = [S("s_ldw0"), S("s_ldw1")]
    cx.s_stc = [S("s_stc0"), S("s_stc1")]
    cx.s_ldcg = [S("s_ldcg0"), S("s_ldcg1"), S("s_ldcg2"), S("s_ldcg3")]
    cx.s_lda = [S("s_lda0"), S("s_lda1")]
    cx.s_dt = S("s_dt")
    cx.s_wcv = [S("s_wcv0"), S("s_wcv1")]
    cx.s_cin = [S("s_cin0"), S("s_cin1")]

    cx.block = es.enter_context(nc.Block())

    emit_const_loads(cx)
    emit_ln_router(cx, src=cx.u, xln=cx.xln1)
    emit_ffn(cx, xln=cx.xln1, resid=cx.u, dst=(cx.x1 if stage >= 2 else cx.out))
    if stage >= 2:
        emit_attn(cx)
    if stage == 2:
        bb = cx.block

        @bb.sync
        def _(sy):
            sy.wait_ge(cx.s_st[0].h, cx.s_st[0].v)
            sy.wait_ge(cx.s_st[1].h, cx.s_st[1].v)
            cx.s_st[0].inc(sy.dma_start(out=cx.out, in_=cx.x2), 16)
    if stage >= 3:
        emit_ln_router(cx, src=cx.x2, xln=cx.xln2)
        emit_ffn(cx, xln=cx.xln2, resid=cx.x2, dst=cx.out)
    emit_final_wait(cx)
    return cx



def preload_nlexp(cx, eng):
    from concourse.hw_specs import get_activation_tables
    import concourse.mybir as _mb
    for idx, (name, funcs) in enumerate(get_activation_tables(cx.nc.m.arch).items()):
        if name == "natural_log_exp_and_others":
            return eng.add_instruction(
                _mb.InstLoadActFuncSet(
                    name=cx.nc.get_next_instruction_name(),
                    act_func_set_id=idx, ins=[], outs=[]))
    raise ValueError


def emit_const_loads(cx):
    b = cx.block
    s = cx.s_ldc

    @b.sync
    def _(sy):
        pairs = [
            (cx.ident32[:, :], cx.ident32d), (cx.identb[:, :], cx.identbd),
            (cx.ones_row[:, :], cx.ones_rowd), (cx.ones_col[:, :], cx.ones_cold),
            (cx.eps_col[:, :], cx.eps_cold),
            (cx.router_t[:, :, :], cx.router_td),
            (cx.biasr[:, :], cx.bias_rowd),
            (cx.dtbc[:, :], cx.dtb_cold),
            (cx.alogc[:, :], cx.alog_cold), (cx.d_rep[:, :], cx.d_repd),
            (cx.norm_sb[:, :], cx.norm_td),
            (cx.flags[:, 0:1], cx.fsendd), (cx.flags[:, 1:2], cx.frecvd),
        ]
        for dst, src in pairs:
            s.inc(sy.dma_start(out=dst, in_=src), 16)
    cx.c_const = s.v



def emit_ln_router(cx, src, xln):
    b, ar, ps = cx.block, cx.ar, cx.psum
    fence(cx)
    ar.push()
    xt = ar.alloc([128, 2, D], F32)
    scr32 = ar.alloc([128, 2, D], F32)
    xlnb = ar.alloc([128, 2, D], BF16)
    xf32 = ar.alloc([128, 2, 128], F32)
    jnkA = ar.alloc([128, D], F32)
    jnkB = ar.alloc([128, D], F32)
    logits = ar.alloc([128, NTT, NE], F32)
    lwork = ar.alloc([128, NTT, NE], F32)
    lwork2 = ar.alloc([128, NTT, NE], F32)
    awork = ar.alloc([128, NTT, 2], F32)

    s_ld, s_st = cx.s_ld, cx.s_st
    s_pe, s_ev, s_sc = cx.s_pe, cx.s_ev, cx.s_sc
    pslot = [ps[:, 2048:2176], ps[:, 2560:2688]]
    plog = ps[:, 3584:3600]

    ld_done = [None] * NTT
    st_done = [None] * NTT
    sc_sq = [None] * NTT
    sc_apply = [None] * NTT
    ev_stats = [None] * NTT
    ev_cast = [None] * NTT
    ev_evac = [[None] * HC for _ in range(NTT)]
    pe_tr = [[None] * HC for _ in range(NTT)]
    pe_mm = [None] * NTT
    lg = [None] * NTT

    M, SQ, NV, RS, NM, RST, NMR = 0, 1, 2, 3, 4, 5, 6

    for tt in range(NTT):
        par = tt % 2

        @b.sync
        def _(sy, tt=tt, par=par):
            if tt >= 2:
                sy.wait_ge(s_sc.h, sc_apply[tt - 2])
            ld_done[tt] = s_ld[par].inc(
                sy.dma_start(out=xt[:, par, :], in_=src[tt * 128:(tt + 1) * 128, :]), 16)

        @b.scalar
        def _(s, tt=tt, par=par):
            s.wait_ge(s_ld[par].h, ld_done[tt])
            if tt >= 2:
                s.wait_ge(s_pe.h, pe_tr[tt - 2][HC - 1])
            sc_sq[tt] = s_sc.inc(
                s.activation(jnkB[:, :], xt[:, par, :], AF.Square,
                             accum_out=cx.statw[:, SQ:SQ + 1]), 1)

        @b.vector
        def _(v, tt=tt, par=par):
            v.wait_ge(s_ld[par].h, ld_done[tt])
            acv = s_ev.inc(v.tensor_scalar(jnkA[:, :], xt[:, par, :], 1.0, None, ALU.mult,
                                           ALU.add, accum_out=cx.statw[:, M:M + 1]), 1)
            v.wait_ge(s_ev.h, acv)
            v.wait_ge(s_sc.h, sc_sq[tt])

            def vstep(ins):
                v.wait_ge(s_ev.h, s_ev.inc(ins, 1))

            vstep(v.tensor_scalar(cx.statw[:, M:M + 1], cx.statw[:, M:M + 1],
                                  1.0 / D, None, ALU.mult))
            vstep(v.tensor_scalar(cx.statw[:, SQ:SQ + 1], cx.statw[:, SQ:SQ + 1],
                                  1.0 / D, None, ALU.mult))
            vstep(v.scalar_tensor_tensor(cx.statw[:, NV:NV + 1], cx.statw[:, M:M + 1],
                                         cx.statw[:, M:M + 1], cx.statw[:, SQ:SQ + 1],
                                         ALU.mult, ALU.subtract))
            vstep(v.tensor_scalar(cx.statw[:, NV:NV + 1], cx.statw[:, NV:NV + 1], -1.0, EPS,
                                  ALU.mult, ALU.add))
            vstep(v.tensor_scalar(cx.statw[:, NM:NM + 1], cx.statw[:, M:M + 1], -1.0, None, ALU.mult))
            ev_stats[tt] = s_ev.inc(v.reciprocal(cx.statw[:, RS:RS + 1], cx.statw[:, NV:NV + 1]), 1)

        @b.scalar
        def _(s, tt=tt, par=par):
            s.wait_ge(s_ev.h, ev_stats[tt])
            sv = s_sc.inc(s.activation(cx.statw[:, RST:RST + 1], cx.statw[:, RS:RS + 1], AF.Sqrt), 1)
            s.wait_ge(s_sc.h, sv)
            sv2 = s_sc.inc(s.activation(cx.statw[:, NMR:NMR + 1], cx.statw[:, NM:NM + 1], AF.Identity,
                                        scale=cx.statw[:, RST:RST + 1]), 1)
            s.wait_ge(s_sc.h, sv2)
            sc_apply[tt] = s_sc.inc(
                s.activation(scr32[:, par, :], xt[:, par, :], AF.Identity,
                             scale=cx.statw[:, RST:RST + 1], bias=cx.statw[:, NMR:NMR + 1]), 1)

        @b.vector
        def _(v, tt=tt, par=par):
            v.wait_ge(s_sc.h, sc_apply[tt])
            if tt >= 2:
                v.wait_ge(s_st[par].h, st_done[tt - 2])
            ev_cast[tt] = s_ev.inc(v.tensor_copy(xlnb[:, par, :], scr32[:, par, :]), 1)

        @b.sync
        def _(sy, tt=tt, par=par):
            sy.wait_ge(s_ev.h, ev_cast[tt])
            st_done[tt] = s_st[par].inc(
                sy.dma_start(out=xln[tt * 128:(tt + 1) * 128, :], in_=xlnb[:, par, :]), 16)

        peb = s_pe.v
        evb = s_ev.v
        for hc in range(HC):
            pe_tr[tt][hc] = peb + hc + 1
            ev_evac[tt][hc] = evb + hc + 1
        pe_mm[tt] = peb + HC + 1
        lg[tt] = evb + HC + 1

        @b.tensor
        def _(t, tt=tt, par=par):
            if tt == 0:
                t.wait_ge(cx.s_ldc.h, cx.c_const)
            t.wait_ge(s_sc.h, sc_apply[tt])
            for hc in range(HC):
                sl = hc % 2
                if hc >= 2:
                    t.wait_ge(s_ev.h, ev_evac[tt][hc - 2])
                elif tt >= 1:
                    t.wait_ge(s_ev.h, ev_evac[tt - 1][HC - 2 + hc])
                t.transpose(pslot[sl], scr32[:, par, hc * 128:(hc + 1) * 128],
                            cx.ident32[:, :]).then_inc(s_pe.h, 1)
                s_pe.v += 1
                if hc >= 1:
                    t.wait_ge(s_ev.h, ev_evac[tt][hc - 1])
                    if hc == 1 and tt >= 1:
                        t.wait_ge(s_ev.h, lg[tt - 1])
                    t.matmul(plog, xf32[:, (hc - 1) % 2, :], cx.router_t[:, hc - 1, :],
                             start=(hc == 1), stop=False)
            t.wait_ge(s_ev.h, ev_evac[tt][HC - 1])
            t.matmul(plog, xf32[:, (HC - 1) % 2, :], cx.router_t[:, HC - 1, :],
                     start=False, stop=False)
            t.matmul(plog, cx.ones_row[:, :], cx.biasr[:, :],
                     start=False, stop=True).then_inc(s_pe.h, 1)
            s_pe.v += 1

        @b.vector
        def _(v, tt=tt, par=par):
            for hc in range(HC):
                sl = hc % 2
                v.wait_ge(s_pe.h, pe_tr[tt][hc])
                v.tensor_copy(xf32[:, sl, :], pslot[sl]).then_inc(s_ev.h, 1)
                s_ev.v += 1
            v.wait_ge(s_pe.h, pe_mm[tt])
            v.tensor_copy(logits[:, tt, :], plog).then_inc(s_ev.h, 1)
            s_ev.v += 1

        assert s_pe.v == pe_mm[tt] and s_ev.v == lg[tt]

    BIG = 1e9
    m_ = awork[:, :, 0:1]
    d_ = awork[:, :, 1:2]
    mbc = awork[:, :, 0:1].broadcast_to([128, NTT, NE])
    dbc = awork[:, :, 1:2].broadcast_to([128, NTT, NE])
    sc_exp = [None]
    ev_pre = [None]

    lg_all = s_ev.v

    @b.vector
    def _(v):
        def vstep(ins):
            v.wait_ge(s_ev.h, s_ev.inc(ins, 1))

        v.wait_ge(s_ev.h, lg_all)
        vstep(v.tensor_copy(lwork[:, :, :], logits[:, :, :]))
        vstep(v.tensor_reduce(m_, lwork[:, :, :], mybir.AxisListType.X, ALU.max))
        for r in range(3):
            vstep(v.tensor_tensor(lwork2[:, :, :], lwork[:, :, :], mbc, ALU.is_ge))
            vstep(v.tensor_scalar(lwork2[:, :, :], lwork2[:, :, :], -BIG, None, ALU.mult))
            vstep(v.tensor_tensor(lwork[:, :, :], lwork[:, :, :], lwork2[:, :, :], ALU.add))
            vstep(v.tensor_reduce(m_, lwork[:, :, :], mybir.AxisListType.X, ALU.max))
        vstep(v.tensor_tensor(lwork2[:, :, :], logits[:, :, :], mbc, ALU.is_ge))
        vstep(v.tensor_reduce(d_, logits[:, :, :], mybir.AxisListType.X, ALU.max))
        ev_pre[0] = s_ev.inc(
            v.tensor_tensor(lwork[:, :, :], logits[:, :, :], dbc, ALU.subtract), 1)

    @b.scalar
    def _(s):
        s.wait_ge(s_ev.h, ev_pre[0])
        sc_exp[0] = s_sc.inc(s.activation(lwork[:, :, :], lwork[:, :, :], AF.Exp), 1)

    @b.vector
    def _(v):
        def vstep(ins):
            v.wait_ge(s_ev.h, s_ev.inc(ins, 1))

        v.wait_ge(s_sc.h, sc_exp[0])
        vstep(v.tensor_tensor(lwork[:, :, :], lwork[:, :, :], lwork2[:, :, :], ALU.mult))
        vstep(v.tensor_reduce(d_, lwork[:, :, :], mybir.AxisListType.X, ALU.add))
        vstep(v.reciprocal(d_, d_))
        vstep(v.tensor_tensor(cx.alpha[:, :, :], lwork[:, :, :], dbc, ALU.mult))
        cx.alpha_done = s_ev.inc(
            v.tensor_scalar(cx.alpha[:, :, :], cx.alpha[:, :, :], 0.5, None, ALU.mult), 1)

    ar.pop()



def emit_ffn(cx, xln, resid, dst):
    b, ar, ps = cx.block, cx.ar, cx.psum
    fence(cx)
    ar.push()
    xlf = ar.alloc([128, HC, HTOK], BF16)
    acc = ar.alloc([128, NTM, D], F32)
    act = ar.alloc([128, NIM, HTOK], BF16)
    act_off0 = ar.last_off
    wg = ar.alloc([128, 2, HC, 128], BF16)
    wu = ar.alloc([128, 2, HC, 128], BF16)
    wd = ar.alloc([128, NIM, D], BF16)
    wd_off = ar.last_off
    xtile = ar.at(act_off0, [128, 2, D], BF16)
    rt = ar.at(wd_off, [128, 2, D], F32)
    ot = ar.at(wd_off + 16384, [128, 2, D], F32)

    s_ld, s_st = cx.s_ld, cx.s_st
    s_pe, s_ev, s_sc = cx.s_pe, cx.s_ev, cx.s_sc
    pall = ps.rearrange("p (a b) -> p a b", a=2)
    ptrb = [ps[:, 0:1024].bitcast(BF16).rearrange("p (a b) -> p a b", a=HC),
            ps[:, 1024:2048].bitcast(BF16).rearrange("p (a b) -> p a b", a=HC)]

    for hf in range(2):
        t0 = hf * HTOK
        ld_done = [None] * NTM
        tr_done = [None] * NTM
        evx = [None] * NTM
        pe_frontier = s_pe.v
        ev_frontier = s_ev.v
        for tm in range(NTM):
            par = tm % 2

            @b.sync
            def _(sy, tm=tm, par=par):
                if tm >= 2:
                    sy.wait_ge(s_pe.h, tr_done[tm - 2])
                else:
                    sy.wait_ge(s_pe.h, pe_frontier)
                ld_done[tm] = s_ld[par].inc(
                    sy.dma_start(out=xtile[:, par, :],
                                 in_=xln[t0 + tm * 128:t0 + (tm + 1) * 128, :]), 16)

            @b.tensor
            def _(t, tm=tm, par=par):
                t.wait_ge(s_ld[par].h, ld_done[tm])
                if tm >= 2:
                    t.wait_ge(s_ev.h, evx[tm - 2])
                else:
                    t.wait_ge(s_ev.h, ev_frontier)
                for hc in range(HC):
                    ins = t.transpose(ptrb[par][:, hc, :],
                                      xtile[:, par, hc * 128:(hc + 1) * 128],
                                      cx.identb[:, :])
                tr_done[tm] = s_pe.inc(ins, 1)

            @b.vector
            def _(v, tm=tm, par=par):
                v.wait_ge(s_pe.h, tr_done[tm])
                evx[tm] = s_ev.inc(
                    v.tensor_copy(xlf[:, :, tm * 128:(tm + 1) * 128], ptrb[par][:, :, :]), 1)

        xlf_ready = s_ev.v

        pe_base = s_pe.v
        ev_base = s_ev.v
        sc_base = s_sc.v
        pe_gu = [None] * (NE * NIM)
        pe_dn = [[None] * NTM for _ in range(NE)]
        ev_act = [None] * (NE * NIM)
        ev_stt = [[None] * NTM for _ in range(NE)]
        sc_silu = [None] * (NE * NIM)
        kk = 0
        for e in range(NE):
            for im in range(NIM):
                pe_gu[e * NIM + im] = pe_base + kk + 1
                ev_act[e * NIM + im] = ev_base + kk + 1
                kk += 1
            for tm in range(NTM):
                pe_dn[e][tm] = pe_base + kk + 1
                ev_stt[e][tm] = ev_base + kk + 1
                kk += 1
        for k in range(NE * NIM):
            sc_silu[k] = sc_base + k + 1

        w_done = [None] * (NE * NIM)
        wd_done = [None] * NE

        pe_entry = pe_base

        @b.gpsimd
        def _(g):
            for e in range(NE):
                for im in range(NIM):
                    k = e * NIM + im
                    par = k % 2
                    if k >= 2:
                        g.wait_ge(s_pe.h, pe_gu[k - 2])
                    else:
                        g.wait_ge(s_pe.h, pe_entry)
                    w_done[k] = cx.s_wg[par].inc(
                        g.dma_start(out=wg[:, par, :, :], in_=cx.gate_blkd[e * NIM + im]), 16)
                    cx.s_wu[par].inc(
                        g.dma_start(out=wu[:, par, :, :], in_=cx.up_blkd[e * NIM + im]), 16)
                if e >= 1:
                    g.wait_ge(s_pe.h, pe_dn[e - 1][NTM - 1])
                else:
                    g.wait_ge(s_st[0].h, s_st[0].v)
                    g.wait_ge(s_st[1].h, s_st[1].v)
                for ic in range(NIM):
                    wd_done[e] = cx.s_wd.inc(
                        g.dma_start(out=wd[:, ic, :], in_=cx.down_natd[e * NIM + ic]), 16)

        @b.tensor
        def _(t):
            t.wait_ge(s_ev.h, xlf_ready)
            for e in range(NE):
                for im in range(NIM):
                    k = e * NIM + im
                    par = k % 2
                    t.wait_ge(cx.s_wg[par].h, w_done[k])
                    t.wait_ge(cx.s_wu[par].h, w_done[k])
                    if k >= 2:
                        t.wait_ge(s_ev.h, ev_act[k - 2])
                    if im < 2 and e >= 1:
                        t.wait_ge(s_ev.h, ev_stt[e - 1][NTM - 2 + par])
                    for hc in range(HC):
                        t.matmul(pall[:, par, 0:512], wg[:, par, hc, :], xlf[:, hc, 0:512],
                                 start=(hc == 0), stop=(hc == HC - 1))
                        t.matmul(pall[:, par, 512:1024], wg[:, par, hc, :], xlf[:, hc, 512:1024],
                                 start=(hc == 0), stop=(hc == HC - 1))
                    for hc in range(HC):
                        t.matmul(pall[:, par, 1024:1536], wu[:, par, hc, :], xlf[:, hc, 0:512],
                                 start=(hc == 0), stop=(hc == HC - 1))
                        ins = t.matmul(pall[:, par, 1536:2048], wu[:, par, hc, :],
                                       xlf[:, hc, 512:1024],
                                       start=(hc == 0), stop=(hc == HC - 1))
                    ins.then_inc(s_pe.h, 1)
                    s_pe.v += 1
                t.wait_ge(cx.s_wd.h, wd_done[e])
                t.wait_ge(s_ev.h, ev_act[e * NIM + NIM - 1])
                for tm in range(NTM):
                    par = tm % 2
                    if tm >= 2:
                        t.wait_ge(s_ev.h, ev_stt[e][tm - 2])
                    elif e >= 1:
                        t.wait_ge(s_ev.h, ev_stt[e - 1][NTM - 2 + tm])
                    for ic in range(NIM):
                        for hb in range(NHB):
                            ins = t.matmul(pall[:, par, hb * 512:(hb + 1) * 512],
                                           act[:, ic, tm * 128:(tm + 1) * 128],
                                           wd[:, ic, hb * 512:(hb + 1) * 512],
                                           start=(ic == 0), stop=(ic == NIM - 1))
                    ins.then_inc(s_pe.h, 1)
                    s_pe.v += 1

        @b.scalar
        def _(s):
            for e in range(NE):
                for im in range(NIM):
                    k = e * NIM + im
                    par = k % 2
                    s.wait_ge(s_pe.h, pe_gu[k])
                    s.activation(act[:, im, :], pall[:, par, 0:1024], AF.Silu).then_inc(s_sc.h, 1)
                    s_sc.v += 1

        @b.vector
        def _(v):
            v.wait_ge(s_ev.h, cx.alpha_done)
            for e in range(NE):
                for im in range(NIM):
                    k = e * NIM + im
                    par = k % 2
                    v.wait_ge(s_sc.h, sc_silu[k])
                    v.tensor_tensor(act[:, im, :], act[:, im, :],
                                    pall[:, par, 1024:2048], ALU.mult).then_inc(s_ev.h, 1)
                    s_ev.v += 1
                for tm in range(NTM):
                    par = tm % 2
                    v.wait_ge(s_pe.h, pe_dn[e][tm])
                    aco = cx.alpha[:, hf * NTM + tm, e:e + 1]
                    for hb in range(NHB):
                        dstv = acc[:, tm, hb * 512:(hb + 1) * 512]
                        srcv = pall[:, par, hb * 512:(hb + 1) * 512]
                        if e == 0:
                            ins = v.tensor_scalar(dstv, srcv, aco, None, ALU.mult)
                        else:
                            ins = v.scalar_tensor_tensor(dstv, srcv, aco, dstv, ALU.mult, ALU.add)
                    ins.then_inc(s_ev.h, 1)
                    s_ev.v += 1

        assert s_pe.v == pe_base + NE * (NIM + NTM)
        assert s_ev.v == ev_base + NE * (NIM + NTM)
        assert s_sc.v == sc_base + NE * NIM

        rld = [None] * NTM
        evo = [None] * NTM
        sto = [None] * NTM
        pe_all_done = s_pe.v
        for tm in range(NTM):
            par = tm % 2

            @b.sync
            def _(sy, tm=tm, par=par):
                if tm >= 2:
                    sy.wait_ge(s_ev.h, evo[tm - 2])
                else:
                    sy.wait_ge(s_pe.h, pe_all_done)
                rld[tm] = s_ld[par].inc(
                    sy.dma_start(out=rt[:, par, :],
                                 in_=resid[t0 + tm * 128:t0 + (tm + 1) * 128, :]), 16)

            @b.vector
            def _(v, tm=tm, par=par):
                v.wait_ge(s_ld[par].h, rld[tm])
                v.wait_ge(s_ev.h, ev_stt[NE - 1][tm])
                if tm >= 2:
                    v.wait_ge(s_st[par].h, sto[tm - 2])
                evo[tm] = s_ev.inc(
                    v.tensor_tensor(ot[:, par, :], acc[:, tm, :], rt[:, par, :], ALU.add), 1)

            @b.sync
            def _(sy, tm=tm, par=par):
                sy.wait_ge(s_ev.h, evo[tm])
                sto[tm] = s_st[par].inc(
                    sy.dma_start(out=dst[t0 + tm * 128:t0 + (tm + 1) * 128, :],
                                 in_=ot[:, par, :]), 16)
    ar.pop()



def emit_attn(cx):
    b, ar, ps = cx.block, cx.ar, cx.psum
    fence(cx)
    ar.push()
    cx.negbig = ar.alloc([128, 2, CL], F32)
    cx.convw = ar.alloc([128, NFC, 4], F32)
    cx.convb = ar.alloc([128, NFC], F32)
    cx.dbcc = ar.alloc([128, CH, NH], F32)
    cx.ddTf = ar.alloc([128, 16, NH], F32)
    cx.biasTf = ar.alloc([128, 16, NH], F32)

    s_ld, s_st, s_ld2 = cx.s_ld, cx.s_st, cx.s_ld2
    s_pe, s_ev, s_sc, s_cc = cx.s_pe, cx.s_ev, cx.s_sc, cx.s_cc
    s_gx, s_ldx, s_ldw, s_stc, s_ldcg = cx.s_gx, cx.s_ldx, cx.s_ldw, cx.s_stc, cx.s_ldcg

    @b.sync
    def _(sy):
        for dst, src in [(cx.negbig[:, :, :], cx.negbigd), (cx.convw[:, :, :], cx.convw_td),
                         (cx.convb[:, :], cx.convb_td)]:
            cx.s_ldc.inc(sy.dma_start(out=dst, in_=src), 16)
    cx.attn_const = cx.s_ldc.v

    ar.push()
    dtA = ar.alloc([NH, T], F32)
    dtB = ar.alloc([NH, T], F32)
    dtC = ar.alloc([NH, T], F32)
    dtD = ar.alloc([NH, T], F32)
    dcol = ar.alloc([NH, CH], F32)
    drow = ar.alloc([1, NH * CH], F32)
    uaf = ar.alloc([128, HC, T], BF16)
    uaf_off = ar.last_off
    xt = ar.alloc([128, 2, D], F32)
    xlnb = ar.alloc([128, 2, D], BF16)
    xlnb_off = ar.last_off
    wbuf = ar.alloc([128, 2, HC, 128], BF16)
    bounce = ar.alloc([128, 2, D], BF16)
    bounce_off = ar.last_off
    ex1sb = ar.alloc([128, NFC, 3], F32)
    tailr = ar.alloc([128, NFC * 3], F32)
    cin = ar.alloc([128, 2, T + 3], BF16)
    cv = ar.alloc([128, 2, T], F32)
    cv_off = ar.last_off
    cout = ar.alloc([128, 2, T], BF16)
    cout_off = ar.last_off
    wcv = ar.alloc([128, 2, D], BF16)
    jnkB = ar.at(cout_off, [128, D], F32)
    jnkA = ar.at(cv_off, [128, D], F32)
    jnkdt = ar.at(xlnb_off, [128, D], F32)

    M, SQ, NV, RS, NM, RST, NMR = 0, 1, 2, 3, 4, 5, 6
    pslotb = [ps[:, 0:1024].bitcast(BF16).rearrange("p (a b) -> p a b", a=HC),
              ps[:, 1024:2048].bitcast(BF16).rearrange("p (a b) -> p a b", a=HC)]

    ld_done = [None] * NTT
    sc_apply = [None] * NTT
    tr_done = [None] * NTT
    evx = [None] * NTT
    ev_entry0 = s_ev.v
    for tt in range(NTT):
        par = tt % 2

        @b.sync
        def _(sy, tt=tt, par=par):
            if tt >= 2:
                sy.wait_ge(s_sc.h, sc_apply[tt - 2])
            ld_done[tt] = s_ld[par].inc(
                sy.dma_start(out=xt[:, par, :], in_=cx.x1[tt * 128:(tt + 1) * 128, :]), 16)

        @b.scalar
        def _(s, tt=tt, par=par):
            s.wait_ge(s_ld[par].h, ld_done[tt])
            if tt >= 2:
                s.wait_ge(s_pe.h, tr_done[tt - 2])
            s.activation(jnkB[:, :], xt[:, par, :], AF.Square,
                         accum_out=cx.statw[:, SQ:SQ + 1]).then_inc(s_sc.h, 1)
            s_sc.v += 1
        sc_sq = s_sc.v

        @b.vector
        def _(v, tt=tt, par=par, sc_sq=sc_sq):
            def vstep(ins):
                v.wait_ge(s_ev.h, s_ev.inc(ins, 1))

            v.wait_ge(s_ld[par].h, ld_done[tt])
            vstep(v.tensor_scalar(jnkA[:, :], xt[:, par, :], 1.0, None, ALU.mult,
                                  ALU.add, accum_out=cx.statw[:, M:M + 1]))
            v.wait_ge(s_sc.h, sc_sq)
            vstep(v.tensor_scalar(cx.statw[:, M:M + 1], cx.statw[:, M:M + 1],
                                  1.0 / D, None, ALU.mult))
            vstep(v.tensor_scalar(cx.statw[:, SQ:SQ + 1], cx.statw[:, SQ:SQ + 1],
                                  1.0 / D, None, ALU.mult))
            vstep(v.scalar_tensor_tensor(cx.statw[:, NV:NV + 1], cx.statw[:, M:M + 1],
                                         cx.statw[:, M:M + 1], cx.statw[:, SQ:SQ + 1],
                                         ALU.mult, ALU.subtract))
            vstep(v.tensor_scalar(cx.statw[:, NV:NV + 1], cx.statw[:, NV:NV + 1], -1.0, EPS,
                                  ALU.mult, ALU.add))
            vstep(v.tensor_scalar(cx.statw[:, NM:NM + 1], cx.statw[:, M:M + 1], -1.0, None,
                                  ALU.mult))
            s_ev.inc(v.reciprocal(cx.statw[:, RS:RS + 1], cx.statw[:, NV:NV + 1]), 1)
        ev_stats = s_ev.v

        @b.scalar
        def _(s, tt=tt, par=par, ev_stats=ev_stats):
            s.wait_ge(s_ev.h, ev_stats)
            s.wait_ge(s_sc.h, s_sc.inc(
                s.activation(cx.statw[:, RST:RST + 1], cx.statw[:, RS:RS + 1], AF.Sqrt), 1))
            s.wait_ge(s_sc.h, s_sc.inc(
                s.activation(cx.statw[:, NMR:NMR + 1], cx.statw[:, NM:NM + 1], AF.Identity,
                             scale=cx.statw[:, RST:RST + 1]), 1))
            sc_apply[tt] = s_sc.inc(
                s.activation(xlnb[:, par, :], xt[:, par, :], AF.Identity,
                             scale=cx.statw[:, RST:RST + 1], bias=cx.statw[:, NMR:NMR + 1]), 1)

        @b.tensor
        def _(t, tt=tt, par=par):
            t.wait_ge(s_sc.h, sc_apply[tt])
            if tt >= 2:
                t.wait_ge(s_ev.h, evx[tt - 2])
            else:
                t.wait_ge(s_ev.h, ev_entry0)
            for hc in range(HC):
                ins = t.transpose(pslotb[par][:, hc, :], xlnb[:, par, hc * 128:(hc + 1) * 128],
                                  cx.identb[:, :])
            tr_done[tt] = s_pe.inc(ins, 1)

        @b.vector
        def _(v, tt=tt, par=par):
            v.wait_ge(s_pe.h, tr_done[tt])
            evx[tt] = s_ev.inc(
                v.tensor_copy(uaf[:, :, tt * 128:(tt + 1) * 128], pslotb[par][:, :, :]), 1)

    uaf_ready = s_ev.v

    wcv_ld = [None] * NG
    wcv_st = [None] * NG
    for i in range(NG):
        par = i % 2
        wcv_st[i] = cx.s_wcv[par].v + 16 * (i // 2 + 1)

    @b.gpsimd
    def _(g):
        for i in range(NG):
            par = i % 2
            if i >= 2:
                g.wait_ge(cx.s_wcv[par].h, wcv_st[i - 2])
            else:
                g.wait_ge(s_ev.h, ev_entry0)
            wcv_ld[i] = s_ldw[par].inc(
                g.dma_start(out=wcv[:, par, :], in_=cx.wout_natd[i]), 16)

    @b.sync
    def _(sy):
        for i in range(NG):
            par = i % 2
            sy.wait_ge(s_ldw[par].h, wcv_ld[i])
            val = cx.s_wcv[par].inc(sy.dma_start(out=cx.wout_bfd[i], in_=wcv[:, par, :]), 16)
            assert val == wcv_st[i]
    wout_conv_done = [cx.s_wcv[0].v, cx.s_wcv[1].v]

    fb_order = list(range(32, NFB)) + list(range(0, 32))
    pall = ps.rearrange("p (a b) -> p a b", a=2)
    w_ld = {}
    pe_fb = {}
    ev_fb = {}
    st_fb = {}
    pe_base = s_pe.v
    ev_base = s_ev.v
    for j, fb in enumerate(fb_order):
        pe_fb[fb] = pe_base + j + 1
        ev_fb[fb] = ev_base + j + 1
    stv = [cx.s_st[0].v, cx.s_st[1].v]
    for j, fb in enumerate(fb_order):
        par = j % 2
        if fb != 80:
            stv[par] += 16
            st_fb[fb] = stv[par]
    evv = ev_base
    for j, fb in enumerate(fb_order):
        if 32 <= fb < 80:
            evv += 2
        elif fb == 80:
            evv += 1
        ev_fb[fb] = evv
    sc_fb = {}
    scv = s_sc.v
    for j, fb in enumerate(fb_order):
        if fb < 32:
            scv += 1
            sc_fb[fb] = scv

    @b.gpsimd
    def _(g):
        for j, fb in enumerate(fb_order):
            par = j % 2
            if j >= 2:
                g.wait_ge(s_pe.h, pe_fb[fb_order[j - 2]])
            else:
                g.wait_ge(s_ev.h, uaf_ready)
            w_ld[fb] = s_gx[par].inc(
                g.dma_start(out=wbuf[:, par, :, :], in_=cx.win_blkd[fb]), 16)

    @b.tensor
    def _(t):
        t.wait_ge(s_ev.h, uaf_ready)
        for j, fb in enumerate(fb_order):
            par = j % 2
            t.wait_ge(s_gx[par].h, w_ld[fb])
            if j >= 2:
                pfb = fb_order[j - 2]
                if pfb < 32:
                    t.wait_ge(s_sc.h, sc_fb[pfb])
                else:
                    t.wait_ge(s_ev.h, ev_fb[pfb])
            for hc in range(HC):
                for tq in range(4):
                    ins = t.matmul(pall[:, par, tq * 512:(tq + 1) * 512],
                                   wbuf[:, par, hc, :], uaf[:, hc, tq * 512:(tq + 1) * 512],
                                   start=(hc == 0), stop=(hc == HC - 1))
            val = s_pe.inc(ins, 1)
            assert val == pe_fb[fb]

    @b.vector
    def _(v):
        for j, fb in enumerate(fb_order):
            if fb < 32:
                continue
            par = j % 2
            v.wait_ge(s_pe.h, pe_fb[fb])
            jj = j - 2
            while jj >= 0 and fb_order[jj] == 80:
                jj -= 2
            if jj >= 0:
                v.wait_ge(s_st[par].h, st_fb[fb_order[jj]])
            if fb == 80:
                v.tensor_copy(dtA[:, :], pall[0:NH, par, :]).then_inc(s_ev.h, 1)
                s_ev.v += 1
            else:
                cpv = s_ev.inc(v.tensor_copy(bounce[:, par, :], pall[:, par, :]), 1)
                v.wait_ge(s_ev.h, cpv)
                v.tensor_scalar(ex1sb[:, fb - 32, :], bounce[:, par, 2045:2048],
                                cx.flags[:, 0:1], None, ALU.mult).then_inc(s_ev.h, 1)
                s_ev.v += 1
            assert s_ev.v == ev_fb[fb]

    @b.scalar
    def _(sc):
        for j, fb in enumerate(fb_order):
            if fb >= 32:
                continue
            par = j % 2
            sc.wait_ge(s_pe.h, pe_fb[fb])
            jj = j - 2
            while jj >= 0 and fb_order[jj] == 80:
                jj -= 2
            if jj >= 0:
                sc.wait_ge(s_st[par].h, st_fb[fb_order[jj]])
            sc.activation(bounce[:, par, :], pall[:, par, :], AF.Silu).then_inc(s_sc.h, 1)
            s_sc.v += 1
            assert s_sc.v == sc_fb[fb]
    dt_ready = ev_fb[80]

    @b.sync
    def _(sy):
        for j, fb in enumerate(fb_order):
            if fb == 80:
                continue
            par = j % 2
            if 32 <= fb < 80:
                sy.wait_ge(s_ev.h, ev_fb[fb])
                val = s_st[par].inc(sy.dma_start(out=cx.xbc_raw[fb - 32], in_=bounce[:, par, :]), 16)
            else:
                sy.wait_ge(s_sc.h, sc_fb[fb])
                val = s_st[par].inc(sy.dma_start(out=cx.z_raw[fb], in_=bounce[:, par, :]), 16)
            assert val == st_fb[fb]

    xbc_export_ev = ev_fb[79]
    inproj_st_frontier = [cx.s_st[0].v, cx.s_st[1].v]

    @b.gpsimd
    def _(g):
        g.wait_ge(s_ev.h, xbc_export_ev)
        g.dma_start(out=cx.ex1_in, in_=ex1sb.rearrange("p a b -> p (a b)")).then_inc(cx.s_po.h, 16)
        cx.s_po.v += 16
        g.wait_ge(cx.s_po.h, cx.s_po.v)
        g.collective_compute(
            "AllReduce", ALU.add,
            replica_groups=RGROUPS,
            ins=[cx.ex1_in.opt()], outs=[cx.ex1_out.opt()],
        ).then_inc(s_cc.h, 1)
        s_cc.v += 1
        g.wait_ge(s_cc.h, s_cc.v)
        g.dma_start(out=tailr[:, :], in_=cx.ex1_out).then_inc(cx.s_po.h, 16)
        cx.s_po.v += 16
    tail_ld = cx.s_po.v

    tail_ev = [None]

    @b.vector
    def _(v):
        v.wait_ge(cx.s_po.h, tail_ld)
        v.wait_ge(cx.s_ldc.h, cx.attn_const)
        tail_ev[0] = s_ev.inc(
            v.tensor_scalar(tailr[:, :], tailr[:, :], cx.flags[:, 1:2], None, ALU.mult), 1)

    dt_prep2(cx, dtA, dtB, dtC, dtD, dcol, drow, jnkdt, dt_ready)

    tailr3 = tailr.rearrange("p (f k) -> p f k", f=NFC)
    cv_ev = {}
    cv_st = {}
    for fc in range(NFC):
        par = fc % 2

        @b.gpsimd
        def _(gq, fc=fc, par=par):
            if fc >= 2:
                gq.wait_ge(s_ev.h, cv_ev[fc - 2])
                sem = s_st if fc - 2 < 16 else s_stc
                gq.wait_ge(sem[par].h, cv_st[fc - 2])
            gq.wait_ge(s_st[0].h, inproj_st_frontier[0])
            gq.wait_ge(s_st[1].h, inproj_st_frontier[1])
            cx.s_cin[par].inc(gq.dma_start(out=cin[:, par, 3:3 + T], in_=cx.xbc_raw[fc]), 16)
        cin_ld = cx.s_cin[par].v

        @b.vector
        def _(v, fc=fc, par=par, cin_ld=cin_ld):
            v.wait_ge(cx.s_cin[par].h, cin_ld)
            v.wait_ge(s_ev.h, tail_ev[0])
            v.wait_ge(s_ev.h, s_ev.inc(v.tensor_copy(cin[:, par, 0:3], tailr3[:, fc, :]), 1))
            v.wait_ge(s_ev.h, s_ev.inc(
                v.tensor_scalar(cv[:, par, :], cin[:, par, 0:T], cx.convw[:, fc, 0:1],
                                cx.convb[:, fc:fc + 1], ALU.mult, ALU.add), 1))
            for k in range(1, 4):
                ins = v.scalar_tensor_tensor(cv[:, par, :], cin[:, par, k:k + T],
                                             cx.convw[:, fc, k:k + 1], cv[:, par, :],
                                             ALU.mult, ALU.add)
                if k < 3:
                    v.wait_ge(s_ev.h, s_ev.inc(ins, 1))
            cv_ev[fc] = s_ev.inc(ins, 1)

        @b.scalar
        def _(s, fc=fc, par=par):
            s.wait_ge(s_ev.h, cv_ev[fc])
            s.activation(cout[:, par, :], cv[:, par, :], AF.Silu).then_inc(s_sc.h, 1)
            s_sc.v += 1
        sil_done = s_sc.v

        @b.sync
        def _(sy, fc=fc, par=par, sil_done=sil_done):
            sy.wait_ge(s_sc.h, sil_done)
            sem = s_st if fc < 16 else s_stc
            cv_st[fc] = sem[par].inc(
                sy.dma_start(out=cx.xbc_conv[fc], in_=cout[:, par, :]), 16)

    xb_conv_st = [s_st[0].v, s_st[1].v]
    conv_all_st = [s_stc[0].v, s_stc[1].v]
    conv_done_ev = s_ev.v


    emit_xbt(cx, uaf_off, bounce_off, xb_conv_st)

    ar.pop()

    ar.push()
    bufs = {}
    bufs["carry"] = ar.alloc([128, DIN], F32)
    bufs["carry_bf"] = ar.alloc([128, DIN], BF16)
    bufs["states"] = ar.alloc([128, CH, DIN], BF16)
    states_off = ar.last_off
    bufs["xbt"] = ar.alloc([128, 2, 2, T], BF16)
    bufs["bck"] = ar.alloc([128, 2, 8, CL], BF16)
    bufs["xfm"] = ar.alloc([128, 2, 8, CL], BF16)
    bufs["xdd"] = ar.alloc([128, 4, 2, 64], BF16)
    bufs["segb"] = ar.alloc([128, 2, 2, 2, CL], F32)
    segb_off = ar.last_off
    bufs["Lb"] = ar.alloc([128, 2, 2, 2, CL], BF16)
    bufs["glb"] = ar.alloc([128, 2, 2, 2, CL], BF16)
    bufs["ebcb"] = ar.alloc([128, 2, 2, CL], BF16)
    bufs["csb"] = ar.alloc([128, 2, 2, CL], BF16)
    bufs["cgb"] = ar.alloc([128, 4, CL], BF16)
    bufs["zgb"] = ar.alloc([128, 2, CL], BF16)
    bufs["ygc"] = ar.alloc([128, 2, CL], F32)
    bufs["y2"] = ar.alloc([128, 2, CL], F32)
    bufs["rrow"] = ar.alloc([1, 2, CL], F32)
    bufs["ynormF"] = ar.alloc([128, NG, 2 * CL], BF16)
    ynorm_off = ar.last_off
    bufs["exc"] = ar.at(ynorm_off, [128, DIN], F32)
    bufs["x1t"] = ar.at(states_off, [128, 2, D], F32)
    bufs["woutb"] = ar.at(segb_off, [128, 2, D], BF16)
    bufs["xb_conv_st"] = xb_conv_st
    bufs["conv_all_st"] = conv_all_st
    bufs["wout_conv_done"] = wout_conv_done

    emit_ssd2(cx, bufs)
    ar.pop()
    ar.pop()



def emit_xbt(cx, xbfm_off, stage_off, conv_xb_st):
    b, ar, ps = cx.block, cx.ar, cx.psum
    s_pe, s_ev = cx.s_pe, cx.s_ev
    xbfm = ar.at(xbfm_off, [128, 16, T], BF16)
    stage = ar.at(stage_off, [128, 2, 16, 128], BF16)
    ld = [None] * 16
    trd = {}
    evd = {}
    std = [None] * 16
    ptr = [ps[:, 0:512].bitcast(BF16).rearrange("p (a b) -> p a b", a=8),
           ps[:, 512:1024].bitcast(BF16).rearrange("p (a b) -> p a b", a=8)]

    @b.sync
    def _(sy):
        sy.wait_ge(cx.s_st[0].h, conv_xb_st[0])
        sy.wait_ge(cx.s_st[1].h, conv_xb_st[1])
        for fc in range(16):
            ld[fc] = cx.s_ldx[fc % 2].inc(
                sy.dma_start(out=xbfm[:, fc, :], in_=cx.xbc_conv[fc]), 16)
    ld_all = [cx.s_ldx[0].v, cx.s_ldx[1].v]

    for tc in range(16):
        par = tc % 2
        for half in range(2):
            bk = (2 * tc + half) % 2

            @b.tensor
            def _(t, tc=tc, half=half, bk=bk):
                if tc == 0 and half == 0:
                    t.wait_ge(cx.s_ldx[0].h, ld_all[0])
                    t.wait_ge(cx.s_ldx[1].h, ld_all[1])
                key = (tc, half - 1) if half == 1 else (tc - 1, 1)
                if key in evd:
                    t.wait_ge(s_ev.h, evd[key])
                for i in range(8):
                    fc = half * 8 + i
                    ins = t.transpose(ptr[bk][:, i, :], xbfm[:, fc, tc * 128:(tc + 1) * 128],
                                      cx.identb[:, :])
                trd[(tc, half)] = s_pe.inc(ins, 1)

            @b.vector
            def _(v, tc=tc, half=half, bk=bk, par=par):
                v.wait_ge(s_pe.h, trd[(tc, half)])
                if tc >= 2:
                    v.wait_ge(cx.s_st[par].h, std[tc - 2])
                evd[(tc, half)] = s_ev.inc(
                    v.tensor_copy(stage[:, par, half * 8:(half + 1) * 8, :],
                                  ptr[bk][:, :, :]), 1)

        @b.sync
        def _(sy, tc=tc, par=par):
            sy.wait_ge(s_ev.h, evd[(tc, 1)])
            std[tc] = cx.s_st[par].inc(
                sy.dma_start(out=cx.xbt_dram[tc],
                             in_=stage[:, par, :, :].rearrange("p a b -> p (a b)")), 16)
    cx.xbt_st = [cx.s_st[0].v, cx.s_st[1].v]


def dt_prep2(cx, dtA, dtB, dtC, dtD, dcol, drow, jnk, dt_ready):
    b, ps = cx.block, cx.psum
    s_ev, s_sc, s_pe = cx.s_ev, cx.s_sc, cx.s_pe
    jnk64 = jnk[0:NH, :]
    ev0 = [None] * 8

    @b.vector
    def _(v):
        def vstep(ins):
            v.wait_ge(s_ev.h, s_ev.inc(ins, 1))

        v.wait_ge(s_ev.h, dt_ready)
        vstep(v.tensor_scalar(dtA[:, :], dtA[:, :], cx.dtbc[:, :], None, ALU.add))
        ev0[0] = s_ev.inc(v.tensor_scalar(dtB[:, :], dtA[:, :], 0.0, None, ALU.max), 1)

    @b.scalar
    def _(s):
        s.wait_ge(s_ev.h, ev0[0])
        s.wait_ge(s_sc.h, s_sc.inc(s.activation(jnk64[:, :], dtA[:, :], AF.Abs), 1))
        s.wait_ge(s_sc.h, s_sc.inc(s.activation(jnk64[:, :], jnk64[:, :], AF.Exp, scale=-1.0), 1))
        s.wait_ge(s_sc.h, s_sc.inc(s.activation(jnk64[:, :], jnk64[:, :], AF.Ln, bias=1.0), 1))
        ev0[1] = s_sc.inc(s.activation(cx.acol[:, :], cx.alogc[:, :], AF.Exp), 1)

    @b.vector
    def _(v):
        def vstep(ins):
            v.wait_ge(s_ev.h, s_ev.inc(ins, 1))

        v.wait_ge(s_sc.h, ev0[1])
        vstep(v.tensor_tensor(dtA[:, :], dtB[:, :], jnk64[:, :], ALU.add))
        vstep(v.tensor_scalar(dtA[:, :], dtA[:, :], 1e-30, None, ALU.max))
        ev0[2] = s_ev.inc(v.tensor_scalar(cx.acol[:, :], cx.acol[:, :], -1.0, None, ALU.mult), 1)

    @b.scalar
    def _(s):
        s.wait_ge(s_ev.h, ev0[2])
        ev0[3] = s_sc.inc(s.activation(dtD[:, :], dtA[:, :], AF.Ln), 1)

    @b.vector
    def _(v):
        def vstep(ins):
            v.wait_ge(s_ev.h, s_ev.inc(ins, 1))

        v.wait_ge(s_sc.h, ev0[3])
        vstep(v.tensor_scalar(dtC[:, :], dtA[:, :], cx.acol[:, :], None, ALU.mult))
        src, dst = dtC, jnk64
        sh = 1
        while sh < CL:
            sv = src.rearrange("p (c t) -> p c t", t=CL)
            dv = dst.rearrange("p (c t) -> p c t", t=CL)
            vstep(v.tensor_copy(dv[:, :, 0:sh], sv[:, :, 0:sh]))
            vstep(v.tensor_tensor(dv[:, :, sh:CL], sv[:, :, sh:CL], sv[:, :, 0:CL - sh], ALU.add))
            src, dst = dst, src
            sh *= 2
        assert src is dtC
        ev0[4] = s_ev.inc(v.tensor_tensor(dtD[:, :], dtD[:, :], dtC[:, :], ALU.subtract), 1)

    @b.scalar
    def _(s):
        s.wait_ge(s_ev.h, ev0[4])
        c3 = dtC.rearrange("p (c t) -> p c t", t=CL)
        b3 = dtB.rearrange("p (c t) -> p c t", t=CL)
        for c in range(CH):
            ins = s.activation(b3[:, c, :], c3[:, c, :], AF.Exp, scale=-1.0,
                               bias=c3[:, c, CL - 1:CL])
        s.wait_ge(s_sc.h, s_sc.inc(ins, 1))
        dc3 = dcol.unsqueeze(2)
        ev0[5] = s_sc.inc(s.activation(dc3[:, :, :], c3[:, :, CL - 1:CL], AF.Exp), 1)

    @b.vector
    def _(v):
        v.wait_ge(s_sc.h, ev0[5])
        ev0[6] = s_ev.inc(v.tensor_tensor(dtB[:, :], dtA[:, :], dtB[:, :], ALU.mult), 1)


    ddv = cx.ddTf
    bsv = cx.biasTf
    trv = {}
    evv = {}
    for i, (srcT, dstT, rdy, sname) in enumerate(
            [(dtB, ddv, ev0[6], "dd"), (dtD, bsv, ev0[4], "bs")]):
        for tc in range(16):
            sl = tc % 2

            @b.tensor
            def _(t, i=i, tc=tc, sl=sl, srcT=srcT, rdy=rdy):
                t.wait_ge(s_ev.h, rdy)
                if (i, tc - 2) in evv:
                    t.wait_ge(s_ev.h, evv[(i, tc - 2)])
                elif i == 1 and tc < 2:
                    t.wait_ge(s_ev.h, evv[(0, 14 + tc)])
                poff = 3072 if sl == 0 else 3584
                trv[(i, tc)] = s_pe.inc(
                    t.transpose(ps[:, poff:poff + 64],
                                srcT[:, tc * 128:(tc + 1) * 128], cx.ident32[0:NH, 0:NH]), 1)

            @b.vector
            def _(v, i=i, tc=tc, sl=sl, dstT=dstT):
                v.wait_ge(s_pe.h, trv[(i, tc)])
                poff = 3072 if sl == 0 else 3584
                evv[(i, tc)] = s_ev.inc(
                    v.tensor_copy(dstT[:, tc, :], ps[:, poff:poff + 64]), 1)

    @b.tensor
    def _(t):
        t.wait_ge(s_sc.h, ev0[5])
        t.wait_ge(s_ev.h, evv[(1, 15)])
        ev0[7] = s_pe.inc(t.transpose(ps[:, 3584:3648][0:CH, :], dcol[:, :],
                                      cx.ident32[0:NH, 0:NH]), 1)

    dtr_ev = [None]

    @b.vector
    def _(v):
        v.wait_ge(s_pe.h, ev0[7])
        dtr_ev[0] = s_ev.inc(v.tensor_copy(jnk[0:CH, 0:NH], ps[:, 3584:3648][0:CH, :]), 1)

    dall_st = [None]
    dall_ld = [None]
    acs_st = [None]
    expa_sb = jnk.bitcast(BF16)[0:NH, 0:T]
    expa_ev = [None]

    @b.sync
    def _(sy):
        sy.wait_ge(s_ev.h, dtr_ev[0])
        dall_st[0] = cx.s_dt.inc(sy.dma_start(out=cx.dall_dram, in_=jnk[0:CH, 0:NH]), 16)
        sy.wait_ge(cx.s_dt.h, dall_st[0])
        dall_ld[0] = cx.s_dt.inc(sy.dma_start(out=drow[:, :], in_=cx.dall_dram), 16)
        sy.wait_ge(cx.s_dt.h, dall_ld[0])
        sy.wait_ge(s_ev.h, ev0[4])
        cx.s_dt.inc(sy.dma_start(out=cx.acs_dram, in_=dtC[:, :]), 16)

    dbv = cx.dbcc
    cx._dbc_ev = {}
    for c in range(CH):
        @b.tensor
        def _(t, c=c):
            if c == 0:
                t.wait_ge(cx.s_dt.h, dall_ld[0])
            else:
                t.wait_ge(s_ev.h, cx._dbc_ev[c - 1])
            cx._dbc_pe = s_pe.inc(
                t.matmul(ps[:, 3648:3712], cx.ones_row[:, :],
                         drow[0:1, c * NH:(c + 1) * NH], start=True, stop=True), 1)

        @b.vector
        def _(v, c=c):
            v.wait_ge(s_pe.h, cx._dbc_pe)
            cx._dbc_ev[c] = s_ev.inc(v.tensor_copy(dbv[:, c, :], ps[:, 3648:3712]), 1)
    @b.scalar
    def _(s):
        s.wait_ge(s_ev.h, ev0[4])
        s.wait_ge(cx.s_dt.h, dall_ld[0])
        expa_ev[0] = s_sc.inc(s.activation(expa_sb, dtC[:, :], AF.Exp), 1)

    @b.sync
    def _(sy):
        sy.wait_ge(s_sc.h, expa_ev[0])
        acs_st[0] = cx.s_dt.inc(sy.dma_start(out=cx.expacs_dram, in_=expa_sb), 16)

    cx.dt_done_ev = cx._dbc_ev[CH - 1]
    cx.acs_st = acs_st[0]


def emit_ssd2(cx, bf):
    b, ps = cx.block, cx.psum
    s_pe, s_ev, s_sc, s_cc = cx.s_pe, cx.s_ev, cx.s_sc, cx.s_cc
    s_ld, s_st, s_ld2 = cx.s_ld, cx.s_st, cx.s_ld2
    s_ldx, s_ldw, s_ldcg = cx.s_ldx, cx.s_ldw, cx.s_ldcg

    carry, carry_bf, states = bf["carry"], bf["carry_bf"], bf["states"]
    xbt, bck, xfm = bf["xbt"], bf["bck"], bf["xfm"]
    xdd = bf["xdd"]
    segb, Lb, glb = bf["segb"], bf["Lb"], bf["glb"]
    ebcb, csb, cgb = bf["ebcb"], bf["csb"], bf["cgb"]
    zgb, ygc, y2 = bf["zgb"], bf["ygc"], bf["y2"]
    y2q = ygc
    rrow = bf["rrow"]
    ynormF = bf["ynormF"]
    exc, x1t, woutb = bf["exc"], bf["x1t"], bf["woutb"]
    x2t = x1t

    xbt_ld = [None] * CH
    p1_mm = {}
    p1_xdd = {}
    p1_cu = {}
    st_banks = [ps[:, 0:64], ps[:, 512:576], ps[:, 1024:1088], ps[:, 1536:1600]]

    for c in range(CH):
        cpar = c % 2

        @b.sync
        def _(sy, c=c, cpar=cpar):
            if c >= 2:
                sy.wait_ge(s_pe.h, p1_mm[(c - 2, NH - 1)])
            else:
                sy.wait_ge(s_st[0].h, cx.xbt_st[0])
                sy.wait_ge(s_st[1].h, cx.xbt_st[1])
            xbt_ld[c] = s_ldx[cpar].inc(
                sy.dma_start(out=xbt[:, cpar, :, :],
                             in_=cx.xbt_dram[2 * c:2 * c + 2].rearrange("a p f -> p a f")), 16)

        for h in range(NH):
            hp = h % 4
            xc0 = (h // 8) * 128 + (h % 2) * 64

            @b.vector
            def _(v, c=c, h=h, hp=hp, xc0=xc0, cpar=cpar):
                v.wait_ge(s_ldx[cpar].h, xbt_ld[c])
                if (c, h - 4) in p1_mm:
                    v.wait_ge(s_pe.h, p1_mm[(c, h - 4)])
                elif c >= 1 and h < 4:
                    v.wait_ge(s_pe.h, p1_mm[(c - 1, NH - 4 + h)])
                if c == 0 and h < 4:
                    v.wait_ge(s_ev.h, cx.dt_done_ev)
                    v.wait_ge(cx.s_dt.h, cx.acs_st)
                for st in range(2):
                    ins = v.tensor_scalar(xdd[:, hp, st, :], xbt[:, cpar, st, xc0:xc0 + 64],
                                          cx.ddTf[:, 2 * c + st, h:h + 1], None, ALU.mult)
                p1_xdd[(c, h)] = s_ev.inc(ins, 1)

            @b.tensor
            def _(t, c=c, h=h, hp=hp, cpar=cpar):
                t.wait_ge(s_ev.h, p1_xdd[(c, h)])
                if (c, h - 4) in p1_cu:
                    t.wait_ge(s_ev.h, p1_cu[(c, h - 4)])
                elif c >= 1 and h < 4:
                    t.wait_ge(s_ev.h, p1_cu[(c - 1, NH - 4 + h)])
                bc0 = 1024 + (h // 8) * 128
                for st in range(2):
                    ins = t.matmul(st_banks[hp], xbt[:, cpar, st, bc0:bc0 + 128],
                                   xdd[:, hp, st, :], start=(st == 0), stop=(st == 1))
                p1_mm[(c, h)] = s_pe.inc(ins, 1)

            @b.vector
            def _(v, c=c, h=h, hp=hp):
                v.wait_ge(s_pe.h, p1_mm[(c, h)])
                v.tensor_copy(states[:, c, h * 64:(h + 1) * 64], st_banks[hp])
                if c == 0:
                    ins = v.tensor_copy(carry[:, h * 64:(h + 1) * 64], st_banks[hp])
                else:
                    ins = v.scalar_tensor_tensor(
                        carry[:, h * 64:(h + 1) * 64], carry[:, h * 64:(h + 1) * 64],
                        cx.dbcc[:, c, h:h + 1], st_banks[hp], ALU.mult, ALU.add)
                p1_cu[(c, h)] = s_ev.inc(ins, 1)

    exp_ev = [None]
    cin_ev = [None]

    @b.vector
    def _(v):
        v.wait_ge(s_ev.h, p1_cu[(CH - 1, NH - 1)])
        v.wait_ge(cx.s_stc[0].h, bf["conv_all_st"][0])
        v.wait_ge(cx.s_stc[1].h, bf["conv_all_st"][1])
        v.wait_ge(cx.s_wcv[0].h, bf["wout_conv_done"][0])
        v.wait_ge(cx.s_wcv[1].h, bf["wout_conv_done"][1])
        exp_ev[0] = s_ev.inc(
            v.tensor_scalar(exc[:, :], carry[:, :], cx.flags[:, 0:1], None, ALU.mult), 1)

    @b.gpsimd
    def _(g):
        g.wait_ge(s_ev.h, exp_ev[0])
        g.dma_start(out=cx.ex2_in, in_=exc[:, :]).then_inc(cx.s_po.h, 16)
        cx.s_po.v += 16
        g.wait_ge(cx.s_po.h, cx.s_po.v)
        g.collective_compute(
            "AllReduce", ALU.add,
            replica_groups=RGROUPS,
            ins=[cx.ex2_in.opt()], outs=[cx.ex2_out.opt()],
        ).then_inc(s_cc.h, 1)
        s_cc.v += 1
        g.wait_ge(s_cc.h, s_cc.v)
        g.dma_start(out=exc[:, :], in_=cx.ex2_out).then_inc(cx.s_po.h, 16)
        cx.s_po.v += 16
    exc_ld = cx.s_po.v

    @b.vector
    def _(v):
        v.wait_ge(cx.s_po.h, exc_ld)
        v.wait_ge(s_ev.h, s_ev.inc(
            v.tensor_scalar(carry[:, :], exc[:, :], cx.flags[:, 1:2], None, ALU.mult), 1))
        cin_ev[0] = s_ev.inc(v.tensor_copy(carry_bf[:, :], carry[:, :]), 1)

    p2_xbt = [None] * CH
    p2_G = {}
    p2_seg = {}
    p2_pre = {}
    p2_ea = {}
    p2_L = {}
    p2_gl = {}
    p2_Y = {}
    p2_yev = {}
    p2_sil = {}
    p2_y2 = {}
    p2_sq = {}
    p2_ms = {}
    p2_rr = {}
    p2_rb = {}
    p2_yn = {}
    p2_cg = {}
    op_ev = {}
    op_st = {}
    segb, Lb, glb = bf["segb"], bf["Lb"], bf["glb"]
    ebcb, csb, cgb = bf["ebcb"], bf["csb"], bf["cgb"]
    expab = ebcb
    Gb = [ps[:, i * 512:(i + 1) * 512].rearrange("p (a b) -> p a b", a=2) for i in range(4)]
    Yb = [ps[:, 2048:2304][0:64, :], ps[:, 2560:2816][0:64, :]]

    for c in range(CH):
        cpar = c % 2

        @b.sync
        def _(sy, c=c, cpar=cpar):
            if c >= 2:
                sy.wait_ge(s_ev.h, p2_yev[(c - 2, NH - 1)])
            else:
                sy.wait_ge(s_ev.h, cin_ev[0])
            sy.dma_start(out=xbt[:, cpar, :, :],
                         in_=cx.xbt_dram[2 * c:2 * c + 2].rearrange("a p f -> p a f")
                         ).then_inc(s_ldx[cpar].h, 16)
            s_ldx[cpar].v += 16
            sy.dma_start(out=bck[:, cpar, :, :],
                         in_=cx.xbc_conv[8:16, :, c * CL:(c + 1) * CL].rearrange(
                             "f p t -> p f t")).then_inc(s_ldx[cpar].h, 16)
            s_ldx[cpar].v += 16
            p2_xbt[c] = s_ldx[cpar].inc(
                sy.dma_start(out=xfm[:, cpar, :, :],
                             in_=cx.xbc_conv[0:8, :, c * CL:(c + 1) * CL].rearrange(
                                 "f p t -> p f t")), 16)

        for g in range(NG):
            gp4 = g % 4
            gp = g % 2
            pp = g % 2

            @b.vector
            def _(v, c=c, g=g, pp=pp):
                if (c, g - 2) in p2_L:
                    v.wait_ge(s_sc.h, p2_L[(c, g - 2)])
                elif c >= 1 and g < 2:
                    v.wait_ge(s_sc.h, p2_L[(c - 1, NG - 2 + g)])
                elif c == 0 and g < 2:
                    v.wait_ge(cx.s_ldc.h, cx.attn_const)
                    v.wait_ge(s_ev.h, cin_ev[0])
                p2_pre[(c, g)] = s_ev.inc(v.tensor_copy(
                    segb[:, pp, :, :, :],
                    cx.negbig.unsqueeze(1).broadcast_to([128, 2, 2, CL])), 1)

            @b.gpsimd
            def _(g_, c=c, g=g, pp=pp):
                g_.wait_ge(s_ev.h, p2_pre[(c, g)])
                if c == 0 and g < 2:
                    g_.wait_ge(cx.s_dt.h, cx.acs_st)
                for hh in range(2):
                    g_.dma_start(out=segb[:, pp, hh, :, :],
                                 in_=cx.acs_dram[2 * g + hh:2 * g + hh + 1,
                                                 c * CL:(c + 1) * CL]
                                 .unsqueeze(0).broadcast_to([128, 2, CL]),
                                 accum_op=ALU.add).then_inc(cx.s_lda[pp].h, 16)
                    cx.s_lda[pp].v += 16
                p2_seg[(c, g)] = cx.s_lda[pp].inc(
                    g_.dma_start(out=expab[:, pp, :, :],
                                 in_=cx.expacs_dram[2 * g:2 * g + 2, c * CL:(c + 1) * CL]
                                 .unsqueeze(0).broadcast_to([128, 2, CL])), 16)

            @b.sync
            def _(sy, c=c, g=g, gp=gp):
                if (c, 2 * (g - 4) + 1) in p2_gl:
                    sy.wait_ge(s_ev.h, p2_gl[(c, 2 * (g - 4) + 1)])
                elif c >= 1 and g < 4:
                    sy.wait_ge(s_ev.h, p2_gl[(c - 1, 2 * (NG - 4 + g) + 1)])
                else:
                    sy.wait_ge(cx.s_stc[0].h, bf["conv_all_st"][0])
                    sy.wait_ge(cx.s_stc[1].h, bf["conv_all_st"][1])
                p2_cg[(c, g)] = s_ldcg[g % 4].inc(
                    sy.dma_start(out=cgb[:, g % 4, :],
                                 in_=cx.xbc_conv[16 + g, :, c * CL:(c + 1) * CL]), 16)

            @b.tensor
            def _(t, c=c, g=g, gp4=gp4, cpar=cpar):
                if c >= 2 and c % 2 == 0 and g == 0:
                    t.wait_ge(s_ev.h, op_ev[(c - 2) // 2])
                t.wait_ge(s_ldcg[g % 4].h, p2_cg[(c, g)])
                t.wait_ge(s_ldx[cpar].h, p2_xbt[c])
                if (c, 2 * (g - 4) + 1) in p2_gl:
                    t.wait_ge(s_ev.h, p2_gl[(c, 2 * (g - 4) + 1)])
                elif c >= 1 and g < 4:
                    t.wait_ge(s_ev.h, p2_gl[(c - 1, 2 * (NG - 4 + g) + 1)])
                for st in range(2):
                    ins = t.matmul(Gb[gp4][:, st, :],
                                   bck[:, cpar, g // 4, st * 128:(st + 1) * 128],
                                   cgb[:, g % 4, :], start=True, stop=True)
                p2_G[(c, g)] = s_pe.inc(ins, 1)

            @b.scalar
            def _(s, c=c, g=g, pp=pp):
                s.wait_ge(cx.s_lda[pp].h, p2_seg[(c, g)])
                if c == 0 and g == 0:
                    s.wait_ge(s_ev.h, cx.dt_done_ev)
                for hh in range(2):
                    hgl = 2 * g + hh
                    for st in range(2):
                        ins = s.activation(Lb[:, pp, hh, st, :], segb[:, pp, hh, st, :], AF.Exp,
                                           bias=cx.biasTf[:, 2 * c + st, hgl:hgl + 1])
                p2_L[(c, g)] = s_sc.inc(ins, 1)

            @b.vector
            def _(v, c=c, g=g, pp=pp, gp4=gp4):
                v.wait_ge(s_sc.h, p2_L[(c, g)])
                v.wait_ge(s_pe.h, p2_G[(c, g)])
                v.tensor_tensor(csb[:, pp, :, :],
                                cgb[:, g % 4, :].unsqueeze(1).broadcast_to([128, 2, CL]),
                                expab[:, pp, :, :], ALU.mult)
                p2_gl[(c, 2 * g + 1)] = s_ev.inc(v.tensor_tensor(
                    glb[:, pp, :, :, :],
                    Gb[gp4].unsqueeze(1).broadcast_to([128, 2, 2, CL]),
                    Lb[:, pp, :, :, :], ALU.mult), 1)

            for hh in range(2):
                h = 2 * g + hh
                xc0 = (h // 8) * 128 + (h % 2) * 64

                @b.tensor
                def _(t, c=c, g=g, h=h, hh=hh, pp=pp, cpar=cpar, xc0=xc0):
                    t.wait_ge(s_ev.h, p2_gl[(c, 2 * g + 1)])
                    if (c, h - 2) in p2_yev:
                        t.wait_ge(s_ev.h, p2_yev[(c, h - 2)])
                    elif c >= 1 and h < 2:
                        t.wait_ge(s_ev.h, p2_yev[(c - 1, NH - 2 + h)])
                    t.matmul(Yb[hh], xbt[:, cpar, 0, xc0:xc0 + 64], glb[:, pp, hh, 0, :],
                             start=True, stop=False)
                    t.matmul(Yb[hh], xbt[:, cpar, 1, xc0:xc0 + 64], glb[:, pp, hh, 1, :],
                             start=False, stop=False)
                    p2_Y[(c, h)] = s_pe.inc(
                        t.matmul(Yb[hh], carry_bf[:, h * 64:(h + 1) * 64], csb[:, pp, hh, :],
                                 start=False, stop=True), 1)

                @b.vector
                def _(v, c=c, g=g, h=h, hh=hh, gp=gp, cpar=cpar):
                    v.wait_ge(s_pe.h, p2_Y[(c, h)])
                    po = (h % 2) * 64
                    v.scalar_tensor_tensor(ygc[hh * 64:hh * 64 + 64, gp, :],
                                           xfm[po:po + 64, cpar, h // 8, :],
                                           cx.d_rep[po:po + 64, h:h + 1],
                                           Yb[hh], ALU.mult, ALU.add)
                    v.wait_ge(s_ev.h, s_ev.inc(
                        v.scalar_tensor_tensor(carry[:, h * 64:(h + 1) * 64],
                                               carry[:, h * 64:(h + 1) * 64],
                                               cx.dbcc[:, c, h:h + 1],
                                               states[:, c, h * 64:(h + 1) * 64],
                                               ALU.mult, ALU.add), 1))
                    p2_yev[(c, h)] = s_ev.inc(
                        v.tensor_copy(carry_bf[:, h * 64:(h + 1) * 64],
                                      carry[:, h * 64:(h + 1) * 64]), 1)

            @b.scalar
            def _(s, c=c, g=g, gp=gp):
                if (c, g - 2) in p2_sil:
                    s.wait_ge(s_ev.h, p2_y2[(c, g - 2)])
                elif c >= 1 and g < 2:
                    s.wait_ge(s_ev.h, p2_y2[(c - 1, NG - 2 + g)])
                p2_sil[(c, g)] = cx.s_ld2[gp].inc(
                    s.dma_start(out=zgb[:, gp, :],
                                in_=cx.z_raw[g, :, c * CL:(c + 1) * CL]), 16)

            @b.vector
            def _(v, c=c, g=g, gp=gp):
                v.wait_ge(cx.s_ld2[gp].h, p2_sil[(c, g)])
                v.wait_ge(s_ev.h, p2_yev[(c, 2 * g + 1)])
                v.wait_ge(s_ev.h, s_ev.inc(
                    v.tensor_tensor(y2[:, gp, :], ygc[:, gp, :], zgb[:, gp, :], ALU.mult), 1))
                p2_sq[(c, g)] = s_ev.inc(
                    v.tensor_tensor(y2q[:, gp, :], y2[:, gp, :], y2[:, gp, :], ALU.mult), 1)
                p2_y2[(c, g)] = p2_sq[(c, g)]

            @b.tensor
            def _(t, c=c, g=g, gp=gp):
                t.wait_ge(s_ev.h, p2_sq[(c, g)])
                if (c, g - 1) in p2_rr:
                    t.wait_ge(s_sc.h, p2_rr[(c, g - 1)])
                elif c >= 1 and g == 0:
                    t.wait_ge(s_sc.h, p2_rr[(c - 1, NG - 1)])
                p2_ms[(c, g)] = s_pe.inc(
                    t.matmul(ps[:, 3072:3328][0:1, :], cx.ones_col[:, :], y2q[:, gp, :],
                             start=True, stop=True), 1)

            @b.scalar
            def _(s, c=c, g=g, gp=gp):
                s.wait_ge(s_pe.h, p2_ms[(c, g)])
                s.wait_ge(s_sc.h, s_sc.inc(
                    s.activation(rrow[:, gp, :], ps[:, 3072:3328][0:1, :], AF.Ln,
                                 scale=1.0 / 128.0, bias=cx.eps_col[0:1, :]), 1))
                p2_rr[(c, g)] = s_sc.inc(
                    s.activation(rrow[:, gp, :], rrow[:, gp, :], AF.Exp, scale=-0.5), 1)

            @b.tensor
            def _(t, c=c, g=g, gp=gp):
                t.wait_ge(s_sc.h, p2_rr[(c, g)])
                if (c, g - 1) in p2_yn:
                    t.wait_ge(s_ev.h, p2_yn[(c, g - 1)])
                elif c >= 1 and g == 0:
                    t.wait_ge(s_ev.h, p2_yn[(c - 1, NG - 1)])
                p2_rb[(c, g)] = s_pe.inc(
                    t.matmul(ps[:, 3584:3840][:, 0:256], cx.ones_row[:, :], rrow[0:1, gp, :],
                             start=True, stop=True), 1)

            @b.vector
            def _(v, c=c, g=g, gp=gp, cpar=cpar):
                v.wait_ge(s_pe.h, p2_rb[(c, g)])
                p2_yn[(c, g)] = s_ev.inc(
                    v.scalar_tensor_tensor(ynormF[:, g, cpar * CL:(cpar + 1) * CL],
                                           ps[:, 3584:3840][:, 0:256], cx.norm_sb[:, g:g + 1],
                                           y2[:, gp, :], ALU.mult, ALU.mult), 1)

        if c % 2 == 1:
            g2 = c // 2
            pallv = ps.rearrange("p (a b) -> p a b", a=2)
            yn_done = p2_yn[(c, NG - 1)]
            pe_op = {}
            ev_op = {}
            wld = {}
            x1_ld = {}
            pe0 = s_pe.v
            kk = 0
            for jp in range(2):
                for hc in range(NG):
                    pe_op[(jp, hc)] = pe0 + kk + 1
                    kk += 1
            if not hasattr(cx, "_op_pe_last"):
                cx._op_pe_last = {}
            cx._op_pe_last[g2] = pe0 + kk

            @b.gpsimd
            def _(g_, g2=g2, c=c):
                g_.wait_ge(s_pe.h, p2_Y[(c, NH - 1)])
                for jp in range(2):
                    for hc in range(NG):
                        wpar = hc % 2
                        if hc >= 2:
                            g_.wait_ge(s_pe.h, pe_op[(jp, hc - 2)])
                        elif jp >= 1:
                            g_.wait_ge(s_pe.h, pe_op[(jp - 1, NG - 2 + hc)])
                        elif g2 >= 1:
                            g_.wait_ge(s_pe.h, cx._op_pe_last[g2 - 1])
                        else:
                            g_.wait_ge(cx.s_wcv[0].h, bf["wout_conv_done"][0])
                            g_.wait_ge(cx.s_wcv[1].h, bf["wout_conv_done"][1])
                        wld[(jp, hc)] = s_ldw[wpar].inc(
                            g_.dma_start(out=woutb[:, wpar, :], in_=cx.wout_bfd[hc]), 16)

            for jp in range(2):
                for tm in (2 * jp, 2 * jp + 1):
                    tpar = tm % 2

                    @b.sync
                    def _(sy, tm=tm, tpar=tpar, g2=g2):
                        if tm >= 2:
                            sy.wait_ge(s_ev.h, ev_op[tm - 2])
                            sy.wait_ge(s_st[tpar].h, op_st[(g2, tm - 2)])
                        elif g2 >= 1:
                            sy.wait_ge(s_ev.h, op_ev[g2 - 1])
                            sy.wait_ge(s_st[tpar].h, op_st[(g2 - 1, 2 + tm)])
                        else:
                            sy.wait_ge(s_ev.h, p2_yev[(1, NH - 1)])
                        row0 = g2 * 512 + tm * 128
                        x1_ld[tm] = s_ld[tpar].inc(
                            sy.dma_start(out=x1t[:, tpar, :], in_=cx.x1[row0:row0 + 128, :]), 16)

                @b.tensor
                def _(t, jp=jp, g2=g2, yn_done=yn_done):
                    if jp == 0:
                        t.wait_ge(s_ev.h, yn_done)
                        if g2 >= 1:
                            t.wait_ge(s_ev.h, op_ev[g2 - 1])
                    else:
                        t.wait_ge(s_ev.h, ev_op[0])
                        t.wait_ge(s_ev.h, ev_op[1])
                    for hc in range(NG):
                        wpar = hc % 2
                        t.wait_ge(s_ldw[wpar].h, wld[(jp, hc)])
                        for tm in (2 * jp, 2 * jp + 1):
                            tpar = tm % 2
                            for hb in range(NHB):
                                ins = t.matmul(pallv[:, tpar, hb * 512:(hb + 1) * 512],
                                               ynormF[:, hc, tm * 128:(tm + 1) * 128],
                                               woutb[:, wpar, hb * 512:(hb + 1) * 512],
                                               start=(hc == 0), stop=(hc == NG - 1))
                        val = s_pe.inc(ins, 1)
                        assert val == pe_op[(jp, hc)]

                for tm in (2 * jp, 2 * jp + 1):
                    tpar = tm % 2

                    @b.vector
                    def _(v, jp=jp, tm=tm, tpar=tpar, g2=g2):
                        v.wait_ge(s_pe.h, pe_op[(jp, NG - 1)])
                        v.wait_ge(s_ld[tpar].h, x1_ld[tm])
                        if tm >= 2:
                            v.wait_ge(s_st[tpar].h, op_st[(g2, tm - 2)])
                        ev_op[tm] = s_ev.inc(
                            v.tensor_tensor(x2t[:, tpar, :], pallv[:, tpar, :],
                                            x1t[:, tpar, :], ALU.add), 1)

                    @b.sync
                    def _(sy, tm=tm, tpar=tpar, g2=g2):
                        sy.wait_ge(s_ev.h, ev_op[tm])
                        row0 = g2 * 512 + tm * 128
                        op_st[(g2, tm)] = s_st[tpar].inc(
                            sy.dma_start(out=cx.x2[row0:row0 + 128, :], in_=x2t[:, tpar, :]), 16)
            op_ev[g2] = s_ev.v

    cx.attn_done_st = [s_st[0].v, s_st[1].v]


def emit_final_wait(cx):
    b = cx.block

    @b.sync
    def _(sy):
        sy.wait_ge(cx.s_st[0].h, cx.s_st[0].v)
        sy.wait_ge(cx.s_st[1].h, cx.s_st[1].v)


def kernel(**inputs):
    per_core = _prep_core_inputs(inputs)
    cxp = build_program()
    res = run_bass_kernel_spmd(cxp.nc, per_core, core_ids=list(range(NCORES)))
    outs = [res.results[c]["out"] for c in range(NCORES)]
    return np.concatenate(outs, axis=0).reshape(B, L, D)


# version 08
# speedup vs baseline: 1.0001x; 1.0001x over previous
import contextlib
import numpy as np
import ml_dtypes
import concourse.bass as bass
import concourse.mybir as mybir
from concourse.bass_utils import run_bass_kernel_spmd

F32 = mybir.dt.float32
BF16 = mybir.dt.bfloat16
AF = mybir.ActivationFunctionType
ALU = mybir.AluOpType

NCORES = 8
B, L, D = 4, 4096, 2048
T = 2048
NTT = T // 128
HC = D // 128
NE, I = 16, 1792
NIM = I // 128
NHB = D // 512
HTOK = T // 2
NTM = HTOK // 128
DIN = 4096
DXB = 1024
CONVD = 6144
NFC = CONVD // 128
FIN = 10304
NFB = 81
NH = 64
CH, CL = 8, 256
NG = 32
EPS = 1e-5

STAGE = 3
SIM_SAFE = False
RGROUPS = [[0, 1], [2, 3], [4, 5], [6, 7]]



def _prep_core_inputs(inputs):
    f32 = np.float32
    gate = np.asarray(inputs["L0_ffn_gate_proj"], f32)
    up = np.asarray(inputs["L0_ffn_up_proj"], f32)
    down = np.asarray(inputs["L0_ffn_down_proj"], f32)
    W_in = np.asarray(inputs["L0_attn_W_in"], f32)
    W_out = np.asarray(inputs["L0_attn_W_out"], f32)
    router = np.asarray(inputs["L0_ffn_router_w"], f32)
    ebias = np.asarray(inputs["L0_ffn_expert_bias"], f32)
    conv_w = np.asarray(inputs["L0_attn_conv_w"], f32)
    conv_b = np.asarray(inputs["L0_attn_conv_b"], f32)
    dt_bias = np.asarray(inputs["L0_attn_dt_bias"], f32)
    A_log = np.asarray(inputs["L0_attn_A_log"], f32)
    Dv = np.asarray(inputs["L0_attn_D"], f32)
    nw = np.asarray(inputs["L0_attn_norm_weight"], f32)
    u = np.asarray(inputs["L0_attn_u"], f32).reshape(B * L, D)

    sh = {}
    sh["gate_blk"] = np.ascontiguousarray(
        gate.reshape(NE, HC, 128, NIM, 128).transpose(0, 3, 2, 1, 4).reshape(NE * NIM, 128, HC * 128))
    sh["up_blk"] = np.ascontiguousarray(
        up.reshape(NE, HC, 128, NIM, 128).transpose(0, 3, 2, 1, 4).reshape(NE * NIM, 128, HC * 128))
    sh["down_nat"] = np.ascontiguousarray(down.reshape(NE * NIM, 128, D))
    WinT = np.zeros((D, NFB * 128), f32)
    WinT[:, :FIN] = W_in.T
    sh["win_blk"] = np.ascontiguousarray(WinT.reshape(HC, 128, NFB, 128).transpose(2, 1, 0, 3))
    sh["wout_nat"] = np.ascontiguousarray(W_out.T.reshape(NG, 128, D))
    sh["router_t"] = np.ascontiguousarray(router.T.reshape(HC, 128, NE).transpose(1, 0, 2))
    sh["bias_row"] = np.ascontiguousarray(ebias.reshape(1, NE))
    sh["convw_t"] = np.ascontiguousarray(conv_w.reshape(NFC, 128, 4).transpose(1, 0, 2))
    sh["convb_t"] = np.ascontiguousarray(conv_b.reshape(NFC, 128).T)
    sh["dtb_col"] = np.ascontiguousarray(dt_bias.reshape(NH, 1))
    sh["alog_col"] = np.ascontiguousarray(A_log.reshape(NH, 1))
    sh["d_rep"] = np.ascontiguousarray(np.broadcast_to(Dv[None, :], (128, NH)).copy())
    sh["norm_t"] = np.ascontiguousarray(nw.reshape(NG, 128).T)
    sh["ident32"] = np.eye(128, dtype=f32)
    sh["identb"] = np.eye(128, dtype=f32).astype(ml_dtypes.bfloat16)
    sh["ones_row"] = np.ones((1, 128), f32)
    sh["ones_col"] = np.ones((128, 1), f32)
    sh["eps_col"] = np.full((128, 1), EPS, f32)
    tidx = np.arange(CL)[None, None, :]
    sidx = np.arange(128)[:, None, None] + 128 * np.arange(2)[None, :, None]
    sh["negbig"] = np.where(tidx >= sidx, 0.0, -1e9).astype(f32)

    per_core = []
    for c in range(NCORES):
        d = dict(sh)
        d["u_slice"] = np.ascontiguousarray(u[c * T:(c + 1) * T, :])
        half = c % 2
        d["fsend"] = np.full((128, 1), 1.0 if half == 0 else 0.0, f32)
        d["frecv"] = np.full((128, 1), 1.0 if half == 1 else 0.0, f32)
        per_core.append(d)
    return per_core



class Sem:
    __slots__ = ("h", "v", "name")

    def __init__(self, es, nc, name):
        self.h = es.enter_context(nc.semaphore(name))
        self.v = 0
        self.name = name

    def inc(self, ins, n=1):
        ins.then_inc(self.h, n)
        self.v += n
        return self.v


class Arena:
    def __init__(self, ap_u8, nbytes):
        self.ap = ap_u8
        self.nbytes = nbytes
        self.off = 0
        self.marks = []

    def push(self):
        self.marks.append(self.off)

    def pop(self):
        self.off = self.marks.pop()

    def _view(self, off, shape, dtype):
        esz = 2 if dtype == BF16 else 4
        n = 1
        for s in shape[1:]:
            n *= s
        nb = n * esz
        assert off + nb <= self.nbytes, f"arena overflow: {off}+{nb}>{self.nbytes}"
        v = self.ap[:, off:off + nb].bitcast(dtype)
        if len(shape) > 2:
            dims = shape[1:]
            pat_in = "p (" + " ".join(f"d{i}" for i in range(len(dims))) + ")"
            pat_out = "p " + " ".join(f"d{i}" for i in range(len(dims)))
            v = v.rearrange(pat_in + " -> " + pat_out, **{f"d{i}": dims[i] for i in range(len(dims))})
        if shape[0] != 128:
            v = v[0:shape[0]]
        return v, off + nb

    def alloc(self, shape, dtype):
        off = (self.off + 3) & ~3
        v, end = self._view(off, shape, dtype)
        self.off = end
        self.last_off = off
        return v

    def at(self, off_bytes, shape, dtype):
        v, _ = self._view(off_bytes, shape, dtype)
        return v


class Ctx:
    pass


def ssync(eng, sem, ins):
    if SIM_SAFE:
        eng.wait_ge(sem.h, sem.inc(ins, 1))
    return ins


def fence(cx, engines=("sync", "tensor", "vector", "scalar", "gpsimd")):
    b = cx.block
    sems = ([cx.s_pe, cx.s_ev, cx.s_sc, cx.s_ldc, cx.s_wd, cx.s_cc] + cx.s_st + cx.s_ld + cx.s_ld2 +
            cx.s_wg + cx.s_wu + cx.s_gx + getattr(cx, 's_ldx', []) + getattr(cx, 's_ldw', []) +
            getattr(cx, 's_stc', []) + getattr(cx, 's_ldcg', []) + getattr(cx, 's_lda', []) +
            getattr(cx, 's_cin', []) + getattr(cx, 's_wcv', []) + [x for x in [getattr(cx, 's_dt', None), getattr(cx, 's_po', None)] if x])
    vals = [(s.h, s.v) for s in sems if s.v > 0]

    for name in engines:
        @getattr(b, name)
        def _(e, vals=vals):
            for h, v in vals:
                e.wait_ge(h, v)



def build_program(stage=None):
    if stage is None:
        stage = STAGE
    nc = bass.Bass("TRN2", target_bir_lowering=False)
    cx = Ctx()
    cx.nc = nc
    cx.stage = stage

    def din(name, shape, dtype=F32):
        return nc.dram_tensor(name, shape, dtype, kind="ExternalInput").ap()

    cx.u = din("u_slice", [T, D])
    cx.gate_blkd = din("gate_blk", [NE * NIM, 128, HC * 128])
    cx.up_blkd = din("up_blk", [NE * NIM, 128, HC * 128])
    cx.down_natd = din("down_nat", [NE * NIM, 128, D])
    cx.win_blkd = din("win_blk", [NFB, 128, HC, 128])
    cx.wout_natd = din("wout_nat", [NG, 128, D])
    cx.router_td = din("router_t", [128, HC, NE])
    cx.bias_rowd = din("bias_row", [1, NE])
    cx.convw_td = din("convw_t", [128, NFC, 4])
    cx.convb_td = din("convb_t", [128, NFC])
    cx.dtb_cold = din("dtb_col", [NH, 1])
    cx.alog_cold = din("alog_col", [NH, 1])
    cx.d_repd = din("d_rep", [128, NH])
    cx.norm_td = din("norm_t", [128, NG])
    cx.ident32d = din("ident32", [128, 128])
    cx.identbd = din("identb", [128, 128], BF16)
    cx.ones_rowd = din("ones_row", [1, 128])
    cx.ones_cold = din("ones_col", [128, 1])
    cx.eps_cold = din("eps_col", [128, 1])
    cx.negbigd = din("negbig", [128, 2, CL])
    cx.fsendd = din("fsend", [128, 1])
    cx.frecvd = din("frecv", [128, 1])

    cx.out = nc.dram_tensor("out", [T, D], F32, kind="ExternalOutput").ap()

    def dsc(name, shape, dtype):
        return nc.dram_tensor(name, shape, dtype).ap()

    cx.xln1 = dsc("xln1", [T, D], BF16)
    cx.x1 = dsc("x1", [T, D], F32)
    cx.xln2 = dsc("xln2", [T, D], BF16)
    cx.x2 = dsc("x2", [T, D], F32)
    cx.xbc_raw = dsc("xbc_raw", [NFC, 128, T], BF16)
    cx.z_raw = dsc("z_raw", [NG, 128, T], BF16)
    cx.xbc_conv = dsc("xbc_conv", [NFC, 128, T], BF16)
    cx.acs_dram = dsc("acs_d", [NH, T], F32)
    cx.xbt_dram = dsc("xbt_dram2", [16, 128, 2048], BF16)
    cx.expacs_dram = dsc("expacs_d", [NH, T], BF16)
    cx.dall_dram = dsc("dall_d", [1, NH * CH], F32)
    cx.wout_bfd = dsc("wout_bf", [NG, 128, D], BF16)
    cx.ex1_in = dsc("ex1_in", [128, NFC * 3], F32)
    cx.ex1_out = dsc("ex1_out", [128, NFC * 3], F32)
    cx.ex2_in = dsc("ex2_in", [128, DIN], F32)
    cx.ex2_out = dsc("ex2_out", [128, DIN], F32)

    es = contextlib.ExitStack()
    cx.es = es
    es.__enter__()

    sbt = lambda name, shape, dt=F32: es.enter_context(nc.sbuf_tensor(name, shape, dt))
    cx.ident32 = sbt("c_id32", [128, 128])
    cx.identb = sbt("c_idb", [128, 128], BF16)
    cx.ones_row = sbt("c_ones_r", [1, 128])
    cx.ones_col = sbt("c_ones_c", [128, 1])
    cx.eps_col = sbt("c_eps_c", [128, 1])
    cx.router_t = sbt("c_router", [128, HC, NE])
    cx.biasr = sbt("c_biasr", [1, NE])
    cx.alpha = sbt("alpha", [128, NTT, NE])
    cx.dtbc = sbt("c_dtbc", [NH, 1])
    cx.alogc = sbt("c_alogc", [NH, 1])
    cx.d_rep = sbt("c_drep", [128, NH])
    cx.norm_sb = sbt("c_norm", [128, NG])
    cx.flags = sbt("c_flags", [128, 2])
    cx.statw = sbt("statw", [128, 8])
    cx.acol = sbt("acolA", [NH, 1])

    AR_BYTES = 206848
    cx.arena_t = es.enter_context(nc.sbuf_tensor("arena", [128, AR_BYTES], mybir.dt.uint8))
    cx.ar = Arena(cx.arena_t.ap(), AR_BYTES)

    cx.psum_t = es.enter_context(nc.psum_tensor("psum", [128, 4096], F32))
    cx.psum = cx.psum_t.ap()

    S = lambda n: Sem(es, nc, n)
    cx.s_ldc = S("s_ldc")
    cx.s_ld = [S("s_ld0"), S("s_ld1")]
    cx.s_st = [S("s_st0"), S("s_st1")]
    cx.s_ld2 = [S("s_ld2a"), S("s_ld2b")]
    cx.s_wg = [S("s_wg0"), S("s_wg1")]
    cx.s_wu = [S("s_wu0"), S("s_wu1")]
    cx.s_wd = S("s_wd")
    cx.s_gx = [S("s_gx0"), S("s_gx1")]
    cx.s_pe = S("s_pe")
    cx.s_ev = S("s_ev")
    cx.s_sc = S("s_sc")
    cx.s_po = S("s_po")
    cx.s_cc = S("s_cc")
    cx.s_ldx = [S("s_ldx0"), S("s_ldx1")]
    cx.s_ldw = [S("s_ldw0"), S("s_ldw1")]
    cx.s_stc = [S("s_stc0"), S("s_stc1")]
    cx.s_ldcg = [S("s_ldcg0"), S("s_ldcg1"), S("s_ldcg2"), S("s_ldcg3")]
    cx.s_lda = [S("s_lda0"), S("s_lda1")]
    cx.s_dt = S("s_dt")
    cx.s_wcv = [S("s_wcv0"), S("s_wcv1")]
    cx.s_cin = [S("s_cin0"), S("s_cin1")]

    cx.block = es.enter_context(nc.Block())

    emit_const_loads(cx)
    emit_ln_router(cx, src=cx.u, xln=cx.xln1)
    emit_ffn(cx, xln=cx.xln1, resid=cx.u, dst=(cx.x1 if stage >= 2 else cx.out))
    if stage >= 2:
        emit_attn(cx)
    if stage == 2:
        bb = cx.block

        @bb.sync
        def _(sy):
            sy.wait_ge(cx.s_st[0].h, cx.s_st[0].v)
            sy.wait_ge(cx.s_st[1].h, cx.s_st[1].v)
            cx.s_st[0].inc(sy.dma_start(out=cx.out, in_=cx.x2), 16)
    if stage >= 3:
        emit_ln_router(cx, src=cx.x2, xln=cx.xln2)
        emit_ffn(cx, xln=cx.xln2, resid=cx.x2, dst=cx.out)
    emit_final_wait(cx)
    return cx



def preload_nlexp(cx, eng):
    from concourse.hw_specs import get_activation_tables
    import concourse.mybir as _mb
    for idx, (name, funcs) in enumerate(get_activation_tables(cx.nc.m.arch).items()):
        if name == "natural_log_exp_and_others":
            return eng.add_instruction(
                _mb.InstLoadActFuncSet(
                    name=cx.nc.get_next_instruction_name(),
                    act_func_set_id=idx, ins=[], outs=[]))
    raise ValueError


def emit_const_loads(cx):
    b = cx.block
    s = cx.s_ldc

    @b.sync
    def _(sy):
        pairs = [
            (cx.ident32[:, :], cx.ident32d), (cx.identb[:, :], cx.identbd),
            (cx.ones_row[:, :], cx.ones_rowd), (cx.ones_col[:, :], cx.ones_cold),
            (cx.eps_col[:, :], cx.eps_cold),
            (cx.router_t[:, :, :], cx.router_td),
            (cx.biasr[:, :], cx.bias_rowd),
            (cx.dtbc[:, :], cx.dtb_cold),
            (cx.alogc[:, :], cx.alog_cold), (cx.d_rep[:, :], cx.d_repd),
            (cx.norm_sb[:, :], cx.norm_td),
            (cx.flags[:, 0:1], cx.fsendd), (cx.flags[:, 1:2], cx.frecvd),
        ]
        for dst, src in pairs:
            s.inc(sy.dma_start(out=dst, in_=src), 16)
    cx.c_const = s.v



def emit_ln_router(cx, src, xln):
    b, ar, ps = cx.block, cx.ar, cx.psum
    fence(cx)
    ar.push()
    xt = ar.alloc([128, 2, D], F32)
    scr32 = ar.alloc([128, 2, D], F32)
    xlnb = ar.alloc([128, 2, D], BF16)
    xf32 = ar.alloc([128, 2, 128], F32)
    jnkA = ar.alloc([128, D], F32)
    jnkB = ar.alloc([128, D], F32)
    logits = ar.alloc([128, NTT, NE], F32)
    lwork = ar.alloc([128, NTT, NE], F32)
    lwork2 = ar.alloc([128, NTT, NE], F32)
    awork = ar.alloc([128, NTT, 2], F32)

    s_ld, s_st = cx.s_ld, cx.s_st
    s_pe, s_ev, s_sc = cx.s_pe, cx.s_ev, cx.s_sc
    pslot = [ps[:, 2048:2176], ps[:, 2560:2688]]
    plog = ps[:, 3584:3600]

    ld_done = [None] * NTT
    st_done = [None] * NTT
    sc_sq = [None] * NTT
    sc_apply = [None] * NTT
    ev_stats = [None] * NTT
    ev_cast = [None] * NTT
    ev_evac = [[None] * HC for _ in range(NTT)]
    pe_tr = [[None] * HC for _ in range(NTT)]
    pe_mm = [None] * NTT
    lg = [None] * NTT

    M, SQ, NV, RS, NM, RST, NMR = 0, 1, 2, 3, 4, 5, 6

    for tt in range(NTT):
        par = tt % 2

        @b.sync
        def _(sy, tt=tt, par=par):
            if tt >= 2:
                sy.wait_ge(s_sc.h, sc_apply[tt - 2])
            ld_done[tt] = s_ld[par].inc(
                sy.dma_start(out=xt[:, par, :], in_=src[tt * 128:(tt + 1) * 128, :]), 16)

        @b.scalar
        def _(s, tt=tt, par=par):
            s.wait_ge(s_ld[par].h, ld_done[tt])
            if tt >= 2:
                s.wait_ge(s_pe.h, pe_tr[tt - 2][HC - 1])
            sc_sq[tt] = s_sc.inc(
                s.activation(jnkB[:, :], xt[:, par, :], AF.Square,
                             accum_out=cx.statw[:, SQ:SQ + 1]), 1)

        @b.vector
        def _(v, tt=tt, par=par):
            v.wait_ge(s_ld[par].h, ld_done[tt])
            acv = s_ev.inc(v.tensor_scalar(jnkA[:, :], xt[:, par, :], 1.0, None, ALU.mult,
                                           ALU.add, accum_out=cx.statw[:, M:M + 1]), 1)
            v.wait_ge(s_ev.h, acv)
            v.wait_ge(s_sc.h, sc_sq[tt])

            def vstep(ins):
                v.wait_ge(s_ev.h, s_ev.inc(ins, 1))

            vstep(v.tensor_scalar(cx.statw[:, M:M + 1], cx.statw[:, M:M + 1],
                                  1.0 / D, None, ALU.mult))
            vstep(v.tensor_scalar(cx.statw[:, SQ:SQ + 1], cx.statw[:, SQ:SQ + 1],
                                  1.0 / D, None, ALU.mult))
            vstep(v.scalar_tensor_tensor(cx.statw[:, NV:NV + 1], cx.statw[:, M:M + 1],
                                         cx.statw[:, M:M + 1], cx.statw[:, SQ:SQ + 1],
                                         ALU.mult, ALU.subtract))
            vstep(v.tensor_scalar(cx.statw[:, NV:NV + 1], cx.statw[:, NV:NV + 1], -1.0, EPS,
                                  ALU.mult, ALU.add))
            vstep(v.tensor_scalar(cx.statw[:, NM:NM + 1], cx.statw[:, M:M + 1], -1.0, None, ALU.mult))
            ev_stats[tt] = s_ev.inc(v.reciprocal(cx.statw[:, RS:RS + 1], cx.statw[:, NV:NV + 1]), 1)

        @b.scalar
        def _(s, tt=tt, par=par):
            s.wait_ge(s_ev.h, ev_stats[tt])
            sv = s_sc.inc(s.activation(cx.statw[:, RST:RST + 1], cx.statw[:, RS:RS + 1], AF.Sqrt), 1)
            s.wait_ge(s_sc.h, sv)
            sv2 = s_sc.inc(s.activation(cx.statw[:, NMR:NMR + 1], cx.statw[:, NM:NM + 1], AF.Identity,
                                        scale=cx.statw[:, RST:RST + 1]), 1)
            s.wait_ge(s_sc.h, sv2)
            sc_apply[tt] = s_sc.inc(
                s.activation(scr32[:, par, :], xt[:, par, :], AF.Identity,
                             scale=cx.statw[:, RST:RST + 1], bias=cx.statw[:, NMR:NMR + 1]), 1)

        @b.vector
        def _(v, tt=tt, par=par):
            v.wait_ge(s_sc.h, sc_apply[tt])
            if tt >= 2:
                v.wait_ge(s_st[par].h, st_done[tt - 2])
            ev_cast[tt] = s_ev.inc(v.tensor_copy(xlnb[:, par, :], scr32[:, par, :]), 1)

        @b.sync
        def _(sy, tt=tt, par=par):
            sy.wait_ge(s_ev.h, ev_cast[tt])
            st_done[tt] = s_st[par].inc(
                sy.dma_start(out=xln[tt * 128:(tt + 1) * 128, :], in_=xlnb[:, par, :]), 16)

        peb = s_pe.v
        evb = s_ev.v
        for hc in range(HC):
            pe_tr[tt][hc] = peb + hc + 1
            ev_evac[tt][hc] = evb + hc + 1
        pe_mm[tt] = peb + HC + 1
        lg[tt] = evb + HC + 1

        @b.tensor
        def _(t, tt=tt, par=par):
            if tt == 0:
                t.wait_ge(cx.s_ldc.h, cx.c_const)
            t.wait_ge(s_sc.h, sc_apply[tt])
            for hc in range(HC):
                sl = hc % 2
                if hc >= 2:
                    t.wait_ge(s_ev.h, ev_evac[tt][hc - 2])
                elif tt >= 1:
                    t.wait_ge(s_ev.h, ev_evac[tt - 1][HC - 2 + hc])
                t.transpose(pslot[sl], scr32[:, par, hc * 128:(hc + 1) * 128],
                            cx.ident32[:, :]).then_inc(s_pe.h, 1)
                s_pe.v += 1
                if hc >= 1:
                    t.wait_ge(s_ev.h, ev_evac[tt][hc - 1])
                    if hc == 1 and tt >= 1:
                        t.wait_ge(s_ev.h, lg[tt - 1])
                    t.matmul(plog, xf32[:, (hc - 1) % 2, :], cx.router_t[:, hc - 1, :],
                             start=(hc == 1), stop=False)
            t.wait_ge(s_ev.h, ev_evac[tt][HC - 1])
            t.matmul(plog, xf32[:, (HC - 1) % 2, :], cx.router_t[:, HC - 1, :],
                     start=False, stop=False)
            t.matmul(plog, cx.ones_row[:, :], cx.biasr[:, :],
                     start=False, stop=True).then_inc(s_pe.h, 1)
            s_pe.v += 1

        @b.vector
        def _(v, tt=tt, par=par):
            for hc in range(HC):
                sl = hc % 2
                v.wait_ge(s_pe.h, pe_tr[tt][hc])
                v.tensor_copy(xf32[:, sl, :], pslot[sl]).then_inc(s_ev.h, 1)
                s_ev.v += 1
            v.wait_ge(s_pe.h, pe_mm[tt])
            v.tensor_copy(logits[:, tt, :], plog).then_inc(s_ev.h, 1)
            s_ev.v += 1

        assert s_pe.v == pe_mm[tt] and s_ev.v == lg[tt]

    BIG = 1e9
    m_ = awork[:, :, 0:1]
    d_ = awork[:, :, 1:2]
    mbc = awork[:, :, 0:1].broadcast_to([128, NTT, NE])
    dbc = awork[:, :, 1:2].broadcast_to([128, NTT, NE])
    sc_exp = [None]
    ev_pre = [None]

    lg_all = s_ev.v

    @b.vector
    def _(v):
        def vstep(ins):
            v.wait_ge(s_ev.h, s_ev.inc(ins, 1))

        v.wait_ge(s_ev.h, lg_all)
        vstep(v.tensor_copy(lwork[:, :, :], logits[:, :, :]))
        vstep(v.tensor_reduce(m_, lwork[:, :, :], mybir.AxisListType.X, ALU.max))
        for r in range(3):
            vstep(v.tensor_tensor(lwork2[:, :, :], lwork[:, :, :], mbc, ALU.is_ge))
            vstep(v.tensor_scalar(lwork2[:, :, :], lwork2[:, :, :], -BIG, None, ALU.mult))
            vstep(v.tensor_tensor(lwork[:, :, :], lwork[:, :, :], lwork2[:, :, :], ALU.add))
            vstep(v.tensor_reduce(m_, lwork[:, :, :], mybir.AxisListType.X, ALU.max))
        vstep(v.tensor_tensor(lwork2[:, :, :], logits[:, :, :], mbc, ALU.is_ge))
        vstep(v.tensor_reduce(d_, logits[:, :, :], mybir.AxisListType.X, ALU.max))
        ev_pre[0] = s_ev.inc(
            v.tensor_tensor(lwork[:, :, :], logits[:, :, :], dbc, ALU.subtract), 1)

    @b.scalar
    def _(s):
        s.wait_ge(s_ev.h, ev_pre[0])
        sc_exp[0] = s_sc.inc(s.activation(lwork[:, :, :], lwork[:, :, :], AF.Exp), 1)

    @b.vector
    def _(v):
        def vstep(ins):
            v.wait_ge(s_ev.h, s_ev.inc(ins, 1))

        v.wait_ge(s_sc.h, sc_exp[0])
        vstep(v.tensor_tensor(lwork[:, :, :], lwork[:, :, :], lwork2[:, :, :], ALU.mult))
        vstep(v.tensor_reduce(d_, lwork[:, :, :], mybir.AxisListType.X, ALU.add))
        vstep(v.reciprocal(d_, d_))
        vstep(v.tensor_tensor(cx.alpha[:, :, :], lwork[:, :, :], dbc, ALU.mult))
        cx.alpha_done = s_ev.inc(
            v.tensor_scalar(cx.alpha[:, :, :], cx.alpha[:, :, :], 0.5, None, ALU.mult), 1)

    ar.pop()



def emit_ffn(cx, xln, resid, dst):
    b, ar, ps = cx.block, cx.ar, cx.psum
    fence(cx)
    ar.push()
    xlf = ar.alloc([128, HC, HTOK], BF16)
    acc = ar.alloc([128, NTM, D], F32)
    act = ar.alloc([128, NIM, HTOK], BF16)
    act_off0 = ar.last_off
    wg = ar.alloc([128, 2, HC, 128], BF16)
    wu = ar.alloc([128, 2, HC, 128], BF16)
    wd = ar.alloc([128, NIM, D], BF16)
    wd_off = ar.last_off
    xtile = ar.at(act_off0, [128, 2, D], BF16)
    rt = ar.at(wd_off, [128, 2, D], F32)
    ot = ar.at(wd_off + 16384, [128, 2, D], F32)

    s_ld, s_st = cx.s_ld, cx.s_st
    s_pe, s_ev, s_sc = cx.s_pe, cx.s_ev, cx.s_sc
    pall = ps.rearrange("p (a b) -> p a b", a=2)
    ptrb = [ps[:, 0:1024].bitcast(BF16).rearrange("p (a b) -> p a b", a=HC),
            ps[:, 1024:2048].bitcast(BF16).rearrange("p (a b) -> p a b", a=HC)]

    for hf in range(2):
        t0 = hf * HTOK
        ld_done = [None] * NTM
        tr_done = [None] * NTM
        evx = [None] * NTM
        pe_frontier = s_pe.v
        ev_frontier = s_ev.v
        for tm in range(NTM):
            par = tm % 2

            @b.sync
            def _(sy, tm=tm, par=par):
                if tm >= 2:
                    sy.wait_ge(s_pe.h, tr_done[tm - 2])
                else:
                    sy.wait_ge(s_pe.h, pe_frontier)
                ld_done[tm] = s_ld[par].inc(
                    sy.dma_start(out=xtile[:, par, :],
                                 in_=xln[t0 + tm * 128:t0 + (tm + 1) * 128, :]), 16)

            @b.tensor
            def _(t, tm=tm, par=par):
                t.wait_ge(s_ld[par].h, ld_done[tm])
                if tm >= 2:
                    t.wait_ge(s_ev.h, evx[tm - 2])
                else:
                    t.wait_ge(s_ev.h, ev_frontier)
                for hc in range(HC):
                    ins = t.transpose(ptrb[par][:, hc, :],
                                      xtile[:, par, hc * 128:(hc + 1) * 128],
                                      cx.identb[:, :])
                tr_done[tm] = s_pe.inc(ins, 1)

            @b.vector
            def _(v, tm=tm, par=par):
                v.wait_ge(s_pe.h, tr_done[tm])
                evx[tm] = s_ev.inc(
                    v.tensor_copy(xlf[:, :, tm * 128:(tm + 1) * 128], ptrb[par][:, :, :]), 1)

        xlf_ready = s_ev.v

        pe_base = s_pe.v
        ev_base = s_ev.v
        sc_base = s_sc.v
        pe_gu = [None] * (NE * NIM)
        pe_dn = [[None] * NTM for _ in range(NE)]
        ev_act = [None] * (NE * NIM)
        ev_stt = [[None] * NTM for _ in range(NE)]
        sc_silu = [None] * (NE * NIM)
        kk = 0
        for e in range(NE):
            for im in range(NIM):
                pe_gu[e * NIM + im] = pe_base + kk + 1
                ev_act[e * NIM + im] = ev_base + kk + 1
                kk += 1
            for tm in range(NTM):
                pe_dn[e][tm] = pe_base + kk + 1
                ev_stt[e][tm] = ev_base + kk + 1
                kk += 1
        for k in range(NE * NIM):
            sc_silu[k] = sc_base + k + 1

        w_done = [None] * (NE * NIM)
        wd_done = [None] * NE

        pe_entry = pe_base

        @b.gpsimd
        def _(g):
            for e in range(NE):
                for im in range(NIM):
                    k = e * NIM + im
                    par = k % 2
                    if k >= 2:
                        g.wait_ge(s_pe.h, pe_gu[k - 2])
                    else:
                        g.wait_ge(s_pe.h, pe_entry)
                    w_done[k] = cx.s_wg[par].inc(
                        g.dma_start(out=wg[:, par, :, :], in_=cx.gate_blkd[e * NIM + im]), 16)
                    cx.s_wu[par].inc(
                        g.dma_start(out=wu[:, par, :, :], in_=cx.up_blkd[e * NIM + im]), 16)
                if e >= 1:
                    g.wait_ge(s_pe.h, pe_dn[e - 1][NTM - 1])
                else:
                    g.wait_ge(s_st[0].h, s_st[0].v)
                    g.wait_ge(s_st[1].h, s_st[1].v)
                for ic in range(NIM):
                    wd_done[e] = cx.s_wd.inc(
                        g.dma_start(out=wd[:, ic, :], in_=cx.down_natd[e * NIM + ic]), 16)

        @b.tensor
        def _(t):
            t.wait_ge(s_ev.h, xlf_ready)
            for e in range(NE):
                for im in range(NIM):
                    k = e * NIM + im
                    par = k % 2
                    t.wait_ge(cx.s_wg[par].h, w_done[k])
                    t.wait_ge(cx.s_wu[par].h, w_done[k])
                    if k >= 2:
                        t.wait_ge(s_ev.h, ev_act[k - 2])
                    if im < 2 and e >= 1:
                        t.wait_ge(s_ev.h, ev_stt[e - 1][NTM - 2 + par])
                    for hc in range(HC):
                        t.matmul(pall[:, par, 0:512], wg[:, par, hc, :], xlf[:, hc, 0:512],
                                 start=(hc == 0), stop=(hc == HC - 1))
                        t.matmul(pall[:, par, 512:1024], wg[:, par, hc, :], xlf[:, hc, 512:1024],
                                 start=(hc == 0), stop=(hc == HC - 1))
                    for hc in range(HC):
                        t.matmul(pall[:, par, 1024:1536], wu[:, par, hc, :], xlf[:, hc, 0:512],
                                 start=(hc == 0), stop=(hc == HC - 1))
                        ins = t.matmul(pall[:, par, 1536:2048], wu[:, par, hc, :],
                                       xlf[:, hc, 512:1024],
                                       start=(hc == 0), stop=(hc == HC - 1))
                    ins.then_inc(s_pe.h, 1)
                    s_pe.v += 1
                t.wait_ge(cx.s_wd.h, wd_done[e])
                t.wait_ge(s_ev.h, ev_act[e * NIM + NIM - 1])
                for tm in range(NTM):
                    par = tm % 2
                    if tm >= 2:
                        t.wait_ge(s_ev.h, ev_stt[e][tm - 2])
                    elif e >= 1:
                        t.wait_ge(s_ev.h, ev_stt[e - 1][NTM - 2 + tm])
                    for ic in range(NIM):
                        for hb in range(NHB):
                            ins = t.matmul(pall[:, par, hb * 512:(hb + 1) * 512],
                                           act[:, ic, tm * 128:(tm + 1) * 128],
                                           wd[:, ic, hb * 512:(hb + 1) * 512],
                                           start=(ic == 0), stop=(ic == NIM - 1))
                    ins.then_inc(s_pe.h, 1)
                    s_pe.v += 1

        @b.scalar
        def _(s):
            for e in range(NE):
                for im in range(NIM):
                    k = e * NIM + im
                    par = k % 2
                    s.wait_ge(s_pe.h, pe_gu[k])
                    s.activation(act[:, im, :], pall[:, par, 0:1024], AF.Silu).then_inc(s_sc.h, 1)
                    s_sc.v += 1

        @b.vector
        def _(v):
            v.wait_ge(s_ev.h, cx.alpha_done)
            for e in range(NE):
                for im in range(NIM):
                    k = e * NIM + im
                    par = k % 2
                    v.wait_ge(s_sc.h, sc_silu[k])
                    v.tensor_tensor(act[:, im, :], act[:, im, :],
                                    pall[:, par, 1024:2048], ALU.mult).then_inc(s_ev.h, 1)
                    s_ev.v += 1
                for tm in range(NTM):
                    par = tm % 2
                    v.wait_ge(s_pe.h, pe_dn[e][tm])
                    aco = cx.alpha[:, hf * NTM + tm, e:e + 1]
                    for hb in range(NHB):
                        dstv = acc[:, tm, hb * 512:(hb + 1) * 512]
                        srcv = pall[:, par, hb * 512:(hb + 1) * 512]
                        if e == 0:
                            ins = v.tensor_scalar(dstv, srcv, aco, None, ALU.mult)
                        else:
                            ins = v.scalar_tensor_tensor(dstv, srcv, aco, dstv, ALU.mult, ALU.add)
                    ins.then_inc(s_ev.h, 1)
                    s_ev.v += 1

        assert s_pe.v == pe_base + NE * (NIM + NTM)
        assert s_ev.v == ev_base + NE * (NIM + NTM)
        assert s_sc.v == sc_base + NE * NIM

        rld = [None] * NTM
        evo = [None] * NTM
        sto = [None] * NTM
        pe_all_done = s_pe.v
        for tm in range(NTM):
            par = tm % 2

            @b.sync
            def _(sy, tm=tm, par=par):
                if tm >= 2:
                    sy.wait_ge(s_ev.h, evo[tm - 2])
                else:
                    sy.wait_ge(s_pe.h, pe_all_done)
                rld[tm] = s_ld[par].inc(
                    sy.dma_start(out=rt[:, par, :],
                                 in_=resid[t0 + tm * 128:t0 + (tm + 1) * 128, :]), 16)

            @b.vector
            def _(v, tm=tm, par=par):
                v.wait_ge(s_ld[par].h, rld[tm])
                v.wait_ge(s_ev.h, ev_stt[NE - 1][tm])
                if tm >= 2:
                    v.wait_ge(s_st[par].h, sto[tm - 2])
                evo[tm] = s_ev.inc(
                    v.tensor_tensor(ot[:, par, :], acc[:, tm, :], rt[:, par, :], ALU.add), 1)

            @b.sync
            def _(sy, tm=tm, par=par):
                sy.wait_ge(s_ev.h, evo[tm])
                sto[tm] = s_st[par].inc(
                    sy.dma_start(out=dst[t0 + tm * 128:t0 + (tm + 1) * 128, :],
                                 in_=ot[:, par, :]), 16)
    ar.pop()



def emit_attn(cx):
    b, ar, ps = cx.block, cx.ar, cx.psum
    fence(cx)
    ar.push()
    cx.negbig = ar.alloc([128, 2, CL], F32)
    cx.convw = ar.alloc([128, NFC, 4], F32)
    cx.convb = ar.alloc([128, NFC], F32)
    cx.dbcc = ar.alloc([128, CH, NH], F32)
    cx.ddTf = ar.alloc([128, 16, NH], F32)
    cx.biasTf = ar.alloc([128, 16, NH], F32)

    s_ld, s_st, s_ld2 = cx.s_ld, cx.s_st, cx.s_ld2
    s_pe, s_ev, s_sc, s_cc = cx.s_pe, cx.s_ev, cx.s_sc, cx.s_cc
    s_gx, s_ldx, s_ldw, s_stc, s_ldcg = cx.s_gx, cx.s_ldx, cx.s_ldw, cx.s_stc, cx.s_ldcg

    @b.sync
    def _(sy):
        for dst, src in [(cx.negbig[:, :, :], cx.negbigd), (cx.convw[:, :, :], cx.convw_td),
                         (cx.convb[:, :], cx.convb_td)]:
            cx.s_ldc.inc(sy.dma_start(out=dst, in_=src), 16)
    cx.attn_const = cx.s_ldc.v

    ar.push()
    dtA = ar.alloc([NH, T], F32)
    dtB = ar.alloc([NH, T], F32)
    dtC = ar.alloc([NH, T], F32)
    dtD = ar.alloc([NH, T], F32)
    dcol = ar.alloc([NH, CH], F32)
    drow = ar.alloc([1, NH * CH], F32)
    uaf = ar.alloc([128, HC, T], BF16)
    uaf_off = ar.last_off
    xt = ar.alloc([128, 2, D], F32)
    xlnb = ar.alloc([128, 2, D], BF16)
    xlnb_off = ar.last_off
    wbuf = ar.alloc([128, 2, HC, 128], BF16)
    bounce = ar.alloc([128, 2, D], BF16)
    bounce_off = ar.last_off
    ex1sb = ar.alloc([128, NFC, 3], F32)
    tailr = ar.alloc([128, NFC * 3], F32)
    cin = ar.alloc([128, 2, T + 3], BF16)
    cv = ar.alloc([128, 2, T], F32)
    cv_off = ar.last_off
    cout = ar.alloc([128, 2, T], BF16)
    cout_off = ar.last_off
    wcv = ar.alloc([128, 2, D], BF16)
    jnkB = ar.at(cout_off, [128, D], F32)
    jnkA = ar.at(cv_off, [128, D], F32)
    jnkdt = ar.at(xlnb_off, [128, D], F32)

    M, SQ, NV, RS, NM, RST, NMR = 0, 1, 2, 3, 4, 5, 6
    pslotb = [ps[:, 0:1024].bitcast(BF16).rearrange("p (a b) -> p a b", a=HC),
              ps[:, 1024:2048].bitcast(BF16).rearrange("p (a b) -> p a b", a=HC)]

    ld_done = [None] * NTT
    sc_apply = [None] * NTT
    tr_done = [None] * NTT
    evx = [None] * NTT
    ev_entry0 = s_ev.v
    for tt in range(NTT):
        par = tt % 2

        @b.sync
        def _(sy, tt=tt, par=par):
            if tt >= 2:
                sy.wait_ge(s_sc.h, sc_apply[tt - 2])
            ld_done[tt] = s_ld[par].inc(
                sy.dma_start(out=xt[:, par, :], in_=cx.x1[tt * 128:(tt + 1) * 128, :]), 16)

        @b.scalar
        def _(s, tt=tt, par=par):
            s.wait_ge(s_ld[par].h, ld_done[tt])
            if tt >= 2:
                s.wait_ge(s_pe.h, tr_done[tt - 2])
            s.activation(jnkB[:, :], xt[:, par, :], AF.Square,
                         accum_out=cx.statw[:, SQ:SQ + 1]).then_inc(s_sc.h, 1)
            s_sc.v += 1
        sc_sq = s_sc.v

        @b.vector
        def _(v, tt=tt, par=par, sc_sq=sc_sq):
            def vstep(ins):
                v.wait_ge(s_ev.h, s_ev.inc(ins, 1))

            v.wait_ge(s_ld[par].h, ld_done[tt])
            vstep(v.tensor_scalar(jnkA[:, :], xt[:, par, :], 1.0, None, ALU.mult,
                                  ALU.add, accum_out=cx.statw[:, M:M + 1]))
            v.wait_ge(s_sc.h, sc_sq)
            vstep(v.tensor_scalar(cx.statw[:, M:M + 1], cx.statw[:, M:M + 1],
                                  1.0 / D, None, ALU.mult))
            vstep(v.tensor_scalar(cx.statw[:, SQ:SQ + 1], cx.statw[:, SQ:SQ + 1],
                                  1.0 / D, None, ALU.mult))
            vstep(v.scalar_tensor_tensor(cx.statw[:, NV:NV + 1], cx.statw[:, M:M + 1],
                                         cx.statw[:, M:M + 1], cx.statw[:, SQ:SQ + 1],
                                         ALU.mult, ALU.subtract))
            vstep(v.tensor_scalar(cx.statw[:, NV:NV + 1], cx.statw[:, NV:NV + 1], -1.0, EPS,
                                  ALU.mult, ALU.add))
            vstep(v.tensor_scalar(cx.statw[:, NM:NM + 1], cx.statw[:, M:M + 1], -1.0, None,
                                  ALU.mult))
            s_ev.inc(v.reciprocal(cx.statw[:, RS:RS + 1], cx.statw[:, NV:NV + 1]), 1)
        ev_stats = s_ev.v

        @b.scalar
        def _(s, tt=tt, par=par, ev_stats=ev_stats):
            s.wait_ge(s_ev.h, ev_stats)
            s.wait_ge(s_sc.h, s_sc.inc(
                s.activation(cx.statw[:, RST:RST + 1], cx.statw[:, RS:RS + 1], AF.Sqrt), 1))
            s.wait_ge(s_sc.h, s_sc.inc(
                s.activation(cx.statw[:, NMR:NMR + 1], cx.statw[:, NM:NM + 1], AF.Identity,
                             scale=cx.statw[:, RST:RST + 1]), 1))
            sc_apply[tt] = s_sc.inc(
                s.activation(xlnb[:, par, :], xt[:, par, :], AF.Identity,
                             scale=cx.statw[:, RST:RST + 1], bias=cx.statw[:, NMR:NMR + 1]), 1)

        @b.tensor
        def _(t, tt=tt, par=par):
            t.wait_ge(s_sc.h, sc_apply[tt])
            if tt >= 2:
                t.wait_ge(s_ev.h, evx[tt - 2])
            else:
                t.wait_ge(s_ev.h, ev_entry0)
            for hc in range(HC):
                ins = t.transpose(pslotb[par][:, hc, :], xlnb[:, par, hc * 128:(hc + 1) * 128],
                                  cx.identb[:, :])
            tr_done[tt] = s_pe.inc(ins, 1)

        @b.vector
        def _(v, tt=tt, par=par):
            v.wait_ge(s_pe.h, tr_done[tt])
            evx[tt] = s_ev.inc(
                v.tensor_copy(uaf[:, :, tt * 128:(tt + 1) * 128], pslotb[par][:, :, :]), 1)

    uaf_ready = s_ev.v

    wcv_ld = [None] * NG
    wcv_st = [None] * NG
    for i in range(NG):
        par = i % 2
        wcv_st[i] = cx.s_wcv[par].v + 16 * (i // 2 + 1)

    @b.gpsimd
    def _(g):
        for i in range(NG):
            par = i % 2
            if i >= 2:
                g.wait_ge(cx.s_wcv[par].h, wcv_st[i - 2])
            else:
                g.wait_ge(s_ev.h, ev_entry0)
            wcv_ld[i] = s_ldw[par].inc(
                g.dma_start(out=wcv[:, par, :], in_=cx.wout_natd[i]), 16)

    @b.sync
    def _(sy):
        for i in range(NG):
            par = i % 2
            sy.wait_ge(s_ldw[par].h, wcv_ld[i])
            val = cx.s_wcv[par].inc(sy.dma_start(out=cx.wout_bfd[i], in_=wcv[:, par, :]), 16)
            assert val == wcv_st[i]
    wout_conv_done = [cx.s_wcv[0].v, cx.s_wcv[1].v]

    fb_order = list(range(32, NFB)) + list(range(0, 32))
    pall = ps.rearrange("p (a b) -> p a b", a=2)
    w_ld = {}
    pe_fb = {}
    ev_fb = {}
    st_fb = {}
    pe_base = s_pe.v
    ev_base = s_ev.v
    for j, fb in enumerate(fb_order):
        pe_fb[fb] = pe_base + j + 1
        ev_fb[fb] = ev_base + j + 1
    stv = [cx.s_st[0].v, cx.s_st[1].v]
    for j, fb in enumerate(fb_order):
        par = j % 2
        if fb != 80:
            stv[par] += 16
            st_fb[fb] = stv[par]
    evv = ev_base
    for j, fb in enumerate(fb_order):
        if 32 <= fb < 80:
            evv += 2
        elif fb == 80:
            evv += 1
        ev_fb[fb] = evv
    sc_fb = {}
    scv = s_sc.v
    for j, fb in enumerate(fb_order):
        if fb < 32:
            scv += 1
            sc_fb[fb] = scv

    @b.gpsimd
    def _(g):
        for j, fb in enumerate(fb_order):
            par = j % 2
            if j >= 2:
                g.wait_ge(s_pe.h, pe_fb[fb_order[j - 2]])
            else:
                g.wait_ge(s_ev.h, uaf_ready)
            w_ld[fb] = s_gx[par].inc(
                g.dma_start(out=wbuf[:, par, :, :], in_=cx.win_blkd[fb]), 16)

    @b.tensor
    def _(t):
        t.wait_ge(s_ev.h, uaf_ready)
        for j, fb in enumerate(fb_order):
            par = j % 2
            t.wait_ge(s_gx[par].h, w_ld[fb])
            if j >= 2:
                pfb = fb_order[j - 2]
                if pfb < 32:
                    t.wait_ge(s_sc.h, sc_fb[pfb])
                else:
                    t.wait_ge(s_ev.h, ev_fb[pfb])
            for hc in range(HC):
                for tq in range(4):
                    ins = t.matmul(pall[:, par, tq * 512:(tq + 1) * 512],
                                   wbuf[:, par, hc, :], uaf[:, hc, tq * 512:(tq + 1) * 512],
                                   start=(hc == 0), stop=(hc == HC - 1))
            val = s_pe.inc(ins, 1)
            assert val == pe_fb[fb]

    @b.vector
    def _(v):
        for j, fb in enumerate(fb_order):
            if fb < 32:
                continue
            par = j % 2
            v.wait_ge(s_pe.h, pe_fb[fb])
            jj = j - 2
            while jj >= 0 and fb_order[jj] == 80:
                jj -= 2
            if jj >= 0:
                v.wait_ge(s_st[par].h, st_fb[fb_order[jj]])
            if fb == 80:
                v.tensor_copy(dtA[:, :], pall[0:NH, par, :]).then_inc(s_ev.h, 1)
                s_ev.v += 1
            else:
                cpv = s_ev.inc(v.tensor_copy(bounce[:, par, :], pall[:, par, :]), 1)
                if SIM_SAFE:
                    v.wait_ge(s_ev.h, cpv)
                v.tensor_scalar(ex1sb[:, fb - 32, :], bounce[:, par, 2045:2048],
                                cx.flags[:, 0:1], None, ALU.mult).then_inc(s_ev.h, 1)
                s_ev.v += 1
            assert s_ev.v == ev_fb[fb]

    @b.scalar
    def _(sc):
        for j, fb in enumerate(fb_order):
            if fb >= 32:
                continue
            par = j % 2
            sc.wait_ge(s_pe.h, pe_fb[fb])
            jj = j - 2
            while jj >= 0 and fb_order[jj] == 80:
                jj -= 2
            if jj >= 0:
                sc.wait_ge(s_st[par].h, st_fb[fb_order[jj]])
            sc.activation(bounce[:, par, :], pall[:, par, :], AF.Silu).then_inc(s_sc.h, 1)
            s_sc.v += 1
            assert s_sc.v == sc_fb[fb]
    dt_ready = ev_fb[80]

    @b.sync
    def _(sy):
        for j, fb in enumerate(fb_order):
            if fb == 80:
                continue
            par = j % 2
            if 32 <= fb < 80:
                sy.wait_ge(s_ev.h, ev_fb[fb])
                val = s_st[par].inc(sy.dma_start(out=cx.xbc_raw[fb - 32], in_=bounce[:, par, :]), 16)
            else:
                sy.wait_ge(s_sc.h, sc_fb[fb])
                val = s_st[par].inc(sy.dma_start(out=cx.z_raw[fb], in_=bounce[:, par, :]), 16)
            assert val == st_fb[fb]

    xbc_export_ev = ev_fb[79]
    inproj_st_frontier = [cx.s_st[0].v, cx.s_st[1].v]

    @b.gpsimd
    def _(g):
        g.wait_ge(s_ev.h, xbc_export_ev)
        g.dma_start(out=cx.ex1_in, in_=ex1sb.rearrange("p a b -> p (a b)")).then_inc(cx.s_po.h, 16)
        cx.s_po.v += 16
        g.wait_ge(cx.s_po.h, cx.s_po.v)
        g.collective_compute(
            "AllReduce", ALU.add,
            replica_groups=RGROUPS,
            ins=[cx.ex1_in.opt()], outs=[cx.ex1_out.opt()],
        ).then_inc(s_cc.h, 1)
        s_cc.v += 1
        g.wait_ge(s_cc.h, s_cc.v)
        g.dma_start(out=tailr[:, :], in_=cx.ex1_out).then_inc(cx.s_po.h, 16)
        cx.s_po.v += 16
    tail_ld = cx.s_po.v

    tail_ev = [None]

    @b.vector
    def _(v):
        v.wait_ge(cx.s_po.h, tail_ld)
        v.wait_ge(cx.s_ldc.h, cx.attn_const)
        tail_ev[0] = s_ev.inc(
            v.tensor_scalar(tailr[:, :], tailr[:, :], cx.flags[:, 1:2], None, ALU.mult), 1)

    dt_prep2(cx, dtA, dtB, dtC, dtD, dcol, drow, jnkdt, dt_ready)

    tailr3 = tailr.rearrange("p (f k) -> p f k", f=NFC)
    cv_ev = {}
    cv_st = {}
    for fc in range(NFC):
        par = fc % 2

        @b.gpsimd
        def _(gq, fc=fc, par=par):
            if fc >= 2:
                gq.wait_ge(s_ev.h, cv_ev[fc - 2])
                sem = s_st if fc - 2 < 16 else s_stc
                gq.wait_ge(sem[par].h, cv_st[fc - 2])
            gq.wait_ge(s_st[0].h, inproj_st_frontier[0])
            gq.wait_ge(s_st[1].h, inproj_st_frontier[1])
            cx.s_cin[par].inc(gq.dma_start(out=cin[:, par, 3:3 + T], in_=cx.xbc_raw[fc]), 16)
        cin_ld = cx.s_cin[par].v

        @b.vector
        def _(v, fc=fc, par=par, cin_ld=cin_ld):
            v.wait_ge(cx.s_cin[par].h, cin_ld)
            v.wait_ge(s_ev.h, tail_ev[0])
            ssync(v, s_ev, v.tensor_copy(cin[:, par, 0:3], tailr3[:, fc, :]))
            ssync(v, s_ev, v.tensor_scalar(cv[:, par, :], cin[:, par, 0:T], cx.convw[:, fc, 0:1],
                                           cx.convb[:, fc:fc + 1], ALU.mult, ALU.add))
            for k in range(1, 4):
                ins = v.scalar_tensor_tensor(cv[:, par, :], cin[:, par, k:k + T],
                                             cx.convw[:, fc, k:k + 1], cv[:, par, :],
                                             ALU.mult, ALU.add)
                if k < 3:
                    ssync(v, s_ev, ins)
            cv_ev[fc] = s_ev.inc(ins, 1)

        @b.scalar
        def _(s, fc=fc, par=par):
            s.wait_ge(s_ev.h, cv_ev[fc])
            s.activation(cout[:, par, :], cv[:, par, :], AF.Silu).then_inc(s_sc.h, 1)
            s_sc.v += 1
        sil_done = s_sc.v

        @b.sync
        def _(sy, fc=fc, par=par, sil_done=sil_done):
            sy.wait_ge(s_sc.h, sil_done)
            sem = s_st if fc < 16 else s_stc
            cv_st[fc] = sem[par].inc(
                sy.dma_start(out=cx.xbc_conv[fc], in_=cout[:, par, :]), 16)

    xb_conv_st = [s_st[0].v, s_st[1].v]
    conv_all_st = [s_stc[0].v, s_stc[1].v]
    conv_done_ev = s_ev.v


    emit_xbt(cx, uaf_off, bounce_off, xb_conv_st)

    ar.pop()

    ar.push()
    bufs = {}
    bufs["carry"] = ar.alloc([128, DIN], F32)
    bufs["carry_bf"] = ar.alloc([128, DIN], BF16)
    bufs["states"] = ar.alloc([128, CH, DIN], BF16)
    states_off = ar.last_off
    bufs["xbt"] = ar.alloc([128, 2, 2, T], BF16)
    bufs["bck"] = ar.alloc([128, 2, 8, CL], BF16)
    bufs["xfm"] = ar.alloc([128, 2, 8, CL], BF16)
    bufs["xdd"] = ar.alloc([128, 4, 2, 64], BF16)
    bufs["segb"] = ar.alloc([128, 2, 2, 2, CL], F32)
    segb_off = ar.last_off
    bufs["Lb"] = ar.alloc([128, 2, 2, 2, CL], BF16)
    bufs["glb"] = ar.alloc([128, 2, 2, 2, CL], BF16)
    bufs["ebcb"] = ar.alloc([128, 2, 2, CL], BF16)
    bufs["csb"] = ar.alloc([128, 2, 2, CL], BF16)
    bufs["cgb"] = ar.alloc([128, 4, CL], BF16)
    bufs["zgb"] = ar.alloc([128, 2, CL], BF16)
    bufs["ygc"] = ar.alloc([128, 2, CL], F32)
    bufs["y2"] = ar.alloc([128, 2, CL], F32)
    bufs["rrow"] = ar.alloc([1, 2, CL], F32)
    bufs["ynormF"] = ar.alloc([128, NG, 2 * CL], BF16)
    ynorm_off = ar.last_off
    bufs["exc"] = ar.at(ynorm_off, [128, DIN], F32)
    bufs["x1t"] = ar.at(states_off, [128, 2, D], F32)
    bufs["woutb"] = ar.at(segb_off, [128, 2, D], BF16)
    bufs["xb_conv_st"] = xb_conv_st
    bufs["conv_all_st"] = conv_all_st
    bufs["wout_conv_done"] = wout_conv_done

    emit_ssd2(cx, bufs)
    ar.pop()
    ar.pop()



def emit_xbt(cx, xbfm_off, stage_off, conv_xb_st):
    b, ar, ps = cx.block, cx.ar, cx.psum
    s_pe, s_ev = cx.s_pe, cx.s_ev
    xbfm = ar.at(xbfm_off, [128, 16, T], BF16)
    stage = ar.at(stage_off, [128, 2, 16, 128], BF16)
    ld = [None] * 16
    trd = {}
    evd = {}
    std = [None] * 16
    ptr = [ps[:, 0:512].bitcast(BF16).rearrange("p (a b) -> p a b", a=8),
           ps[:, 512:1024].bitcast(BF16).rearrange("p (a b) -> p a b", a=8)]

    @b.sync
    def _(sy):
        sy.wait_ge(cx.s_st[0].h, conv_xb_st[0])
        sy.wait_ge(cx.s_st[1].h, conv_xb_st[1])
        for fc in range(16):
            ld[fc] = cx.s_ldx[fc % 2].inc(
                sy.dma_start(out=xbfm[:, fc, :], in_=cx.xbc_conv[fc]), 16)
    ld_all = [cx.s_ldx[0].v, cx.s_ldx[1].v]

    for tc in range(16):
        par = tc % 2
        for half in range(2):
            bk = (2 * tc + half) % 2

            @b.tensor
            def _(t, tc=tc, half=half, bk=bk):
                if tc == 0 and half == 0:
                    t.wait_ge(cx.s_ldx[0].h, ld_all[0])
                    t.wait_ge(cx.s_ldx[1].h, ld_all[1])
                key = (tc, half - 1) if half == 1 else (tc - 1, 1)
                if key in evd:
                    t.wait_ge(s_ev.h, evd[key])
                for i in range(8):
                    fc = half * 8 + i
                    ins = t.transpose(ptr[bk][:, i, :], xbfm[:, fc, tc * 128:(tc + 1) * 128],
                                      cx.identb[:, :])
                trd[(tc, half)] = s_pe.inc(ins, 1)

            @b.vector
            def _(v, tc=tc, half=half, bk=bk, par=par):
                v.wait_ge(s_pe.h, trd[(tc, half)])
                if tc >= 2:
                    v.wait_ge(cx.s_st[par].h, std[tc - 2])
                evd[(tc, half)] = s_ev.inc(
                    v.tensor_copy(stage[:, par, half * 8:(half + 1) * 8, :],
                                  ptr[bk][:, :, :]), 1)

        @b.sync
        def _(sy, tc=tc, par=par):
            sy.wait_ge(s_ev.h, evd[(tc, 1)])
            std[tc] = cx.s_st[par].inc(
                sy.dma_start(out=cx.xbt_dram[tc],
                             in_=stage[:, par, :, :].rearrange("p a b -> p (a b)")), 16)
    cx.xbt_st = [cx.s_st[0].v, cx.s_st[1].v]


def dt_prep2(cx, dtA, dtB, dtC, dtD, dcol, drow, jnk, dt_ready):
    b, ps = cx.block, cx.psum
    s_ev, s_sc, s_pe = cx.s_ev, cx.s_sc, cx.s_pe
    jnk64 = jnk[0:NH, :]
    ev0 = [None] * 8

    @b.vector
    def _(v):
        def vstep(ins):
            v.wait_ge(s_ev.h, s_ev.inc(ins, 1))

        v.wait_ge(s_ev.h, dt_ready)
        vstep(v.tensor_scalar(dtA[:, :], dtA[:, :], cx.dtbc[:, :], None, ALU.add))
        ev0[0] = s_ev.inc(v.tensor_scalar(dtB[:, :], dtA[:, :], 0.0, None, ALU.max), 1)

    @b.scalar
    def _(s):
        s.wait_ge(s_ev.h, ev0[0])
        s.wait_ge(s_sc.h, s_sc.inc(s.activation(jnk64[:, :], dtA[:, :], AF.Abs), 1))
        s.wait_ge(s_sc.h, s_sc.inc(s.activation(jnk64[:, :], jnk64[:, :], AF.Exp, scale=-1.0), 1))
        s.wait_ge(s_sc.h, s_sc.inc(s.activation(jnk64[:, :], jnk64[:, :], AF.Ln, bias=1.0), 1))
        ev0[1] = s_sc.inc(s.activation(cx.acol[:, :], cx.alogc[:, :], AF.Exp), 1)

    @b.vector
    def _(v):
        def vstep(ins):
            v.wait_ge(s_ev.h, s_ev.inc(ins, 1))

        v.wait_ge(s_sc.h, ev0[1])
        vstep(v.tensor_tensor(dtA[:, :], dtB[:, :], jnk64[:, :], ALU.add))
        vstep(v.tensor_scalar(dtA[:, :], dtA[:, :], 1e-30, None, ALU.max))
        ev0[2] = s_ev.inc(v.tensor_scalar(cx.acol[:, :], cx.acol[:, :], -1.0, None, ALU.mult), 1)

    @b.scalar
    def _(s):
        s.wait_ge(s_ev.h, ev0[2])
        ev0[3] = s_sc.inc(s.activation(dtD[:, :], dtA[:, :], AF.Ln), 1)

    @b.vector
    def _(v):
        def vstep(ins):
            v.wait_ge(s_ev.h, s_ev.inc(ins, 1))

        v.wait_ge(s_sc.h, ev0[3])
        vstep(v.tensor_scalar(dtC[:, :], dtA[:, :], cx.acol[:, :], None, ALU.mult))
        src, dst = dtC, jnk64
        sh = 1
        while sh < CL:
            sv = src.rearrange("p (c t) -> p c t", t=CL)
            dv = dst.rearrange("p (c t) -> p c t", t=CL)
            ssync(v, s_ev, v.tensor_copy(dv[:, :, 0:sh], sv[:, :, 0:sh]))
            ssync(v, s_ev, v.tensor_tensor(dv[:, :, sh:CL], sv[:, :, sh:CL],
                                           sv[:, :, 0:CL - sh], ALU.add))
            src, dst = dst, src
            sh *= 2
        assert src is dtC
        ev0[4] = s_ev.inc(v.tensor_tensor(dtD[:, :], dtD[:, :], dtC[:, :], ALU.subtract), 1)

    @b.scalar
    def _(s):
        s.wait_ge(s_ev.h, ev0[4])
        c3 = dtC.rearrange("p (c t) -> p c t", t=CL)
        b3 = dtB.rearrange("p (c t) -> p c t", t=CL)
        for c in range(CH):
            ins = s.activation(b3[:, c, :], c3[:, c, :], AF.Exp, scale=-1.0,
                               bias=c3[:, c, CL - 1:CL])
        s.wait_ge(s_sc.h, s_sc.inc(ins, 1))
        dc3 = dcol.unsqueeze(2)
        ev0[5] = s_sc.inc(s.activation(dc3[:, :, :], c3[:, :, CL - 1:CL], AF.Exp), 1)

    @b.vector
    def _(v):
        v.wait_ge(s_sc.h, ev0[5])
        ev0[6] = s_ev.inc(v.tensor_tensor(dtB[:, :], dtA[:, :], dtB[:, :], ALU.mult), 1)


    ddv = cx.ddTf
    bsv = cx.biasTf
    trv = {}
    evv = {}
    for i, (srcT, dstT, rdy, sname) in enumerate(
            [(dtB, ddv, ev0[6], "dd"), (dtD, bsv, ev0[4], "bs")]):
        for tc in range(16):
            sl = tc % 2

            @b.tensor
            def _(t, i=i, tc=tc, sl=sl, srcT=srcT, rdy=rdy):
                t.wait_ge(s_ev.h, rdy)
                if (i, tc - 2) in evv:
                    t.wait_ge(s_ev.h, evv[(i, tc - 2)])
                elif i == 1 and tc < 2:
                    t.wait_ge(s_ev.h, evv[(0, 14 + tc)])
                poff = 3072 if sl == 0 else 3584
                trv[(i, tc)] = s_pe.inc(
                    t.transpose(ps[:, poff:poff + 64],
                                srcT[:, tc * 128:(tc + 1) * 128], cx.ident32[0:NH, 0:NH]), 1)

            @b.vector
            def _(v, i=i, tc=tc, sl=sl, dstT=dstT):
                v.wait_ge(s_pe.h, trv[(i, tc)])
                poff = 3072 if sl == 0 else 3584
                evv[(i, tc)] = s_ev.inc(
                    v.tensor_copy(dstT[:, tc, :], ps[:, poff:poff + 64]), 1)

    @b.tensor
    def _(t):
        t.wait_ge(s_sc.h, ev0[5])
        t.wait_ge(s_ev.h, evv[(1, 15)])
        ev0[7] = s_pe.inc(t.transpose(ps[:, 3584:3648][0:CH, :], dcol[:, :],
                                      cx.ident32[0:NH, 0:NH]), 1)

    dtr_ev = [None]

    @b.vector
    def _(v):
        v.wait_ge(s_pe.h, ev0[7])
        dtr_ev[0] = s_ev.inc(v.tensor_copy(jnk[0:CH, 0:NH], ps[:, 3584:3648][0:CH, :]), 1)

    dall_st = [None]
    dall_ld = [None]
    acs_st = [None]
    expa_sb = jnk.bitcast(BF16)[0:NH, 0:T]
    expa_ev = [None]

    @b.sync
    def _(sy):
        sy.wait_ge(s_ev.h, dtr_ev[0])
        dall_st[0] = cx.s_dt.inc(sy.dma_start(out=cx.dall_dram, in_=jnk[0:CH, 0:NH]), 16)
        sy.wait_ge(cx.s_dt.h, dall_st[0])
        dall_ld[0] = cx.s_dt.inc(sy.dma_start(out=drow[:, :], in_=cx.dall_dram), 16)
        sy.wait_ge(cx.s_dt.h, dall_ld[0])
        sy.wait_ge(s_ev.h, ev0[4])
        cx.s_dt.inc(sy.dma_start(out=cx.acs_dram, in_=dtC[:, :]), 16)

    dbv = cx.dbcc
    cx._dbc_ev = {}
    for c in range(CH):
        @b.tensor
        def _(t, c=c):
            if c == 0:
                t.wait_ge(cx.s_dt.h, dall_ld[0])
            else:
                t.wait_ge(s_ev.h, cx._dbc_ev[c - 1])
            cx._dbc_pe = s_pe.inc(
                t.matmul(ps[:, 3648:3712], cx.ones_row[:, :],
                         drow[0:1, c * NH:(c + 1) * NH], start=True, stop=True), 1)

        @b.vector
        def _(v, c=c):
            v.wait_ge(s_pe.h, cx._dbc_pe)
            cx._dbc_ev[c] = s_ev.inc(v.tensor_copy(dbv[:, c, :], ps[:, 3648:3712]), 1)
    @b.scalar
    def _(s):
        s.wait_ge(s_ev.h, ev0[4])
        s.wait_ge(cx.s_dt.h, dall_ld[0])
        expa_ev[0] = s_sc.inc(s.activation(expa_sb, dtC[:, :], AF.Exp), 1)

    @b.sync
    def _(sy):
        sy.wait_ge(s_sc.h, expa_ev[0])
        acs_st[0] = cx.s_dt.inc(sy.dma_start(out=cx.expacs_dram, in_=expa_sb), 16)

    cx.dt_done_ev = cx._dbc_ev[CH - 1]
    cx.acs_st = acs_st[0]


def emit_ssd2(cx, bf):
    b, ps = cx.block, cx.psum
    s_pe, s_ev, s_sc, s_cc = cx.s_pe, cx.s_ev, cx.s_sc, cx.s_cc
    s_ld, s_st, s_ld2 = cx.s_ld, cx.s_st, cx.s_ld2
    s_ldx, s_ldw, s_ldcg = cx.s_ldx, cx.s_ldw, cx.s_ldcg

    carry, carry_bf, states = bf["carry"], bf["carry_bf"], bf["states"]
    xbt, bck, xfm = bf["xbt"], bf["bck"], bf["xfm"]
    xdd = bf["xdd"]
    segb, Lb, glb = bf["segb"], bf["Lb"], bf["glb"]
    ebcb, csb, cgb = bf["ebcb"], bf["csb"], bf["cgb"]
    zgb, ygc, y2 = bf["zgb"], bf["ygc"], bf["y2"]
    y2q = ygc
    rrow = bf["rrow"]
    ynormF = bf["ynormF"]
    exc, x1t, woutb = bf["exc"], bf["x1t"], bf["woutb"]
    x2t = x1t

    xbt_ld = [None] * CH
    p1_mm = {}
    p1_xdd = {}
    p1_cu = {}
    st_banks = [ps[:, 0:64], ps[:, 512:576], ps[:, 1024:1088], ps[:, 1536:1600]]

    for c in range(CH):
        cpar = c % 2

        @b.sync
        def _(sy, c=c, cpar=cpar):
            if c >= 2:
                sy.wait_ge(s_pe.h, p1_mm[(c - 2, NH - 1)])
            else:
                sy.wait_ge(s_st[0].h, cx.xbt_st[0])
                sy.wait_ge(s_st[1].h, cx.xbt_st[1])
            xbt_ld[c] = s_ldx[cpar].inc(
                sy.dma_start(out=xbt[:, cpar, :, :],
                             in_=cx.xbt_dram[2 * c:2 * c + 2].rearrange("a p f -> p a f")), 16)

        for h in range(NH):
            hp = h % 4
            xc0 = (h // 8) * 128 + (h % 2) * 64

            @b.vector
            def _(v, c=c, h=h, hp=hp, xc0=xc0, cpar=cpar):
                v.wait_ge(s_ldx[cpar].h, xbt_ld[c])
                if (c, h - 4) in p1_mm:
                    v.wait_ge(s_pe.h, p1_mm[(c, h - 4)])
                elif c >= 1 and h < 4:
                    v.wait_ge(s_pe.h, p1_mm[(c - 1, NH - 4 + h)])
                if c == 0 and h < 4:
                    v.wait_ge(s_ev.h, cx.dt_done_ev)
                    v.wait_ge(cx.s_dt.h, cx.acs_st)
                for st in range(2):
                    ins = v.tensor_scalar(xdd[:, hp, st, :], xbt[:, cpar, st, xc0:xc0 + 64],
                                          cx.ddTf[:, 2 * c + st, h:h + 1], None, ALU.mult)
                p1_xdd[(c, h)] = s_ev.inc(ins, 1)

            @b.tensor
            def _(t, c=c, h=h, hp=hp, cpar=cpar):
                t.wait_ge(s_ev.h, p1_xdd[(c, h)])
                if (c, h - 4) in p1_cu:
                    t.wait_ge(s_ev.h, p1_cu[(c, h - 4)])
                elif c >= 1 and h < 4:
                    t.wait_ge(s_ev.h, p1_cu[(c - 1, NH - 4 + h)])
                bc0 = 1024 + (h // 8) * 128
                for st in range(2):
                    ins = t.matmul(st_banks[hp], xbt[:, cpar, st, bc0:bc0 + 128],
                                   xdd[:, hp, st, :], start=(st == 0), stop=(st == 1))
                p1_mm[(c, h)] = s_pe.inc(ins, 1)

            @b.vector
            def _(v, c=c, h=h, hp=hp):
                v.wait_ge(s_pe.h, p1_mm[(c, h)])
                v.tensor_copy(states[:, c, h * 64:(h + 1) * 64], st_banks[hp])
                if c == 0:
                    ins = v.tensor_copy(carry[:, h * 64:(h + 1) * 64], st_banks[hp])
                else:
                    ins = v.scalar_tensor_tensor(
                        carry[:, h * 64:(h + 1) * 64], carry[:, h * 64:(h + 1) * 64],
                        cx.dbcc[:, c, h:h + 1], st_banks[hp], ALU.mult, ALU.add)
                p1_cu[(c, h)] = s_ev.inc(ins, 1)

    exp_ev = [None]
    cin_ev = [None]

    @b.vector
    def _(v):
        v.wait_ge(s_ev.h, p1_cu[(CH - 1, NH - 1)])
        v.wait_ge(cx.s_stc[0].h, bf["conv_all_st"][0])
        v.wait_ge(cx.s_stc[1].h, bf["conv_all_st"][1])
        v.wait_ge(cx.s_wcv[0].h, bf["wout_conv_done"][0])
        v.wait_ge(cx.s_wcv[1].h, bf["wout_conv_done"][1])
        exp_ev[0] = s_ev.inc(
            v.tensor_scalar(exc[:, :], carry[:, :], cx.flags[:, 0:1], None, ALU.mult), 1)

    @b.gpsimd
    def _(g):
        g.wait_ge(s_ev.h, exp_ev[0])
        g.dma_start(out=cx.ex2_in, in_=exc[:, :]).then_inc(cx.s_po.h, 16)
        cx.s_po.v += 16
        g.wait_ge(cx.s_po.h, cx.s_po.v)
        g.collective_compute(
            "AllReduce", ALU.add,
            replica_groups=RGROUPS,
            ins=[cx.ex2_in.opt()], outs=[cx.ex2_out.opt()],
        ).then_inc(s_cc.h, 1)
        s_cc.v += 1
        g.wait_ge(s_cc.h, s_cc.v)
        g.dma_start(out=exc[:, :], in_=cx.ex2_out).then_inc(cx.s_po.h, 16)
        cx.s_po.v += 16
    exc_ld = cx.s_po.v

    @b.vector
    def _(v):
        v.wait_ge(cx.s_po.h, exc_ld)
        v.wait_ge(s_ev.h, s_ev.inc(
            v.tensor_scalar(carry[:, :], exc[:, :], cx.flags[:, 1:2], None, ALU.mult), 1))
        cin_ev[0] = s_ev.inc(v.tensor_copy(carry_bf[:, :], carry[:, :]), 1)

    p2_xbt = [None] * CH
    p2_G = {}
    p2_seg = {}
    p2_pre = {}
    p2_ea = {}
    p2_L = {}
    p2_gl = {}
    p2_Y = {}
    p2_yev = {}
    p2_sil = {}
    p2_y2 = {}
    p2_sq = {}
    p2_ms = {}
    p2_rr = {}
    p2_rb = {}
    p2_yn = {}
    p2_cg = {}
    op_ev = {}
    op_st = {}
    segb, Lb, glb = bf["segb"], bf["Lb"], bf["glb"]
    ebcb, csb, cgb = bf["ebcb"], bf["csb"], bf["cgb"]
    expab = ebcb
    Gb = [ps[:, i * 512:(i + 1) * 512].rearrange("p (a b) -> p a b", a=2) for i in range(4)]
    Yb = [ps[:, 2048:2304][0:64, :], ps[:, 2560:2816][0:64, :]]

    for c in range(CH):
        cpar = c % 2

        @b.sync
        def _(sy, c=c, cpar=cpar):
            if c >= 2:
                sy.wait_ge(s_ev.h, p2_yev[(c - 2, NH - 1)])
            else:
                sy.wait_ge(s_ev.h, cin_ev[0])
            sy.dma_start(out=xbt[:, cpar, :, :],
                         in_=cx.xbt_dram[2 * c:2 * c + 2].rearrange("a p f -> p a f")
                         ).then_inc(s_ldx[cpar].h, 16)
            s_ldx[cpar].v += 16
            sy.dma_start(out=bck[:, cpar, :, :],
                         in_=cx.xbc_conv[8:16, :, c * CL:(c + 1) * CL].rearrange(
                             "f p t -> p f t")).then_inc(s_ldx[cpar].h, 16)
            s_ldx[cpar].v += 16
            p2_xbt[c] = s_ldx[cpar].inc(
                sy.dma_start(out=xfm[:, cpar, :, :],
                             in_=cx.xbc_conv[0:8, :, c * CL:(c + 1) * CL].rearrange(
                                 "f p t -> p f t")), 16)

        for g in range(NG):
            gp4 = g % 4
            gp = g % 2
            pp = g % 2

            @b.vector
            def _(v, c=c, g=g, pp=pp):
                if (c, g - 2) in p2_L:
                    v.wait_ge(s_sc.h, p2_L[(c, g - 2)])
                elif c >= 1 and g < 2:
                    v.wait_ge(s_sc.h, p2_L[(c - 1, NG - 2 + g)])
                elif c == 0 and g < 2:
                    v.wait_ge(cx.s_ldc.h, cx.attn_const)
                    v.wait_ge(s_ev.h, cin_ev[0])
                p2_pre[(c, g)] = s_ev.inc(v.tensor_copy(
                    segb[:, pp, :, :, :],
                    cx.negbig.unsqueeze(1).broadcast_to([128, 2, 2, CL])), 1)

            @b.gpsimd
            def _(g_, c=c, g=g, pp=pp):
                g_.wait_ge(s_ev.h, p2_pre[(c, g)])
                if c == 0 and g < 2:
                    g_.wait_ge(cx.s_dt.h, cx.acs_st)
                for hh in range(2):
                    g_.dma_start(out=segb[:, pp, hh, :, :],
                                 in_=cx.acs_dram[2 * g + hh:2 * g + hh + 1,
                                                 c * CL:(c + 1) * CL]
                                 .unsqueeze(0).broadcast_to([128, 2, CL]),
                                 accum_op=ALU.add).then_inc(cx.s_lda[pp].h, 16)
                    cx.s_lda[pp].v += 16
                p2_seg[(c, g)] = cx.s_lda[pp].inc(
                    g_.dma_start(out=expab[:, pp, :, :],
                                 in_=cx.expacs_dram[2 * g:2 * g + 2, c * CL:(c + 1) * CL]
                                 .unsqueeze(0).broadcast_to([128, 2, CL])), 16)

            @b.sync
            def _(sy, c=c, g=g, gp=gp):
                if (c, 2 * (g - 4) + 1) in p2_gl:
                    sy.wait_ge(s_ev.h, p2_gl[(c, 2 * (g - 4) + 1)])
                elif c >= 1 and g < 4:
                    sy.wait_ge(s_ev.h, p2_gl[(c - 1, 2 * (NG - 4 + g) + 1)])
                else:
                    sy.wait_ge(cx.s_stc[0].h, bf["conv_all_st"][0])
                    sy.wait_ge(cx.s_stc[1].h, bf["conv_all_st"][1])
                p2_cg[(c, g)] = s_ldcg[g % 4].inc(
                    sy.dma_start(out=cgb[:, g % 4, :],
                                 in_=cx.xbc_conv[16 + g, :, c * CL:(c + 1) * CL]), 16)

            @b.tensor
            def _(t, c=c, g=g, gp4=gp4, cpar=cpar):
                if c >= 2 and c % 2 == 0 and g == 0:
                    t.wait_ge(s_ev.h, op_ev[(c - 2) // 2])
                t.wait_ge(s_ldcg[g % 4].h, p2_cg[(c, g)])
                t.wait_ge(s_ldx[cpar].h, p2_xbt[c])
                if (c, 2 * (g - 4) + 1) in p2_gl:
                    t.wait_ge(s_ev.h, p2_gl[(c, 2 * (g - 4) + 1)])
                elif c >= 1 and g < 4:
                    t.wait_ge(s_ev.h, p2_gl[(c - 1, 2 * (NG - 4 + g) + 1)])
                for st in range(2):
                    ins = t.matmul(Gb[gp4][:, st, :],
                                   bck[:, cpar, g // 4, st * 128:(st + 1) * 128],
                                   cgb[:, g % 4, :], start=True, stop=True)
                p2_G[(c, g)] = s_pe.inc(ins, 1)

            @b.scalar
            def _(s, c=c, g=g, pp=pp):
                s.wait_ge(cx.s_lda[pp].h, p2_seg[(c, g)])
                if c == 0 and g == 0:
                    s.wait_ge(s_ev.h, cx.dt_done_ev)
                for hh in range(2):
                    hgl = 2 * g + hh
                    for st in range(2):
                        ins = s.activation(Lb[:, pp, hh, st, :], segb[:, pp, hh, st, :], AF.Exp,
                                           bias=cx.biasTf[:, 2 * c + st, hgl:hgl + 1])
                p2_L[(c, g)] = s_sc.inc(ins, 1)

            @b.vector
            def _(v, c=c, g=g, pp=pp, gp4=gp4):
                v.wait_ge(s_sc.h, p2_L[(c, g)])
                v.wait_ge(s_pe.h, p2_G[(c, g)])
                v.tensor_tensor(csb[:, pp, :, :],
                                cgb[:, g % 4, :].unsqueeze(1).broadcast_to([128, 2, CL]),
                                expab[:, pp, :, :], ALU.mult)
                p2_gl[(c, 2 * g + 1)] = s_ev.inc(v.tensor_tensor(
                    glb[:, pp, :, :, :],
                    Gb[gp4].unsqueeze(1).broadcast_to([128, 2, 2, CL]),
                    Lb[:, pp, :, :, :], ALU.mult), 1)

            for hh in range(2):
                h = 2 * g + hh
                xc0 = (h // 8) * 128 + (h % 2) * 64

                @b.tensor
                def _(t, c=c, g=g, h=h, hh=hh, pp=pp, cpar=cpar, xc0=xc0):
                    t.wait_ge(s_ev.h, p2_gl[(c, 2 * g + 1)])
                    if (c, h - 2) in p2_yev:
                        t.wait_ge(s_ev.h, p2_yev[(c, h - 2)])
                    elif c >= 1 and h < 2:
                        t.wait_ge(s_ev.h, p2_yev[(c - 1, NH - 2 + h)])
                    t.matmul(Yb[hh], xbt[:, cpar, 0, xc0:xc0 + 64], glb[:, pp, hh, 0, :],
                             start=True, stop=False)
                    t.matmul(Yb[hh], xbt[:, cpar, 1, xc0:xc0 + 64], glb[:, pp, hh, 1, :],
                             start=False, stop=False)
                    p2_Y[(c, h)] = s_pe.inc(
                        t.matmul(Yb[hh], carry_bf[:, h * 64:(h + 1) * 64], csb[:, pp, hh, :],
                                 start=False, stop=True), 1)

                @b.vector
                def _(v, c=c, g=g, h=h, hh=hh, gp=gp, cpar=cpar):
                    v.wait_ge(s_pe.h, p2_Y[(c, h)])
                    po = (h % 2) * 64
                    v.scalar_tensor_tensor(ygc[hh * 64:hh * 64 + 64, gp, :],
                                           xfm[po:po + 64, cpar, h // 8, :],
                                           cx.d_rep[po:po + 64, h:h + 1],
                                           Yb[hh], ALU.mult, ALU.add)
                    ssync(v, s_ev, v.scalar_tensor_tensor(
                        carry[:, h * 64:(h + 1) * 64], carry[:, h * 64:(h + 1) * 64],
                        cx.dbcc[:, c, h:h + 1], states[:, c, h * 64:(h + 1) * 64],
                        ALU.mult, ALU.add))
                    p2_yev[(c, h)] = s_ev.inc(
                        v.tensor_copy(carry_bf[:, h * 64:(h + 1) * 64],
                                      carry[:, h * 64:(h + 1) * 64]), 1)

            @b.scalar
            def _(s, c=c, g=g, gp=gp):
                if (c, g - 2) in p2_sil:
                    s.wait_ge(s_ev.h, p2_y2[(c, g - 2)])
                elif c >= 1 and g < 2:
                    s.wait_ge(s_ev.h, p2_y2[(c - 1, NG - 2 + g)])
                p2_sil[(c, g)] = cx.s_ld2[gp].inc(
                    s.dma_start(out=zgb[:, gp, :],
                                in_=cx.z_raw[g, :, c * CL:(c + 1) * CL]), 16)

            @b.vector
            def _(v, c=c, g=g, gp=gp):
                v.wait_ge(cx.s_ld2[gp].h, p2_sil[(c, g)])
                v.wait_ge(s_ev.h, p2_yev[(c, 2 * g + 1)])
                ssync(v, s_ev, v.tensor_tensor(y2[:, gp, :], ygc[:, gp, :], zgb[:, gp, :], ALU.mult))
                p2_sq[(c, g)] = s_ev.inc(
                    v.tensor_tensor(y2q[:, gp, :], y2[:, gp, :], y2[:, gp, :], ALU.mult), 1)
                p2_y2[(c, g)] = p2_sq[(c, g)]

            @b.tensor
            def _(t, c=c, g=g, gp=gp):
                t.wait_ge(s_ev.h, p2_sq[(c, g)])
                if (c, g - 1) in p2_rr:
                    t.wait_ge(s_sc.h, p2_rr[(c, g - 1)])
                elif c >= 1 and g == 0:
                    t.wait_ge(s_sc.h, p2_rr[(c - 1, NG - 1)])
                p2_ms[(c, g)] = s_pe.inc(
                    t.matmul(ps[:, 3072:3328][0:1, :], cx.ones_col[:, :], y2q[:, gp, :],
                             start=True, stop=True), 1)

            @b.scalar
            def _(s, c=c, g=g, gp=gp):
                s.wait_ge(s_pe.h, p2_ms[(c, g)])
                s.wait_ge(s_sc.h, s_sc.inc(
                    s.activation(rrow[:, gp, :], ps[:, 3072:3328][0:1, :], AF.Ln,
                                 scale=1.0 / 128.0, bias=cx.eps_col[0:1, :]), 1))
                p2_rr[(c, g)] = s_sc.inc(
                    s.activation(rrow[:, gp, :], rrow[:, gp, :], AF.Exp, scale=-0.5), 1)

            @b.tensor
            def _(t, c=c, g=g, gp=gp):
                t.wait_ge(s_sc.h, p2_rr[(c, g)])
                if (c, g - 1) in p2_yn:
                    t.wait_ge(s_ev.h, p2_yn[(c, g - 1)])
                elif c >= 1 and g == 0:
                    t.wait_ge(s_ev.h, p2_yn[(c - 1, NG - 1)])
                p2_rb[(c, g)] = s_pe.inc(
                    t.matmul(ps[:, 3584:3840][:, 0:256], cx.ones_row[:, :], rrow[0:1, gp, :],
                             start=True, stop=True), 1)

            @b.vector
            def _(v, c=c, g=g, gp=gp, cpar=cpar):
                v.wait_ge(s_pe.h, p2_rb[(c, g)])
                p2_yn[(c, g)] = s_ev.inc(
                    v.scalar_tensor_tensor(ynormF[:, g, cpar * CL:(cpar + 1) * CL],
                                           ps[:, 3584:3840][:, 0:256], cx.norm_sb[:, g:g + 1],
                                           y2[:, gp, :], ALU.mult, ALU.mult), 1)

        if c % 2 == 1:
            g2 = c // 2
            pallv = ps.rearrange("p (a b) -> p a b", a=2)
            yn_done = p2_yn[(c, NG - 1)]
            pe_op = {}
            ev_op = {}
            wld = {}
            x1_ld = {}
            pe0 = s_pe.v
            kk = 0
            for jp in range(2):
                for hc in range(NG):
                    pe_op[(jp, hc)] = pe0 + kk + 1
                    kk += 1
            if not hasattr(cx, "_op_pe_last"):
                cx._op_pe_last = {}
            cx._op_pe_last[g2] = pe0 + kk

            @b.gpsimd
            def _(g_, g2=g2, c=c):
                g_.wait_ge(s_pe.h, p2_Y[(c, NH - 1)])
                for jp in range(2):
                    for hc in range(NG):
                        wpar = hc % 2
                        if hc >= 2:
                            g_.wait_ge(s_pe.h, pe_op[(jp, hc - 2)])
                        elif jp >= 1:
                            g_.wait_ge(s_pe.h, pe_op[(jp - 1, NG - 2 + hc)])
                        elif g2 >= 1:
                            g_.wait_ge(s_pe.h, cx._op_pe_last[g2 - 1])
                        else:
                            g_.wait_ge(cx.s_wcv[0].h, bf["wout_conv_done"][0])
                            g_.wait_ge(cx.s_wcv[1].h, bf["wout_conv_done"][1])
                        wld[(jp, hc)] = s_ldw[wpar].inc(
                            g_.dma_start(out=woutb[:, wpar, :], in_=cx.wout_bfd[hc]), 16)

            for jp in range(2):
                for tm in (2 * jp, 2 * jp + 1):
                    tpar = tm % 2

                    @b.sync
                    def _(sy, tm=tm, tpar=tpar, g2=g2):
                        if tm >= 2:
                            sy.wait_ge(s_ev.h, ev_op[tm - 2])
                            sy.wait_ge(s_st[tpar].h, op_st[(g2, tm - 2)])
                        elif g2 >= 1:
                            sy.wait_ge(s_ev.h, op_ev[g2 - 1])
                            sy.wait_ge(s_st[tpar].h, op_st[(g2 - 1, 2 + tm)])
                        else:
                            sy.wait_ge(s_ev.h, p2_yev[(1, NH - 1)])
                        row0 = g2 * 512 + tm * 128
                        x1_ld[tm] = s_ld[tpar].inc(
                            sy.dma_start(out=x1t[:, tpar, :], in_=cx.x1[row0:row0 + 128, :]), 16)

                @b.tensor
                def _(t, jp=jp, g2=g2, yn_done=yn_done):
                    if jp == 0:
                        t.wait_ge(s_ev.h, yn_done)
                        if g2 >= 1:
                            t.wait_ge(s_ev.h, op_ev[g2 - 1])
                    else:
                        t.wait_ge(s_ev.h, ev_op[0])
                        t.wait_ge(s_ev.h, ev_op[1])
                    for hc in range(NG):
                        wpar = hc % 2
                        t.wait_ge(s_ldw[wpar].h, wld[(jp, hc)])
                        for tm in (2 * jp, 2 * jp + 1):
                            tpar = tm % 2
                            for hb in range(NHB):
                                ins = t.matmul(pallv[:, tpar, hb * 512:(hb + 1) * 512],
                                               ynormF[:, hc, tm * 128:(tm + 1) * 128],
                                               woutb[:, wpar, hb * 512:(hb + 1) * 512],
                                               start=(hc == 0), stop=(hc == NG - 1))
                        val = s_pe.inc(ins, 1)
                        assert val == pe_op[(jp, hc)]

                for tm in (2 * jp, 2 * jp + 1):
                    tpar = tm % 2

                    @b.vector
                    def _(v, jp=jp, tm=tm, tpar=tpar, g2=g2):
                        v.wait_ge(s_pe.h, pe_op[(jp, NG - 1)])
                        v.wait_ge(s_ld[tpar].h, x1_ld[tm])
                        if tm >= 2:
                            v.wait_ge(s_st[tpar].h, op_st[(g2, tm - 2)])
                        ev_op[tm] = s_ev.inc(
                            v.tensor_tensor(x2t[:, tpar, :], pallv[:, tpar, :],
                                            x1t[:, tpar, :], ALU.add), 1)

                    @b.sync
                    def _(sy, tm=tm, tpar=tpar, g2=g2):
                        sy.wait_ge(s_ev.h, ev_op[tm])
                        row0 = g2 * 512 + tm * 128
                        op_st[(g2, tm)] = s_st[tpar].inc(
                            sy.dma_start(out=cx.x2[row0:row0 + 128, :], in_=x2t[:, tpar, :]), 16)
            op_ev[g2] = s_ev.v

    cx.attn_done_st = [s_st[0].v, s_st[1].v]


def emit_final_wait(cx):
    b = cx.block

    @b.sync
    def _(sy):
        sy.wait_ge(cx.s_st[0].h, cx.s_st[0].v)
        sy.wait_ge(cx.s_st[1].h, cx.s_st[1].v)


def kernel(**inputs):
    per_core = _prep_core_inputs(inputs)
    cxp = build_program()
    res = run_bass_kernel_spmd(cxp.nc, per_core, core_ids=list(range(NCORES)))
    outs = [res.results[c]["out"] for c in range(NCORES)]
    return np.concatenate(outs, axis=0).reshape(B, L, D)


# version 09
# speedup vs baseline: 1.0442x; 1.0441x over previous
import contextlib
import numpy as np
import ml_dtypes
import concourse.bass as bass
import concourse.mybir as mybir
from concourse.bass_utils import run_bass_kernel_spmd

F32 = mybir.dt.float32
BF16 = mybir.dt.bfloat16
AF = mybir.ActivationFunctionType
ALU = mybir.AluOpType

NCORES = 8
B, L, D = 4, 4096, 2048
T = 2048
NTT = T // 128
HC = D // 128
NE, I = 16, 1792
NIM = I // 128
NHB = D // 512
HTOK = T // 2
NTM = HTOK // 128
DIN = 4096
DXB = 1024
CONVD = 6144
NFC = CONVD // 128
FIN = 10304
NFB = 81
NH = 64
CH, CL = 8, 256
NG = 32
EPS = 1e-5

STAGE = 3
SIM_SAFE = False
RGROUPS = [[0, 1], [2, 3], [4, 5], [6, 7]]



def _prep_core_inputs(inputs):
    f32 = np.float32
    gate = np.asarray(inputs["L0_ffn_gate_proj"], f32)
    up = np.asarray(inputs["L0_ffn_up_proj"], f32)
    down = np.asarray(inputs["L0_ffn_down_proj"], f32)
    W_in = np.asarray(inputs["L0_attn_W_in"], f32)
    W_out = np.asarray(inputs["L0_attn_W_out"], f32)
    router = np.asarray(inputs["L0_ffn_router_w"], f32)
    ebias = np.asarray(inputs["L0_ffn_expert_bias"], f32)
    conv_w = np.asarray(inputs["L0_attn_conv_w"], f32)
    conv_b = np.asarray(inputs["L0_attn_conv_b"], f32)
    dt_bias = np.asarray(inputs["L0_attn_dt_bias"], f32)
    A_log = np.asarray(inputs["L0_attn_A_log"], f32)
    Dv = np.asarray(inputs["L0_attn_D"], f32)
    nw = np.asarray(inputs["L0_attn_norm_weight"], f32)
    u = np.asarray(inputs["L0_attn_u"], f32).reshape(B * L, D)

    sh = {}
    sh["gate_blk"] = np.ascontiguousarray(
        gate.reshape(NE, HC, 128, NIM, 128).transpose(0, 3, 2, 1, 4).reshape(NE * NIM, 128, HC * 128))
    sh["up_blk"] = np.ascontiguousarray(
        up.reshape(NE, HC, 128, NIM, 128).transpose(0, 3, 2, 1, 4).reshape(NE * NIM, 128, HC * 128))
    sh["down_nat"] = np.ascontiguousarray(down.reshape(NE * NIM, 128, D))
    WinT = np.zeros((D, NFB * 128), f32)
    WinT[:, :FIN] = W_in.T
    sh["win_blk"] = np.ascontiguousarray(WinT.reshape(HC, 128, NFB, 128).transpose(2, 1, 0, 3))
    sh["wout_nat"] = np.ascontiguousarray(W_out.T.reshape(NG, 128, D))
    sh["router_t"] = np.ascontiguousarray(router.T.reshape(HC, 128, NE).transpose(1, 0, 2))
    sh["bias_row"] = np.ascontiguousarray(ebias.reshape(1, NE))
    sh["convw_t"] = np.ascontiguousarray(conv_w.reshape(NFC, 128, 4).transpose(1, 0, 2))
    sh["convb_t"] = np.ascontiguousarray(conv_b.reshape(NFC, 128).T)
    sh["dtb_col"] = np.ascontiguousarray(dt_bias.reshape(NH, 1))
    sh["alog_col"] = np.ascontiguousarray(A_log.reshape(NH, 1))
    sh["d_rep"] = np.ascontiguousarray(np.broadcast_to(Dv[None, :], (128, NH)).copy())
    sh["norm_t"] = np.ascontiguousarray(nw.reshape(NG, 128).T)
    sh["ident32"] = np.eye(128, dtype=f32)
    sh["identb"] = np.eye(128, dtype=f32).astype(ml_dtypes.bfloat16)
    sh["ones_row"] = np.ones((1, 128), f32)
    sh["ones_col"] = np.ones((128, 1), f32)
    sh["eps_col"] = np.full((128, 1), EPS, f32)
    tidx = np.arange(CL)[None, None, :]
    sidx = np.arange(128)[:, None, None] + 128 * np.arange(2)[None, :, None]
    sh["negbig"] = np.where(tidx >= sidx, 0.0, -1e9).astype(f32)

    per_core = []
    for c in range(NCORES):
        d = dict(sh)
        d["u_slice"] = np.ascontiguousarray(u[c * T:(c + 1) * T, :])
        half = c % 2
        d["fsend"] = np.full((128, 1), 1.0 if half == 0 else 0.0, f32)
        d["frecv"] = np.full((128, 1), 1.0 if half == 1 else 0.0, f32)
        per_core.append(d)
    return per_core



class Sem:
    __slots__ = ("h", "v", "name")

    def __init__(self, es, nc, name):
        self.h = es.enter_context(nc.semaphore(name))
        self.v = 0
        self.name = name

    def inc(self, ins, n=1):
        ins.then_inc(self.h, n)
        self.v += n
        return self.v


class Arena:
    def __init__(self, ap_u8, nbytes):
        self.ap = ap_u8
        self.nbytes = nbytes
        self.off = 0
        self.marks = []

    def push(self):
        self.marks.append(self.off)

    def pop(self):
        self.off = self.marks.pop()

    def _view(self, off, shape, dtype):
        esz = 2 if dtype == BF16 else 4
        n = 1
        for s in shape[1:]:
            n *= s
        nb = n * esz
        assert off + nb <= self.nbytes, f"arena overflow: {off}+{nb}>{self.nbytes}"
        v = self.ap[:, off:off + nb].bitcast(dtype)
        if len(shape) > 2:
            dims = shape[1:]
            pat_in = "p (" + " ".join(f"d{i}" for i in range(len(dims))) + ")"
            pat_out = "p " + " ".join(f"d{i}" for i in range(len(dims)))
            v = v.rearrange(pat_in + " -> " + pat_out, **{f"d{i}": dims[i] for i in range(len(dims))})
        if shape[0] != 128:
            v = v[0:shape[0]]
        return v, off + nb

    def alloc(self, shape, dtype):
        off = (self.off + 3) & ~3
        v, end = self._view(off, shape, dtype)
        self.off = end
        self.last_off = off
        return v

    def at(self, off_bytes, shape, dtype):
        v, _ = self._view(off_bytes, shape, dtype)
        return v


class Ctx:
    pass


def ssync(eng, sem, ins):
    if SIM_SAFE:
        eng.wait_ge(sem.h, sem.inc(ins, 1))
    return ins


def fence(cx, engines=("sync", "tensor", "vector", "scalar", "gpsimd")):
    b = cx.block
    sems = ([cx.s_pe, cx.s_ev, cx.s_sc, cx.s_ldc, cx.s_wd, cx.s_cc] + cx.s_st + cx.s_ld + cx.s_ld2 +
            cx.s_wg + cx.s_wu + cx.s_gx + getattr(cx, 's_ldx', []) + getattr(cx, 's_ldw', []) +
            getattr(cx, 's_stc', []) + getattr(cx, 's_ldcg', []) + getattr(cx, 's_lda', []) +
            getattr(cx, 's_cin', []) + getattr(cx, 's_wcv', []) + [x for x in [getattr(cx, 's_dt', None), getattr(cx, 's_po', None)] if x])
    vals = [(s.h, s.v) for s in sems if s.v > 0]

    for name in engines:
        @getattr(b, name)
        def _(e, vals=vals):
            for h, v in vals:
                e.wait_ge(h, v)



def build_program(stage=None):
    if stage is None:
        stage = STAGE
    nc = bass.Bass("TRN2", target_bir_lowering=False)
    cx = Ctx()
    cx.nc = nc
    cx.stage = stage

    def din(name, shape, dtype=F32):
        return nc.dram_tensor(name, shape, dtype, kind="ExternalInput").ap()

    cx.u = din("u_slice", [T, D])
    cx.gate_blkd = din("gate_blk", [NE * NIM, 128, HC * 128])
    cx.up_blkd = din("up_blk", [NE * NIM, 128, HC * 128])
    cx.down_natd = din("down_nat", [NE * NIM, 128, D])
    cx.win_blkd = din("win_blk", [NFB, 128, HC, 128])
    cx.wout_natd = din("wout_nat", [NG, 128, D])
    cx.router_td = din("router_t", [128, HC, NE])
    cx.bias_rowd = din("bias_row", [1, NE])
    cx.convw_td = din("convw_t", [128, NFC, 4])
    cx.convb_td = din("convb_t", [128, NFC])
    cx.dtb_cold = din("dtb_col", [NH, 1])
    cx.alog_cold = din("alog_col", [NH, 1])
    cx.d_repd = din("d_rep", [128, NH])
    cx.norm_td = din("norm_t", [128, NG])
    cx.ident32d = din("ident32", [128, 128])
    cx.identbd = din("identb", [128, 128], BF16)
    cx.ones_rowd = din("ones_row", [1, 128])
    cx.ones_cold = din("ones_col", [128, 1])
    cx.eps_cold = din("eps_col", [128, 1])
    cx.negbigd = din("negbig", [128, 2, CL])
    cx.fsendd = din("fsend", [128, 1])
    cx.frecvd = din("frecv", [128, 1])

    cx.out = nc.dram_tensor("out", [T, D], F32, kind="ExternalOutput").ap()

    def dsc(name, shape, dtype):
        return nc.dram_tensor(name, shape, dtype).ap()

    cx.xln1 = dsc("xln1", [T, D], BF16)
    cx.x1 = dsc("x1", [T, D], F32)
    cx.xln2 = dsc("xln2", [T, D], BF16)
    cx.x2 = dsc("x2", [T, D], F32)
    cx.xbc_raw = dsc("xbc_raw", [NFC, 128, T], BF16)
    cx.z_raw = dsc("z_raw", [NG, 128, T], BF16)
    cx.xbc_conv = dsc("xbc_conv", [NFC, 128, T], BF16)
    cx.acs_dram = dsc("acs_d", [NH, T], F32)
    cx.xbt_dram = dsc("xbt_dram2", [16, 128, 2048], BF16)
    cx.expacs_dram = dsc("expacs_d", [NH, T], BF16)
    cx.dall_dram = dsc("dall_d", [1, NH * CH], F32)
    cx.wout_bfd = dsc("wout_bf", [NG, 128, D], BF16)
    cx.ex1_in = dsc("ex1_in", [128, NFC * 3], F32)
    cx.ex1_out = dsc("ex1_out", [128, NFC * 3], F32)
    cx.ex2_in = dsc("ex2_in", [128, DIN], F32)
    cx.ex2_out = dsc("ex2_out", [128, DIN], F32)

    es = contextlib.ExitStack()
    cx.es = es
    es.__enter__()

    sbt = lambda name, shape, dt=F32: es.enter_context(nc.sbuf_tensor(name, shape, dt))
    cx.ident32 = sbt("c_id32", [128, 128])
    cx.identb = sbt("c_idb", [128, 128], BF16)
    cx.ones_row = sbt("c_ones_r", [1, 128])
    cx.ones_col = sbt("c_ones_c", [128, 1])
    cx.eps_col = sbt("c_eps_c", [128, 1])
    cx.router_t = sbt("c_router", [128, HC, NE])
    cx.biasr = sbt("c_biasr", [1, NE])
    cx.alpha = sbt("alpha", [128, NTT, NE])
    cx.dtbc = sbt("c_dtbc", [NH, 1])
    cx.alogc = sbt("c_alogc", [NH, 1])
    cx.d_rep = sbt("c_drep", [128, NH])
    cx.norm_sb = sbt("c_norm", [128, NG])
    cx.flags = sbt("c_flags", [128, 2])
    cx.statw = sbt("statw", [128, 8])
    cx.acol = sbt("acolA", [NH, 1])

    AR_BYTES = 206848
    cx.arena_t = es.enter_context(nc.sbuf_tensor("arena", [128, AR_BYTES], mybir.dt.uint8))
    cx.ar = Arena(cx.arena_t.ap(), AR_BYTES)

    cx.psum_t = es.enter_context(nc.psum_tensor("psum", [128, 4096], F32))
    cx.psum = cx.psum_t.ap()

    S = lambda n: Sem(es, nc, n)
    cx.s_ldc = S("s_ldc")
    cx.s_ld = [S("s_ld0"), S("s_ld1")]
    cx.s_st = [S("s_st0"), S("s_st1")]
    cx.s_ld2 = [S("s_ld2a"), S("s_ld2b")]
    cx.s_wg = [S("s_wg0"), S("s_wg1")]
    cx.s_wu = [S("s_wu0"), S("s_wu1")]
    cx.s_wd = S("s_wd")
    cx.s_gx = [S("s_gx0"), S("s_gx1")]
    cx.s_pe = S("s_pe")
    cx.s_ev = S("s_ev")
    cx.s_sc = S("s_sc")
    cx.s_po = S("s_po")
    cx.s_cc = S("s_cc")
    cx.s_ldx = [S("s_ldx0"), S("s_ldx1")]
    cx.s_ldw = [S("s_ldw0"), S("s_ldw1")]
    cx.s_stc = [S("s_stc0"), S("s_stc1")]
    cx.s_ldcg = [S("s_ldcg0"), S("s_ldcg1"), S("s_ldcg2"), S("s_ldcg3")]
    cx.s_lda = [S("s_lda0"), S("s_lda1")]
    cx.s_dt = S("s_dt")
    cx.s_wcv = [S("s_wcv0"), S("s_wcv1")]
    cx.s_cin = [S("s_cin0"), S("s_cin1")]
    cx.s_abc = [S("s_abc0"), S("s_abc1"), S("s_abc2"), S("s_abc3")]

    cx.block = es.enter_context(nc.Block())

    emit_const_loads(cx)
    emit_ln_router(cx, src=cx.u, xln=cx.xln1)
    emit_ffn(cx, xln=cx.xln1, resid=cx.u, dst=(cx.x1 if stage >= 2 else cx.out))
    if stage >= 2:
        emit_attn(cx)
    if stage == 2:
        bb = cx.block

        @bb.sync
        def _(sy):
            sy.wait_ge(cx.s_st[0].h, cx.s_st[0].v)
            sy.wait_ge(cx.s_st[1].h, cx.s_st[1].v)
            cx.s_st[0].inc(sy.dma_start(out=cx.out, in_=cx.x2), 16)
    if stage >= 3:
        emit_ln_router(cx, src=cx.x2, xln=cx.xln2)
        emit_ffn(cx, xln=cx.xln2, resid=cx.x2, dst=cx.out)
    emit_final_wait(cx)
    return cx



def preload_nlexp(cx, eng):
    from concourse.hw_specs import get_activation_tables
    import concourse.mybir as _mb
    for idx, (name, funcs) in enumerate(get_activation_tables(cx.nc.m.arch).items()):
        if name == "natural_log_exp_and_others":
            return eng.add_instruction(
                _mb.InstLoadActFuncSet(
                    name=cx.nc.get_next_instruction_name(),
                    act_func_set_id=idx, ins=[], outs=[]))
    raise ValueError


def emit_const_loads(cx):
    b = cx.block
    s = cx.s_ldc

    @b.sync
    def _(sy):
        pairs = [
            (cx.ident32[:, :], cx.ident32d), (cx.identb[:, :], cx.identbd),
            (cx.ones_row[:, :], cx.ones_rowd), (cx.ones_col[:, :], cx.ones_cold),
            (cx.eps_col[:, :], cx.eps_cold),
            (cx.router_t[:, :, :], cx.router_td),
            (cx.biasr[:, :], cx.bias_rowd),
            (cx.dtbc[:, :], cx.dtb_cold),
            (cx.alogc[:, :], cx.alog_cold), (cx.d_rep[:, :], cx.d_repd),
            (cx.norm_sb[:, :], cx.norm_td),
            (cx.flags[:, 0:1], cx.fsendd), (cx.flags[:, 1:2], cx.frecvd),
        ]
        for dst, src in pairs:
            s.inc(sy.dma_start(out=dst, in_=src), 16)
    cx.c_const = s.v



def emit_ln_router(cx, src, xln):
    b, ar, ps = cx.block, cx.ar, cx.psum
    fence(cx)
    ar.push()
    xt = ar.alloc([128, 2, D], F32)
    scr32 = ar.alloc([128, 2, D], F32)
    xlnb = ar.alloc([128, 2, D], BF16)
    xf32 = ar.alloc([128, 2, 128], F32)
    jnkA = ar.alloc([128, D], F32)
    jnkB = ar.alloc([128, D], F32)
    logits = ar.alloc([128, NTT, NE], F32)
    lwork = ar.alloc([128, NTT, NE], F32)
    lwork2 = ar.alloc([128, NTT, NE], F32)
    awork = ar.alloc([128, NTT, 2], F32)

    s_ld, s_st = cx.s_ld, cx.s_st
    s_pe, s_ev, s_sc = cx.s_pe, cx.s_ev, cx.s_sc
    pslot = [ps[:, 2048:2176], ps[:, 2560:2688]]
    plog = ps[:, 3584:3600]

    ld_done = [None] * NTT
    st_done = [None] * NTT
    sc_sq = [None] * NTT
    sc_apply = [None] * NTT
    ev_stats = [None] * NTT
    ev_cast = [None] * NTT
    ev_evac = [[None] * HC for _ in range(NTT)]
    pe_tr = [[None] * HC for _ in range(NTT)]
    pe_mm = [None] * NTT
    lg = [None] * NTT

    M, SQ, NV, RS, NM, RST, NMR = 0, 1, 2, 3, 4, 5, 6

    for tt in range(NTT):
        par = tt % 2

        @b.sync
        def _(sy, tt=tt, par=par):
            if tt >= 2:
                sy.wait_ge(s_sc.h, sc_apply[tt - 2])
            ld_done[tt] = s_ld[par].inc(
                sy.dma_start(out=xt[:, par, :], in_=src[tt * 128:(tt + 1) * 128, :]), 16)

        @b.scalar
        def _(s, tt=tt, par=par):
            s.wait_ge(s_ld[par].h, ld_done[tt])
            if tt >= 2:
                s.wait_ge(s_pe.h, pe_tr[tt - 2][HC - 1])
            sc_sq[tt] = s_sc.inc(
                s.activation(jnkB[:, :], xt[:, par, :], AF.Square,
                             accum_out=cx.statw[:, SQ:SQ + 1]), 1)

        @b.vector
        def _(v, tt=tt, par=par):
            v.wait_ge(s_ld[par].h, ld_done[tt])
            acv = s_ev.inc(v.tensor_scalar(jnkA[:, :], xt[:, par, :], 1.0, None, ALU.mult,
                                           ALU.add, accum_out=cx.statw[:, M:M + 1]), 1)
            v.wait_ge(s_ev.h, acv)
            v.wait_ge(s_sc.h, sc_sq[tt])

            def vstep(ins):
                v.wait_ge(s_ev.h, s_ev.inc(ins, 1))

            vstep(v.tensor_scalar(cx.statw[:, M:M + 1], cx.statw[:, M:M + 1],
                                  1.0 / D, None, ALU.mult))
            vstep(v.tensor_scalar(cx.statw[:, SQ:SQ + 1], cx.statw[:, SQ:SQ + 1],
                                  1.0 / D, None, ALU.mult))
            vstep(v.scalar_tensor_tensor(cx.statw[:, NV:NV + 1], cx.statw[:, M:M + 1],
                                         cx.statw[:, M:M + 1], cx.statw[:, SQ:SQ + 1],
                                         ALU.mult, ALU.subtract))
            vstep(v.tensor_scalar(cx.statw[:, NV:NV + 1], cx.statw[:, NV:NV + 1], -1.0, EPS,
                                  ALU.mult, ALU.add))
            vstep(v.tensor_scalar(cx.statw[:, NM:NM + 1], cx.statw[:, M:M + 1], -1.0, None, ALU.mult))
            ev_stats[tt] = s_ev.inc(v.reciprocal(cx.statw[:, RS:RS + 1], cx.statw[:, NV:NV + 1]), 1)

        @b.scalar
        def _(s, tt=tt, par=par):
            s.wait_ge(s_ev.h, ev_stats[tt])
            sv = s_sc.inc(s.activation(cx.statw[:, RST:RST + 1], cx.statw[:, RS:RS + 1], AF.Sqrt), 1)
            s.wait_ge(s_sc.h, sv)
            sv2 = s_sc.inc(s.activation(cx.statw[:, NMR:NMR + 1], cx.statw[:, NM:NM + 1], AF.Identity,
                                        scale=cx.statw[:, RST:RST + 1]), 1)
            s.wait_ge(s_sc.h, sv2)
            sc_apply[tt] = s_sc.inc(
                s.activation(scr32[:, par, :], xt[:, par, :], AF.Identity,
                             scale=cx.statw[:, RST:RST + 1], bias=cx.statw[:, NMR:NMR + 1]), 1)

        @b.vector
        def _(v, tt=tt, par=par):
            v.wait_ge(s_sc.h, sc_apply[tt])
            if tt >= 2:
                v.wait_ge(s_st[par].h, st_done[tt - 2])
            ev_cast[tt] = s_ev.inc(v.tensor_copy(xlnb[:, par, :], scr32[:, par, :]), 1)

        @b.sync
        def _(sy, tt=tt, par=par):
            sy.wait_ge(s_ev.h, ev_cast[tt])
            st_done[tt] = s_st[par].inc(
                sy.dma_start(out=xln[tt * 128:(tt + 1) * 128, :], in_=xlnb[:, par, :]), 16)

        peb = s_pe.v
        evb = s_ev.v
        for hc in range(HC):
            pe_tr[tt][hc] = peb + hc + 1
            ev_evac[tt][hc] = evb + hc + 1
        pe_mm[tt] = peb + HC + 1
        lg[tt] = evb + HC + 1

        @b.tensor
        def _(t, tt=tt, par=par):
            if tt == 0:
                t.wait_ge(cx.s_ldc.h, cx.c_const)
            t.wait_ge(s_sc.h, sc_apply[tt])
            for hc in range(HC):
                sl = hc % 2
                if hc >= 2:
                    t.wait_ge(s_ev.h, ev_evac[tt][hc - 2])
                elif tt >= 1:
                    t.wait_ge(s_ev.h, ev_evac[tt - 1][HC - 2 + hc])
                t.transpose(pslot[sl], scr32[:, par, hc * 128:(hc + 1) * 128],
                            cx.ident32[:, :]).then_inc(s_pe.h, 1)
                s_pe.v += 1
                if hc >= 1:
                    t.wait_ge(s_ev.h, ev_evac[tt][hc - 1])
                    if hc == 1 and tt >= 1:
                        t.wait_ge(s_ev.h, lg[tt - 1])
                    t.matmul(plog, xf32[:, (hc - 1) % 2, :], cx.router_t[:, hc - 1, :],
                             start=(hc == 1), stop=False)
            t.wait_ge(s_ev.h, ev_evac[tt][HC - 1])
            t.matmul(plog, xf32[:, (HC - 1) % 2, :], cx.router_t[:, HC - 1, :],
                     start=False, stop=False)
            t.matmul(plog, cx.ones_row[:, :], cx.biasr[:, :],
                     start=False, stop=True).then_inc(s_pe.h, 1)
            s_pe.v += 1

        @b.vector
        def _(v, tt=tt, par=par):
            for hc in range(HC):
                sl = hc % 2
                v.wait_ge(s_pe.h, pe_tr[tt][hc])
                v.tensor_copy(xf32[:, sl, :], pslot[sl]).then_inc(s_ev.h, 1)
                s_ev.v += 1
            v.wait_ge(s_pe.h, pe_mm[tt])
            v.tensor_copy(logits[:, tt, :], plog).then_inc(s_ev.h, 1)
            s_ev.v += 1

        assert s_pe.v == pe_mm[tt] and s_ev.v == lg[tt]

    BIG = 1e9
    m_ = awork[:, :, 0:1]
    d_ = awork[:, :, 1:2]
    mbc = awork[:, :, 0:1].broadcast_to([128, NTT, NE])
    dbc = awork[:, :, 1:2].broadcast_to([128, NTT, NE])
    sc_exp = [None]
    ev_pre = [None]

    lg_all = s_ev.v

    @b.vector
    def _(v):
        def vstep(ins):
            v.wait_ge(s_ev.h, s_ev.inc(ins, 1))

        v.wait_ge(s_ev.h, lg_all)
        vstep(v.tensor_copy(lwork[:, :, :], logits[:, :, :]))
        vstep(v.tensor_reduce(m_, lwork[:, :, :], mybir.AxisListType.X, ALU.max))
        for r in range(3):
            vstep(v.tensor_tensor(lwork2[:, :, :], lwork[:, :, :], mbc, ALU.is_ge))
            vstep(v.tensor_scalar(lwork2[:, :, :], lwork2[:, :, :], -BIG, None, ALU.mult))
            vstep(v.tensor_tensor(lwork[:, :, :], lwork[:, :, :], lwork2[:, :, :], ALU.add))
            vstep(v.tensor_reduce(m_, lwork[:, :, :], mybir.AxisListType.X, ALU.max))
        vstep(v.tensor_tensor(lwork2[:, :, :], logits[:, :, :], mbc, ALU.is_ge))
        vstep(v.tensor_reduce(d_, logits[:, :, :], mybir.AxisListType.X, ALU.max))
        ev_pre[0] = s_ev.inc(
            v.tensor_tensor(lwork[:, :, :], logits[:, :, :], dbc, ALU.subtract), 1)

    @b.scalar
    def _(s):
        s.wait_ge(s_ev.h, ev_pre[0])
        sc_exp[0] = s_sc.inc(s.activation(lwork[:, :, :], lwork[:, :, :], AF.Exp), 1)

    @b.vector
    def _(v):
        def vstep(ins):
            v.wait_ge(s_ev.h, s_ev.inc(ins, 1))

        v.wait_ge(s_sc.h, sc_exp[0])
        vstep(v.tensor_tensor(lwork[:, :, :], lwork[:, :, :], lwork2[:, :, :], ALU.mult))
        vstep(v.tensor_reduce(d_, lwork[:, :, :], mybir.AxisListType.X, ALU.add))
        vstep(v.reciprocal(d_, d_))
        vstep(v.tensor_tensor(cx.alpha[:, :, :], lwork[:, :, :], dbc, ALU.mult))
        cx.alpha_done = s_ev.inc(
            v.tensor_scalar(cx.alpha[:, :, :], cx.alpha[:, :, :], 0.5, None, ALU.mult), 1)

    ar.pop()



def emit_ffn(cx, xln, resid, dst):
    b, ar, ps = cx.block, cx.ar, cx.psum
    fence(cx)
    ar.push()
    xlf = ar.alloc([128, HC, HTOK], BF16)
    acc = ar.alloc([128, NTM, D], F32)
    act = ar.alloc([128, NIM, HTOK], BF16)
    act_off0 = ar.last_off
    wg = ar.alloc([128, 2, HC, 128], BF16)
    wu = ar.alloc([128, 2, HC, 128], BF16)
    wd = ar.alloc([128, NIM, D], BF16)
    wd_off = ar.last_off
    xtile = ar.at(act_off0, [128, 2, D], BF16)
    rt = ar.at(wd_off, [128, 2, D], F32)
    ot = ar.at(wd_off + 16384, [128, 2, D], F32)

    s_ld, s_st = cx.s_ld, cx.s_st
    s_pe, s_ev, s_sc = cx.s_pe, cx.s_ev, cx.s_sc
    pall = ps.rearrange("p (a b) -> p a b", a=2)
    ptrb = [ps[:, 0:1024].bitcast(BF16).rearrange("p (a b) -> p a b", a=HC),
            ps[:, 1024:2048].bitcast(BF16).rearrange("p (a b) -> p a b", a=HC)]

    for hf in range(2):
        t0 = hf * HTOK
        ld_done = [None] * NTM
        tr_done = [None] * NTM
        evx = [None] * NTM
        pe_frontier = s_pe.v
        ev_frontier = s_ev.v
        for tm in range(NTM):
            par = tm % 2

            @b.sync
            def _(sy, tm=tm, par=par):
                if tm >= 2:
                    sy.wait_ge(s_pe.h, tr_done[tm - 2])
                else:
                    sy.wait_ge(s_pe.h, pe_frontier)
                ld_done[tm] = s_ld[par].inc(
                    sy.dma_start(out=xtile[:, par, :],
                                 in_=xln[t0 + tm * 128:t0 + (tm + 1) * 128, :]), 16)

            @b.tensor
            def _(t, tm=tm, par=par):
                t.wait_ge(s_ld[par].h, ld_done[tm])
                if tm >= 2:
                    t.wait_ge(s_ev.h, evx[tm - 2])
                else:
                    t.wait_ge(s_ev.h, ev_frontier)
                for hc in range(HC):
                    ins = t.transpose(ptrb[par][:, hc, :],
                                      xtile[:, par, hc * 128:(hc + 1) * 128],
                                      cx.identb[:, :])
                tr_done[tm] = s_pe.inc(ins, 1)

            @b.vector
            def _(v, tm=tm, par=par):
                v.wait_ge(s_pe.h, tr_done[tm])
                evx[tm] = s_ev.inc(
                    v.tensor_copy(xlf[:, :, tm * 128:(tm + 1) * 128], ptrb[par][:, :, :]), 1)

        xlf_ready = s_ev.v

        pe_base = s_pe.v
        ev_base = s_ev.v
        sc_base = s_sc.v
        pe_gu = [None] * (NE * NIM)
        pe_dn = [[None] * NTM for _ in range(NE)]
        ev_act = [None] * (NE * NIM)
        ev_stt = [[None] * NTM for _ in range(NE)]
        sc_silu = [None] * (NE * NIM)
        kk = 0
        for e in range(NE):
            for im in range(NIM):
                pe_gu[e * NIM + im] = pe_base + kk + 1
                ev_act[e * NIM + im] = ev_base + kk + 1
                kk += 1
            for tm in range(NTM):
                pe_dn[e][tm] = pe_base + kk + 1
                ev_stt[e][tm] = ev_base + kk + 1
                kk += 1
        for k in range(NE * NIM):
            sc_silu[k] = sc_base + k + 1

        w_done = [None] * (NE * NIM)
        wd_done = [None] * NE

        pe_entry = pe_base

        @b.gpsimd
        def _(g):
            for e in range(NE):
                for im in range(NIM):
                    k = e * NIM + im
                    par = k % 2
                    if k >= 2:
                        g.wait_ge(s_pe.h, pe_gu[k - 2])
                    else:
                        g.wait_ge(s_pe.h, pe_entry)
                    w_done[k] = cx.s_wg[par].inc(
                        g.dma_start(out=wg[:, par, :, :], in_=cx.gate_blkd[e * NIM + im]), 16)
                    cx.s_wu[par].inc(
                        g.dma_start(out=wu[:, par, :, :], in_=cx.up_blkd[e * NIM + im]), 16)
                if e >= 1:
                    g.wait_ge(s_pe.h, pe_dn[e - 1][NTM - 1])
                else:
                    g.wait_ge(s_st[0].h, s_st[0].v)
                    g.wait_ge(s_st[1].h, s_st[1].v)
                for ic in range(NIM):
                    wd_done[e] = cx.s_wd.inc(
                        g.dma_start(out=wd[:, ic, :], in_=cx.down_natd[e * NIM + ic]), 16)

        @b.tensor
        def _(t):
            t.wait_ge(s_ev.h, xlf_ready)
            for e in range(NE):
                for im in range(NIM):
                    k = e * NIM + im
                    par = k % 2
                    t.wait_ge(cx.s_wg[par].h, w_done[k])
                    t.wait_ge(cx.s_wu[par].h, w_done[k])
                    if k >= 2:
                        t.wait_ge(s_ev.h, ev_act[k - 2])
                    if im < 2 and e >= 1:
                        t.wait_ge(s_ev.h, ev_stt[e - 1][NTM - 2 + par])
                    for hc in range(HC):
                        t.matmul(pall[:, par, 0:512], wg[:, par, hc, :], xlf[:, hc, 0:512],
                                 start=(hc == 0), stop=(hc == HC - 1))
                        t.matmul(pall[:, par, 512:1024], wg[:, par, hc, :], xlf[:, hc, 512:1024],
                                 start=(hc == 0), stop=(hc == HC - 1))
                    for hc in range(HC):
                        t.matmul(pall[:, par, 1024:1536], wu[:, par, hc, :], xlf[:, hc, 0:512],
                                 start=(hc == 0), stop=(hc == HC - 1))
                        ins = t.matmul(pall[:, par, 1536:2048], wu[:, par, hc, :],
                                       xlf[:, hc, 512:1024],
                                       start=(hc == 0), stop=(hc == HC - 1))
                    ins.then_inc(s_pe.h, 1)
                    s_pe.v += 1
                t.wait_ge(cx.s_wd.h, wd_done[e])
                t.wait_ge(s_ev.h, ev_act[e * NIM + NIM - 1])
                for tm in range(NTM):
                    par = tm % 2
                    if tm >= 2:
                        t.wait_ge(s_ev.h, ev_stt[e][tm - 2])
                    elif e >= 1:
                        t.wait_ge(s_ev.h, ev_stt[e - 1][NTM - 2 + tm])
                    for ic in range(NIM):
                        for hb in range(NHB):
                            ins = t.matmul(pall[:, par, hb * 512:(hb + 1) * 512],
                                           act[:, ic, tm * 128:(tm + 1) * 128],
                                           wd[:, ic, hb * 512:(hb + 1) * 512],
                                           start=(ic == 0), stop=(ic == NIM - 1))
                    ins.then_inc(s_pe.h, 1)
                    s_pe.v += 1

        @b.scalar
        def _(s):
            for e in range(NE):
                for im in range(NIM):
                    k = e * NIM + im
                    par = k % 2
                    s.wait_ge(s_pe.h, pe_gu[k])
                    s.activation(act[:, im, :], pall[:, par, 0:1024], AF.Silu).then_inc(s_sc.h, 1)
                    s_sc.v += 1

        @b.vector
        def _(v):
            v.wait_ge(s_ev.h, cx.alpha_done)
            for e in range(NE):
                for im in range(NIM):
                    k = e * NIM + im
                    par = k % 2
                    v.wait_ge(s_sc.h, sc_silu[k])
                    v.tensor_tensor(act[:, im, :], act[:, im, :],
                                    pall[:, par, 1024:2048], ALU.mult).then_inc(s_ev.h, 1)
                    s_ev.v += 1
                for tm in range(NTM):
                    par = tm % 2
                    v.wait_ge(s_pe.h, pe_dn[e][tm])
                    aco = cx.alpha[:, hf * NTM + tm, e:e + 1]
                    for hb in range(NHB):
                        dstv = acc[:, tm, hb * 512:(hb + 1) * 512]
                        srcv = pall[:, par, hb * 512:(hb + 1) * 512]
                        if e == 0:
                            ins = v.tensor_scalar(dstv, srcv, aco, None, ALU.mult)
                        else:
                            ins = v.scalar_tensor_tensor(dstv, srcv, aco, dstv, ALU.mult, ALU.add)
                    ins.then_inc(s_ev.h, 1)
                    s_ev.v += 1

        assert s_pe.v == pe_base + NE * (NIM + NTM)
        assert s_ev.v == ev_base + NE * (NIM + NTM)
        assert s_sc.v == sc_base + NE * NIM

        rld = [None] * NTM
        evo = [None] * NTM
        sto = [None] * NTM
        pe_all_done = s_pe.v
        for tm in range(NTM):
            par = tm % 2

            @b.sync
            def _(sy, tm=tm, par=par):
                if tm >= 2:
                    sy.wait_ge(s_ev.h, evo[tm - 2])
                else:
                    sy.wait_ge(s_pe.h, pe_all_done)
                rld[tm] = s_ld[par].inc(
                    sy.dma_start(out=rt[:, par, :],
                                 in_=resid[t0 + tm * 128:t0 + (tm + 1) * 128, :]), 16)

            @b.vector
            def _(v, tm=tm, par=par):
                v.wait_ge(s_ld[par].h, rld[tm])
                v.wait_ge(s_ev.h, ev_stt[NE - 1][tm])
                if tm >= 2:
                    v.wait_ge(s_st[par].h, sto[tm - 2])
                evo[tm] = s_ev.inc(
                    v.tensor_tensor(ot[:, par, :], acc[:, tm, :], rt[:, par, :], ALU.add), 1)

            @b.sync
            def _(sy, tm=tm, par=par):
                sy.wait_ge(s_ev.h, evo[tm])
                sto[tm] = s_st[par].inc(
                    sy.dma_start(out=dst[t0 + tm * 128:t0 + (tm + 1) * 128, :],
                                 in_=ot[:, par, :]), 16)
    ar.pop()



def emit_attn(cx):
    b, ar, ps = cx.block, cx.ar, cx.psum
    fence(cx)
    ar.push()
    cx.negbig = ar.alloc([128, 2, CL], F32)
    cx.convw = ar.alloc([128, NFC, 4], F32)
    cx.convb = ar.alloc([128, NFC], F32)
    cx.dbcc = ar.alloc([128, CH, NH], F32)
    cx.ddTf = ar.alloc([128, 16, NH], F32)
    cx.biasTf = ar.alloc([128, 16, NH], F32)

    s_ld, s_st, s_ld2 = cx.s_ld, cx.s_st, cx.s_ld2
    s_pe, s_ev, s_sc, s_cc = cx.s_pe, cx.s_ev, cx.s_sc, cx.s_cc
    s_gx, s_ldx, s_ldw, s_stc, s_ldcg = cx.s_gx, cx.s_ldx, cx.s_ldw, cx.s_stc, cx.s_ldcg

    @b.sync
    def _(sy):
        for dst, src in [(cx.negbig[:, :, :], cx.negbigd), (cx.convw[:, :, :], cx.convw_td),
                         (cx.convb[:, :], cx.convb_td)]:
            cx.s_ldc.inc(sy.dma_start(out=dst, in_=src), 16)
    cx.attn_const = cx.s_ldc.v

    ar.push()
    dtA = ar.alloc([NH, T], F32)
    dtB = ar.alloc([NH, T], F32)
    dtC = ar.alloc([NH, T], F32)
    dtD = ar.alloc([NH, T], F32)
    dcol = ar.alloc([NH, CH], F32)
    drow = ar.alloc([1, NH * CH], F32)
    uaf = ar.alloc([128, HC, T], BF16)
    uaf_off = ar.last_off
    xt = ar.alloc([128, 2, D], F32)
    xlnb = ar.alloc([128, 2, D], BF16)
    xlnb_off = ar.last_off
    wbuf = ar.alloc([128, 2, HC, 128], BF16)
    bounce = ar.alloc([128, 2, D], BF16)
    bounce_off = ar.last_off
    ex1sb = ar.alloc([128, NFC, 3], F32)
    tailr = ar.alloc([128, NFC * 3], F32)
    cin = ar.alloc([128, 2, T + 3], BF16)
    cv = ar.alloc([128, 2, T], F32)
    cv_off = ar.last_off
    cout = ar.alloc([128, 2, T], BF16)
    cout_off = ar.last_off
    wcv = ar.alloc([128, 2, D], BF16)
    jnkB = ar.at(cout_off, [128, D], F32)
    jnkA = ar.at(cv_off, [128, D], F32)
    jnkdt = ar.at(xlnb_off, [128, D], F32)

    M, SQ, NV, RS, NM, RST, NMR = 0, 1, 2, 3, 4, 5, 6
    pslotb = [ps[:, 0:1024].bitcast(BF16).rearrange("p (a b) -> p a b", a=HC),
              ps[:, 1024:2048].bitcast(BF16).rearrange("p (a b) -> p a b", a=HC)]

    ld_done = [None] * NTT
    sc_apply = [None] * NTT
    tr_done = [None] * NTT
    evx = [None] * NTT
    ev_entry0 = s_ev.v
    for tt in range(NTT):
        par = tt % 2

        @b.sync
        def _(sy, tt=tt, par=par):
            if tt >= 2:
                sy.wait_ge(s_sc.h, sc_apply[tt - 2])
            ld_done[tt] = s_ld[par].inc(
                sy.dma_start(out=xt[:, par, :], in_=cx.x1[tt * 128:(tt + 1) * 128, :]), 16)

        @b.scalar
        def _(s, tt=tt, par=par):
            s.wait_ge(s_ld[par].h, ld_done[tt])
            if tt >= 2:
                s.wait_ge(s_pe.h, tr_done[tt - 2])
            s.activation(jnkB[:, :], xt[:, par, :], AF.Square,
                         accum_out=cx.statw[:, SQ:SQ + 1]).then_inc(s_sc.h, 1)
            s_sc.v += 1
        sc_sq = s_sc.v

        @b.vector
        def _(v, tt=tt, par=par, sc_sq=sc_sq):
            def vstep(ins):
                v.wait_ge(s_ev.h, s_ev.inc(ins, 1))

            v.wait_ge(s_ld[par].h, ld_done[tt])
            vstep(v.tensor_scalar(jnkA[:, :], xt[:, par, :], 1.0, None, ALU.mult,
                                  ALU.add, accum_out=cx.statw[:, M:M + 1]))
            v.wait_ge(s_sc.h, sc_sq)
            vstep(v.tensor_scalar(cx.statw[:, M:M + 1], cx.statw[:, M:M + 1],
                                  1.0 / D, None, ALU.mult))
            vstep(v.tensor_scalar(cx.statw[:, SQ:SQ + 1], cx.statw[:, SQ:SQ + 1],
                                  1.0 / D, None, ALU.mult))
            vstep(v.scalar_tensor_tensor(cx.statw[:, NV:NV + 1], cx.statw[:, M:M + 1],
                                         cx.statw[:, M:M + 1], cx.statw[:, SQ:SQ + 1],
                                         ALU.mult, ALU.subtract))
            vstep(v.tensor_scalar(cx.statw[:, NV:NV + 1], cx.statw[:, NV:NV + 1], -1.0, EPS,
                                  ALU.mult, ALU.add))
            vstep(v.tensor_scalar(cx.statw[:, NM:NM + 1], cx.statw[:, M:M + 1], -1.0, None,
                                  ALU.mult))
            s_ev.inc(v.reciprocal(cx.statw[:, RS:RS + 1], cx.statw[:, NV:NV + 1]), 1)
        ev_stats = s_ev.v

        @b.scalar
        def _(s, tt=tt, par=par, ev_stats=ev_stats):
            s.wait_ge(s_ev.h, ev_stats)
            s.wait_ge(s_sc.h, s_sc.inc(
                s.activation(cx.statw[:, RST:RST + 1], cx.statw[:, RS:RS + 1], AF.Sqrt), 1))
            s.wait_ge(s_sc.h, s_sc.inc(
                s.activation(cx.statw[:, NMR:NMR + 1], cx.statw[:, NM:NM + 1], AF.Identity,
                             scale=cx.statw[:, RST:RST + 1]), 1))
            sc_apply[tt] = s_sc.inc(
                s.activation(xlnb[:, par, :], xt[:, par, :], AF.Identity,
                             scale=cx.statw[:, RST:RST + 1], bias=cx.statw[:, NMR:NMR + 1]), 1)

        @b.tensor
        def _(t, tt=tt, par=par):
            t.wait_ge(s_sc.h, sc_apply[tt])
            if tt >= 2:
                t.wait_ge(s_ev.h, evx[tt - 2])
            else:
                t.wait_ge(s_ev.h, ev_entry0)
            for hc in range(HC):
                ins = t.transpose(pslotb[par][:, hc, :], xlnb[:, par, hc * 128:(hc + 1) * 128],
                                  cx.identb[:, :])
            tr_done[tt] = s_pe.inc(ins, 1)

        @b.vector
        def _(v, tt=tt, par=par):
            v.wait_ge(s_pe.h, tr_done[tt])
            evx[tt] = s_ev.inc(
                v.tensor_copy(uaf[:, :, tt * 128:(tt + 1) * 128], pslotb[par][:, :, :]), 1)

    uaf_ready = s_ev.v

    wcv_ld = [None] * NG
    wcv_st = [None] * NG
    for i in range(NG):
        par = i % 2
        wcv_st[i] = cx.s_wcv[par].v + 16 * (i // 2 + 1)

    @b.gpsimd
    def _(g):
        for i in range(NG):
            par = i % 2
            if i >= 2:
                g.wait_ge(cx.s_wcv[par].h, wcv_st[i - 2])
            else:
                g.wait_ge(s_ev.h, ev_entry0)
            wcv_ld[i] = s_ldw[par].inc(
                g.dma_start(out=wcv[:, par, :], in_=cx.wout_natd[i]), 16)

    @b.sync
    def _(sy):
        for i in range(NG):
            par = i % 2
            sy.wait_ge(s_ldw[par].h, wcv_ld[i])
            val = cx.s_wcv[par].inc(sy.dma_start(out=cx.wout_bfd[i], in_=wcv[:, par, :]), 16)
            assert val == wcv_st[i]
    wout_conv_done = [cx.s_wcv[0].v, cx.s_wcv[1].v]

    fb_order = list(range(32, NFB)) + list(range(0, 32))
    pall = ps.rearrange("p (a b) -> p a b", a=2)
    w_ld = {}
    pe_fb = {}
    ev_fb = {}
    st_fb = {}
    pe_base = s_pe.v
    ev_base = s_ev.v
    for j, fb in enumerate(fb_order):
        pe_fb[fb] = pe_base + j + 1
        ev_fb[fb] = ev_base + j + 1
    stv = [cx.s_st[0].v, cx.s_st[1].v]
    for j, fb in enumerate(fb_order):
        par = j % 2
        if fb != 80:
            stv[par] += 16
            st_fb[fb] = stv[par]
    evv = ev_base
    for j, fb in enumerate(fb_order):
        if 32 <= fb < 80:
            evv += 2
        elif fb == 80:
            evv += 1
        ev_fb[fb] = evv
    sc_fb = {}
    scv = s_sc.v
    for j, fb in enumerate(fb_order):
        if fb < 32:
            scv += 1
            sc_fb[fb] = scv

    @b.gpsimd
    def _(g):
        for j, fb in enumerate(fb_order):
            par = j % 2
            if j >= 2:
                g.wait_ge(s_pe.h, pe_fb[fb_order[j - 2]])
            else:
                g.wait_ge(s_ev.h, uaf_ready)
            w_ld[fb] = s_gx[par].inc(
                g.dma_start(out=wbuf[:, par, :, :], in_=cx.win_blkd[fb]), 16)

    @b.tensor
    def _(t):
        t.wait_ge(s_ev.h, uaf_ready)
        for j, fb in enumerate(fb_order):
            par = j % 2
            t.wait_ge(s_gx[par].h, w_ld[fb])
            if j >= 2:
                pfb = fb_order[j - 2]
                if pfb < 32:
                    t.wait_ge(s_sc.h, sc_fb[pfb])
                else:
                    t.wait_ge(s_ev.h, ev_fb[pfb])
            for hc in range(HC):
                for tq in range(4):
                    ins = t.matmul(pall[:, par, tq * 512:(tq + 1) * 512],
                                   wbuf[:, par, hc, :], uaf[:, hc, tq * 512:(tq + 1) * 512],
                                   start=(hc == 0), stop=(hc == HC - 1))
            val = s_pe.inc(ins, 1)
            assert val == pe_fb[fb]

    @b.vector
    def _(v):
        for j, fb in enumerate(fb_order):
            if fb < 32:
                continue
            par = j % 2
            v.wait_ge(s_pe.h, pe_fb[fb])
            jj = j - 2
            while jj >= 0 and fb_order[jj] == 80:
                jj -= 2
            if jj >= 0:
                v.wait_ge(s_st[par].h, st_fb[fb_order[jj]])
            if fb == 80:
                v.tensor_copy(dtA[:, :], pall[0:NH, par, :]).then_inc(s_ev.h, 1)
                s_ev.v += 1
            else:
                cpv = s_ev.inc(v.tensor_copy(bounce[:, par, :], pall[:, par, :]), 1)
                if SIM_SAFE:
                    v.wait_ge(s_ev.h, cpv)
                v.tensor_scalar(ex1sb[:, fb - 32, :], bounce[:, par, 2045:2048],
                                cx.flags[:, 0:1], None, ALU.mult).then_inc(s_ev.h, 1)
                s_ev.v += 1
            assert s_ev.v == ev_fb[fb]

    @b.scalar
    def _(sc):
        for j, fb in enumerate(fb_order):
            if fb >= 32:
                continue
            par = j % 2
            sc.wait_ge(s_pe.h, pe_fb[fb])
            jj = j - 2
            while jj >= 0 and fb_order[jj] == 80:
                jj -= 2
            if jj >= 0:
                sc.wait_ge(s_st[par].h, st_fb[fb_order[jj]])
            sc.activation(bounce[:, par, :], pall[:, par, :], AF.Silu).then_inc(s_sc.h, 1)
            s_sc.v += 1
            assert s_sc.v == sc_fb[fb]
    dt_ready = ev_fb[80]

    @b.sync
    def _(sy):
        for j, fb in enumerate(fb_order):
            if fb == 80:
                continue
            par = j % 2
            if 32 <= fb < 80:
                sy.wait_ge(s_ev.h, ev_fb[fb])
                val = s_st[par].inc(sy.dma_start(out=cx.xbc_raw[fb - 32], in_=bounce[:, par, :]), 16)
            else:
                sy.wait_ge(s_sc.h, sc_fb[fb])
                val = s_st[par].inc(sy.dma_start(out=cx.z_raw[fb], in_=bounce[:, par, :]), 16)
            assert val == st_fb[fb]

    xbc_export_ev = ev_fb[79]
    inproj_st_frontier = [cx.s_st[0].v, cx.s_st[1].v]

    @b.gpsimd
    def _(g):
        g.wait_ge(s_ev.h, xbc_export_ev)
        g.dma_start(out=cx.ex1_in, in_=ex1sb.rearrange("p a b -> p (a b)")).then_inc(cx.s_po.h, 16)
        cx.s_po.v += 16
        g.wait_ge(cx.s_po.h, cx.s_po.v)
        g.collective_compute(
            "AllReduce", ALU.add,
            replica_groups=RGROUPS,
            ins=[cx.ex1_in.opt()], outs=[cx.ex1_out.opt()],
        ).then_inc(s_cc.h, 1)
        s_cc.v += 1
        g.wait_ge(s_cc.h, s_cc.v)
        g.dma_start(out=tailr[:, :], in_=cx.ex1_out).then_inc(cx.s_po.h, 16)
        cx.s_po.v += 16
    tail_ld = cx.s_po.v

    tail_ev = [None]

    @b.vector
    def _(v):
        v.wait_ge(cx.s_po.h, tail_ld)
        v.wait_ge(cx.s_ldc.h, cx.attn_const)
        tail_ev[0] = s_ev.inc(
            v.tensor_scalar(tailr[:, :], tailr[:, :], cx.flags[:, 1:2], None, ALU.mult), 1)

    dt_prep2(cx, dtA, dtB, dtC, dtD, dcol, drow, jnkdt, dt_ready)

    tailr3 = tailr.rearrange("p (f k) -> p f k", f=NFC)
    cv_ev = {}
    cv_st = {}
    for fc in range(NFC):
        par = fc % 2

        @b.gpsimd
        def _(gq, fc=fc, par=par):
            if fc >= 2:
                gq.wait_ge(s_ev.h, cv_ev[fc - 2])
                sem = s_st if fc - 2 < 16 else s_stc
                gq.wait_ge(sem[par].h, cv_st[fc - 2])
            gq.wait_ge(s_st[0].h, inproj_st_frontier[0])
            gq.wait_ge(s_st[1].h, inproj_st_frontier[1])
            cx.s_cin[par].inc(gq.dma_start(out=cin[:, par, 3:3 + T], in_=cx.xbc_raw[fc]), 16)
        cin_ld = cx.s_cin[par].v

        @b.vector
        def _(v, fc=fc, par=par, cin_ld=cin_ld):
            v.wait_ge(cx.s_cin[par].h, cin_ld)
            v.wait_ge(s_ev.h, tail_ev[0])
            ssync(v, s_ev, v.tensor_copy(cin[:, par, 0:3], tailr3[:, fc, :]))
            ssync(v, s_ev, v.tensor_scalar(cv[:, par, :], cin[:, par, 0:T], cx.convw[:, fc, 0:1],
                                           cx.convb[:, fc:fc + 1], ALU.mult, ALU.add))
            for k in range(1, 4):
                ins = v.scalar_tensor_tensor(cv[:, par, :], cin[:, par, k:k + T],
                                             cx.convw[:, fc, k:k + 1], cv[:, par, :],
                                             ALU.mult, ALU.add)
                if k < 3:
                    ssync(v, s_ev, ins)
            cv_ev[fc] = s_ev.inc(ins, 1)

        @b.scalar
        def _(s, fc=fc, par=par):
            s.wait_ge(s_ev.h, cv_ev[fc])
            s.activation(cout[:, par, :], cv[:, par, :], AF.Silu).then_inc(s_sc.h, 1)
            s_sc.v += 1
        sil_done = s_sc.v

        @b.sync
        def _(sy, fc=fc, par=par, sil_done=sil_done):
            sy.wait_ge(s_sc.h, sil_done)
            sem = s_st if fc < 16 else s_stc
            cv_st[fc] = sem[par].inc(
                sy.dma_start(out=cx.xbc_conv[fc], in_=cout[:, par, :]), 16)

    xb_conv_st = [s_st[0].v, s_st[1].v]
    conv_all_st = [s_stc[0].v, s_stc[1].v]
    conv_done_ev = s_ev.v


    emit_xbt(cx, uaf_off, bounce_off, xb_conv_st)

    ar.pop()

    ar.push()
    bufs = {}
    bufs["carry"] = ar.alloc([128, DIN], F32)
    bufs["carry_bf"] = ar.alloc([128, DIN], BF16)
    bufs["states"] = ar.alloc([128, CH, DIN], BF16)
    states_off = ar.last_off
    bufs["xbt"] = ar.alloc([128, 1, 2, T], BF16)
    bufs["abc"] = ar.alloc([128, 4, 2, CL], F32)
    bufs["bck"] = ar.alloc([128, 2, 8, CL], BF16)
    bufs["xfm"] = ar.alloc([128, 2, 8, CL], BF16)
    bufs["xdd"] = ar.alloc([128, 4, 2, 64], BF16)
    bufs["segb"] = ar.alloc([128, 2, 2, 2, CL], F32)
    segb_off = ar.last_off
    bufs["Lb"] = ar.alloc([128, 2, 2, 2, CL], BF16)
    bufs["glb"] = ar.alloc([128, 2, 2, 2, CL], BF16)
    bufs["ebcb"] = ar.alloc([128, 2, 2, CL], BF16)
    bufs["csb"] = ar.alloc([128, 2, 2, CL], BF16)
    bufs["cgb"] = ar.alloc([128, 4, CL], BF16)
    bufs["zgb"] = ar.alloc([128, 2, CL], BF16)
    bufs["ygc"] = ar.alloc([128, 2, CL], F32)
    bufs["y2"] = ar.alloc([128, 2, CL], F32)
    bufs["rrow"] = ar.alloc([1, 2, CL], F32)
    bufs["ynormF"] = ar.alloc([128, NG, 2 * CL], BF16)
    ynorm_off = ar.last_off
    bufs["exc"] = ar.at(ynorm_off, [128, DIN], F32)
    bufs["x1t"] = ar.at(states_off, [128, 2, D], F32)
    bufs["woutb"] = ar.at(segb_off, [128, 2, D], BF16)
    bufs["xb_conv_st"] = xb_conv_st
    bufs["conv_all_st"] = conv_all_st
    bufs["wout_conv_done"] = wout_conv_done

    emit_ssd2(cx, bufs)
    ar.pop()
    ar.pop()



def emit_xbt(cx, xbfm_off, stage_off, conv_xb_st):
    b, ar, ps = cx.block, cx.ar, cx.psum
    s_pe, s_ev = cx.s_pe, cx.s_ev
    xbfm = ar.at(xbfm_off, [128, 16, T], BF16)
    stage = ar.at(stage_off, [128, 2, 16, 128], BF16)
    ld = [None] * 16
    trd = {}
    evd = {}
    std = [None] * 16
    ptr = [ps[:, 0:512].bitcast(BF16).rearrange("p (a b) -> p a b", a=8),
           ps[:, 512:1024].bitcast(BF16).rearrange("p (a b) -> p a b", a=8)]

    @b.sync
    def _(sy):
        sy.wait_ge(cx.s_st[0].h, conv_xb_st[0])
        sy.wait_ge(cx.s_st[1].h, conv_xb_st[1])
        for fc in range(16):
            ld[fc] = cx.s_ldx[fc % 2].inc(
                sy.dma_start(out=xbfm[:, fc, :], in_=cx.xbc_conv[fc]), 16)
    ld_all = [cx.s_ldx[0].v, cx.s_ldx[1].v]

    for tc in range(16):
        par = tc % 2
        for half in range(2):
            bk = (2 * tc + half) % 2

            @b.tensor
            def _(t, tc=tc, half=half, bk=bk):
                if tc == 0 and half == 0:
                    t.wait_ge(cx.s_ldx[0].h, ld_all[0])
                    t.wait_ge(cx.s_ldx[1].h, ld_all[1])
                key = (tc, half - 1) if half == 1 else (tc - 1, 1)
                if key in evd:
                    t.wait_ge(s_ev.h, evd[key])
                for i in range(8):
                    fc = half * 8 + i
                    ins = t.transpose(ptr[bk][:, i, :], xbfm[:, fc, tc * 128:(tc + 1) * 128],
                                      cx.identb[:, :])
                trd[(tc, half)] = s_pe.inc(ins, 1)

            @b.vector
            def _(v, tc=tc, half=half, bk=bk, par=par):
                v.wait_ge(s_pe.h, trd[(tc, half)])
                if tc >= 2:
                    v.wait_ge(cx.s_st[par].h, std[tc - 2])
                evd[(tc, half)] = s_ev.inc(
                    v.tensor_copy(stage[:, par, half * 8:(half + 1) * 8, :],
                                  ptr[bk][:, :, :]), 1)

        @b.sync
        def _(sy, tc=tc, par=par):
            sy.wait_ge(s_ev.h, evd[(tc, 1)])
            std[tc] = cx.s_st[par].inc(
                sy.dma_start(out=cx.xbt_dram[tc],
                             in_=stage[:, par, :, :].rearrange("p a b -> p (a b)")), 16)
    cx.xbt_st = [cx.s_st[0].v, cx.s_st[1].v]


def dt_prep2(cx, dtA, dtB, dtC, dtD, dcol, drow, jnk, dt_ready):
    b, ps = cx.block, cx.psum
    s_ev, s_sc, s_pe = cx.s_ev, cx.s_sc, cx.s_pe
    jnk64 = jnk[0:NH, :]
    ev0 = [None] * 8

    @b.vector
    def _(v):
        def vstep(ins):
            v.wait_ge(s_ev.h, s_ev.inc(ins, 1))

        v.wait_ge(s_ev.h, dt_ready)
        vstep(v.tensor_scalar(dtA[:, :], dtA[:, :], cx.dtbc[:, :], None, ALU.add))
        ev0[0] = s_ev.inc(v.tensor_scalar(dtB[:, :], dtA[:, :], 0.0, None, ALU.max), 1)

    @b.scalar
    def _(s):
        s.wait_ge(s_ev.h, ev0[0])
        s.wait_ge(s_sc.h, s_sc.inc(s.activation(jnk64[:, :], dtA[:, :], AF.Abs), 1))
        s.wait_ge(s_sc.h, s_sc.inc(s.activation(jnk64[:, :], jnk64[:, :], AF.Exp, scale=-1.0), 1))
        s.wait_ge(s_sc.h, s_sc.inc(s.activation(jnk64[:, :], jnk64[:, :], AF.Ln, bias=1.0), 1))
        ev0[1] = s_sc.inc(s.activation(cx.acol[:, :], cx.alogc[:, :], AF.Exp), 1)

    @b.vector
    def _(v):
        def vstep(ins):
            v.wait_ge(s_ev.h, s_ev.inc(ins, 1))

        v.wait_ge(s_sc.h, ev0[1])
        vstep(v.tensor_tensor(dtA[:, :], dtB[:, :], jnk64[:, :], ALU.add))
        vstep(v.tensor_scalar(dtA[:, :], dtA[:, :], 1e-30, None, ALU.max))
        ev0[2] = s_ev.inc(v.tensor_scalar(cx.acol[:, :], cx.acol[:, :], -1.0, None, ALU.mult), 1)

    @b.scalar
    def _(s):
        s.wait_ge(s_ev.h, ev0[2])
        ev0[3] = s_sc.inc(s.activation(dtD[:, :], dtA[:, :], AF.Ln), 1)

    @b.vector
    def _(v):
        def vstep(ins):
            v.wait_ge(s_ev.h, s_ev.inc(ins, 1))

        v.wait_ge(s_sc.h, ev0[3])
        vstep(v.tensor_scalar(dtC[:, :], dtA[:, :], cx.acol[:, :], None, ALU.mult))
        src, dst = dtC, jnk64
        sh = 1
        while sh < CL:
            sv = src.rearrange("p (c t) -> p c t", t=CL)
            dv = dst.rearrange("p (c t) -> p c t", t=CL)
            ssync(v, s_ev, v.tensor_copy(dv[:, :, 0:sh], sv[:, :, 0:sh]))
            ssync(v, s_ev, v.tensor_tensor(dv[:, :, sh:CL], sv[:, :, sh:CL],
                                           sv[:, :, 0:CL - sh], ALU.add))
            src, dst = dst, src
            sh *= 2
        assert src is dtC
        ev0[4] = s_ev.inc(v.tensor_tensor(dtD[:, :], dtD[:, :], dtC[:, :], ALU.subtract), 1)

    @b.scalar
    def _(s):
        s.wait_ge(s_ev.h, ev0[4])
        c3 = dtC.rearrange("p (c t) -> p c t", t=CL)
        b3 = dtB.rearrange("p (c t) -> p c t", t=CL)
        for c in range(CH):
            ins = s.activation(b3[:, c, :], c3[:, c, :], AF.Exp, scale=-1.0,
                               bias=c3[:, c, CL - 1:CL])
        s.wait_ge(s_sc.h, s_sc.inc(ins, 1))
        dc3 = dcol.unsqueeze(2)
        ev0[5] = s_sc.inc(s.activation(dc3[:, :, :], c3[:, :, CL - 1:CL], AF.Exp), 1)

    @b.vector
    def _(v):
        v.wait_ge(s_sc.h, ev0[5])
        ev0[6] = s_ev.inc(v.tensor_tensor(dtB[:, :], dtA[:, :], dtB[:, :], ALU.mult), 1)


    ddv = cx.ddTf
    bsv = cx.biasTf
    trv = {}
    evv = {}
    for i, (srcT, dstT, rdy, sname) in enumerate(
            [(dtB, ddv, ev0[6], "dd"), (dtD, bsv, ev0[4], "bs")]):
        for tc in range(16):
            sl = tc % 2

            @b.tensor
            def _(t, i=i, tc=tc, sl=sl, srcT=srcT, rdy=rdy):
                t.wait_ge(s_ev.h, rdy)
                if (i, tc - 2) in evv:
                    t.wait_ge(s_ev.h, evv[(i, tc - 2)])
                elif i == 1 and tc < 2:
                    t.wait_ge(s_ev.h, evv[(0, 14 + tc)])
                poff = 3072 if sl == 0 else 3584
                trv[(i, tc)] = s_pe.inc(
                    t.transpose(ps[:, poff:poff + 64],
                                srcT[:, tc * 128:(tc + 1) * 128], cx.ident32[0:NH, 0:NH]), 1)

            @b.vector
            def _(v, i=i, tc=tc, sl=sl, dstT=dstT):
                v.wait_ge(s_pe.h, trv[(i, tc)])
                poff = 3072 if sl == 0 else 3584
                evv[(i, tc)] = s_ev.inc(
                    v.tensor_copy(dstT[:, tc, :], ps[:, poff:poff + 64]), 1)

    @b.tensor
    def _(t):
        t.wait_ge(s_sc.h, ev0[5])
        t.wait_ge(s_ev.h, evv[(1, 15)])
        ev0[7] = s_pe.inc(t.transpose(ps[:, 3584:3648][0:CH, :], dcol[:, :],
                                      cx.ident32[0:NH, 0:NH]), 1)

    dtr_ev = [None]

    @b.vector
    def _(v):
        v.wait_ge(s_pe.h, ev0[7])
        dtr_ev[0] = s_ev.inc(v.tensor_copy(jnk[0:CH, 0:NH], ps[:, 3584:3648][0:CH, :]), 1)

    dall_st = [None]
    dall_ld = [None]
    acs_st = [None]
    expa_sb = jnk.bitcast(BF16)[0:NH, 0:T]
    expa_ev = [None]

    @b.sync
    def _(sy):
        sy.wait_ge(s_ev.h, dtr_ev[0])
        dall_st[0] = cx.s_dt.inc(sy.dma_start(out=cx.dall_dram, in_=jnk[0:CH, 0:NH]), 16)
        sy.wait_ge(cx.s_dt.h, dall_st[0])
        dall_ld[0] = cx.s_dt.inc(sy.dma_start(out=drow[:, :], in_=cx.dall_dram), 16)
        sy.wait_ge(cx.s_dt.h, dall_ld[0])
        sy.wait_ge(s_ev.h, ev0[4])
        cx.s_dt.inc(sy.dma_start(out=cx.acs_dram, in_=dtC[:, :]), 16)

    dbv = cx.dbcc
    cx._dbc_ev = {}
    for c in range(CH):
        @b.tensor
        def _(t, c=c):
            if c == 0:
                t.wait_ge(cx.s_dt.h, dall_ld[0])
            else:
                t.wait_ge(s_ev.h, cx._dbc_ev[c - 1])
            cx._dbc_pe = s_pe.inc(
                t.matmul(ps[:, 3648:3712], cx.ones_row[:, :],
                         drow[0:1, c * NH:(c + 1) * NH], start=True, stop=True), 1)

        @b.vector
        def _(v, c=c):
            v.wait_ge(s_pe.h, cx._dbc_pe)
            cx._dbc_ev[c] = s_ev.inc(v.tensor_copy(dbv[:, c, :], ps[:, 3648:3712]), 1)
    @b.scalar
    def _(s):
        s.wait_ge(s_ev.h, ev0[4])
        s.wait_ge(cx.s_dt.h, dall_ld[0])
        expa_ev[0] = s_sc.inc(s.activation(expa_sb, dtC[:, :], AF.Exp), 1)

    @b.sync
    def _(sy):
        sy.wait_ge(s_sc.h, expa_ev[0])
        acs_st[0] = cx.s_dt.inc(sy.dma_start(out=cx.expacs_dram, in_=expa_sb), 16)

    cx.dt_done_ev = cx._dbc_ev[CH - 1]
    cx.acs_st = acs_st[0]


def emit_ssd2(cx, bf):
    b, ps = cx.block, cx.psum
    s_pe, s_ev, s_sc, s_cc = cx.s_pe, cx.s_ev, cx.s_sc, cx.s_cc
    s_ld, s_st, s_ld2 = cx.s_ld, cx.s_st, cx.s_ld2
    s_ldx, s_ldw, s_ldcg = cx.s_ldx, cx.s_ldw, cx.s_ldcg

    carry, carry_bf, states = bf["carry"], bf["carry_bf"], bf["states"]
    xbt, bck, xfm = bf["xbt"], bf["bck"], bf["xfm"]
    abc = bf["abc"]
    xdd = bf["xdd"]
    segb, Lb, glb = bf["segb"], bf["Lb"], bf["glb"]
    ebcb, csb, cgb = bf["ebcb"], bf["csb"], bf["cgb"]
    zgb, ygc, y2 = bf["zgb"], bf["ygc"], bf["y2"]
    y2q = ygc
    rrow = bf["rrow"]
    ynormF = bf["ynormF"]
    exc, x1t, woutb = bf["exc"], bf["x1t"], bf["woutb"]
    x2t = x1t

    xbt_ld = [None] * CH
    p1_mm = {}
    p1_xdd = {}
    p1_cu = {}
    st_banks = [ps[:, 0:64], ps[:, 512:576], ps[:, 1024:1088], ps[:, 1536:1600]]

    for c in range(CH):
        cpar = c % 2

        @b.sync
        def _(sy, c=c, cpar=cpar):
            if c >= 1:
                sy.wait_ge(s_pe.h, p1_mm[(c - 1, NH - 1)])
            else:
                sy.wait_ge(s_st[0].h, cx.xbt_st[0])
                sy.wait_ge(s_st[1].h, cx.xbt_st[1])
            xbt_ld[c] = s_ldx[cpar].inc(
                sy.dma_start(out=xbt[:, 0, :, :],
                             in_=cx.xbt_dram[2 * c:2 * c + 2].rearrange("a p f -> p a f")), 16)

        for h in range(NH):
            hp = h % 4
            xc0 = (h // 8) * 128 + (h % 2) * 64

            @b.vector
            def _(v, c=c, h=h, hp=hp, xc0=xc0, cpar=cpar):
                v.wait_ge(s_ldx[cpar].h, xbt_ld[c])
                if (c, h - 4) in p1_mm:
                    v.wait_ge(s_pe.h, p1_mm[(c, h - 4)])
                elif c >= 1 and h < 4:
                    v.wait_ge(s_pe.h, p1_mm[(c - 1, NH - 4 + h)])
                if c == 0 and h < 4:
                    v.wait_ge(s_ev.h, cx.dt_done_ev)
                    v.wait_ge(cx.s_dt.h, cx.acs_st)
                for st in range(2):
                    ins = v.tensor_scalar(xdd[:, hp, st, :], xbt[:, 0, st, xc0:xc0 + 64],
                                          cx.ddTf[:, 2 * c + st, h:h + 1], None, ALU.mult)
                p1_xdd[(c, h)] = s_ev.inc(ins, 1)

            @b.tensor
            def _(t, c=c, h=h, hp=hp, cpar=cpar):
                t.wait_ge(s_ev.h, p1_xdd[(c, h)])
                if (c, h - 4) in p1_cu:
                    t.wait_ge(s_ev.h, p1_cu[(c, h - 4)])
                elif c >= 1 and h < 4:
                    t.wait_ge(s_ev.h, p1_cu[(c - 1, NH - 4 + h)])
                bc0 = 1024 + (h // 8) * 128
                for st in range(2):
                    ins = t.matmul(st_banks[hp], xbt[:, 0, st, bc0:bc0 + 128],
                                   xdd[:, hp, st, :], start=(st == 0), stop=(st == 1))
                p1_mm[(c, h)] = s_pe.inc(ins, 1)

            @b.vector
            def _(v, c=c, h=h, hp=hp):
                v.wait_ge(s_pe.h, p1_mm[(c, h)])
                v.tensor_copy(states[:, c, h * 64:(h + 1) * 64], st_banks[hp])
                if c == 0:
                    ins = v.tensor_copy(carry[:, h * 64:(h + 1) * 64], st_banks[hp])
                else:
                    ins = v.scalar_tensor_tensor(
                        carry[:, h * 64:(h + 1) * 64], carry[:, h * 64:(h + 1) * 64],
                        cx.dbcc[:, c, h:h + 1], st_banks[hp], ALU.mult, ALU.add)
                p1_cu[(c, h)] = s_ev.inc(ins, 1)

    exp_ev = [None]
    cin_ev = [None]

    @b.vector
    def _(v):
        v.wait_ge(s_ev.h, p1_cu[(CH - 1, NH - 1)])
        v.wait_ge(cx.s_stc[0].h, bf["conv_all_st"][0])
        v.wait_ge(cx.s_stc[1].h, bf["conv_all_st"][1])
        v.wait_ge(cx.s_wcv[0].h, bf["wout_conv_done"][0])
        v.wait_ge(cx.s_wcv[1].h, bf["wout_conv_done"][1])
        exp_ev[0] = s_ev.inc(
            v.tensor_scalar(exc[:, :], carry[:, :], cx.flags[:, 0:1], None, ALU.mult), 1)

    @b.gpsimd
    def _(g):
        g.wait_ge(s_ev.h, exp_ev[0])
        g.dma_start(out=cx.ex2_in, in_=exc[:, :]).then_inc(cx.s_po.h, 16)
        cx.s_po.v += 16
        g.wait_ge(cx.s_po.h, cx.s_po.v)
        g.collective_compute(
            "AllReduce", ALU.add,
            replica_groups=RGROUPS,
            ins=[cx.ex2_in.opt()], outs=[cx.ex2_out.opt()],
        ).then_inc(s_cc.h, 1)
        s_cc.v += 1
        g.wait_ge(s_cc.h, s_cc.v)
        g.dma_start(out=exc[:, :], in_=cx.ex2_out).then_inc(cx.s_po.h, 16)
        cx.s_po.v += 16
    exc_ld = cx.s_po.v

    @b.vector
    def _(v):
        v.wait_ge(cx.s_po.h, exc_ld)
        v.wait_ge(s_ev.h, s_ev.inc(
            v.tensor_scalar(carry[:, :], exc[:, :], cx.flags[:, 1:2], None, ALU.mult), 1))
        cin_ev[0] = s_ev.inc(v.tensor_copy(carry_bf[:, :], carry[:, :]), 1)

    p2_xbt = [None] * CH
    p2_G = {}
    p2_seg = {}
    p2_gl2 = {}
    cx._abc_ld = {}
    cx._ea_ld = {}
    p2_L = {}
    p2_gl = {}
    p2_Y = {}
    p2_yev = {}
    p2_sil = {}
    p2_y2 = {}
    p2_sq = {}
    p2_ms = {}
    p2_rr = {}
    p2_rb = {}
    p2_yn = {}
    p2_cg = {}
    op_ev = {}
    op_st = {}
    segb, Lb, glb = bf["segb"], bf["Lb"], bf["glb"]
    ebcb, csb, cgb = bf["ebcb"], bf["csb"], bf["cgb"]
    expab = ebcb
    Gb = [ps[:, i * 512:(i + 1) * 512].rearrange("p (a b) -> p a b", a=2) for i in range(4)]
    Yb = [ps[:, 2048:2304][0:64, :], ps[:, 2560:2816][0:64, :]]

    for c in range(CH):
        cpar = c % 2

        @b.sync
        def _(sy, c=c, cpar=cpar):
            if c >= 1:
                sy.wait_ge(s_ev.h, p2_yev[(c - 1, NH - 1)])
            else:
                sy.wait_ge(s_ev.h, cin_ev[0])
            sy.dma_start(out=xbt[:, 0, :, :],
                         in_=cx.xbt_dram[2 * c:2 * c + 2].rearrange("a p f -> p a f")
                         ).then_inc(s_ldx[cpar].h, 16)
            s_ldx[cpar].v += 16
            sy.dma_start(out=bck[:, cpar, :, :],
                         in_=cx.xbc_conv[8:16, :, c * CL:(c + 1) * CL].rearrange(
                             "f p t -> p f t")).then_inc(s_ldx[cpar].h, 16)
            s_ldx[cpar].v += 16
            p2_xbt[c] = s_ldx[cpar].inc(
                sy.dma_start(out=xfm[:, cpar, :, :],
                             in_=cx.xbc_conv[0:8, :, c * CL:(c + 1) * CL].rearrange(
                                 "f p t -> p f t")), 16)

        for g in range(NG):
            gp4 = g % 4
            gp = g % 2
            pp = g % 2

            ab4 = g % 4

            @b.sync
            def _(sy, c=c, g=g, ab4=ab4):
                prev = (c, g - 4) if g >= 4 else ((c - 1, NG - 4 + g) if c >= 1 else None)
                if prev is not None:
                    sy.wait_ge(s_ev.h, p2_seg[prev])
                else:
                    sy.wait_ge(cx.s_dt.h, cx.acs_st)
                    sy.wait_ge(s_ev.h, cin_ev[0])
                p2_abc = cx.s_abc[ab4].inc(
                    sy.dma_start(out=abc[:, ab4, :, :],
                                 in_=cx.acs_dram[2 * g:2 * g + 2, c * CL:(c + 1) * CL]
                                 .unsqueeze(0).broadcast_to([128, 2, CL])), 16)
                if not hasattr(cx, "_abc_ld"):
                    cx._abc_ld = {}
                cx._abc_ld[(c, g)] = p2_abc

            @b.scalar
            def _(se, c=c, g=g, pp=pp):
                if (c, g - 2) in p2_gl2:
                    se.wait_ge(s_ev.h, p2_gl2[(c, g - 2)])
                elif c >= 1 and g < 2:
                    se.wait_ge(s_ev.h, p2_gl2[(c - 1, NG - 2 + g)])
                else:
                    se.wait_ge(cx.s_dt.h, cx.acs_st)
                    se.wait_ge(s_ev.h, cin_ev[0])
                p2_ea = cx.s_lda[pp].inc(
                    se.dma_start(out=expab[:, pp, :, :],
                                 in_=cx.expacs_dram[2 * g:2 * g + 2, c * CL:(c + 1) * CL]
                                 .unsqueeze(0).broadcast_to([128, 2, CL])), 16)
                cx._ea_ld[(c, g)] = p2_ea

            @b.vector
            def _(v, c=c, g=g, pp=pp, ab4=ab4):
                v.wait_ge(cx.s_abc[ab4].h, cx._abc_ld[(c, g)])
                if (c, g - 2) in p2_L:
                    v.wait_ge(s_sc.h, p2_L[(c, g - 2)])
                elif c >= 1 and g < 2:
                    v.wait_ge(s_sc.h, p2_L[(c - 1, NG - 2 + g)])
                elif c == 0 and g < 2:
                    v.wait_ge(cx.s_ldc.h, cx.attn_const)
                    v.wait_ge(s_ev.h, cin_ev[0])
                p2_seg[(c, g)] = s_ev.inc(v.tensor_tensor(
                    segb[:, pp, :, :, :],
                    abc[:, ab4, :, :].unsqueeze(2).broadcast_to([128, 2, 2, CL]),
                    cx.negbig.unsqueeze(1).broadcast_to([128, 2, 2, CL]), ALU.add), 1)

            @b.sync
            def _(sy, c=c, g=g, gp=gp):
                if (c, 2 * (g - 4) + 1) in p2_gl:
                    sy.wait_ge(s_ev.h, p2_gl[(c, 2 * (g - 4) + 1)])
                elif c >= 1 and g < 4:
                    sy.wait_ge(s_ev.h, p2_gl[(c - 1, 2 * (NG - 4 + g) + 1)])
                else:
                    sy.wait_ge(cx.s_stc[0].h, bf["conv_all_st"][0])
                    sy.wait_ge(cx.s_stc[1].h, bf["conv_all_st"][1])
                p2_cg[(c, g)] = s_ldcg[g % 4].inc(
                    sy.dma_start(out=cgb[:, g % 4, :],
                                 in_=cx.xbc_conv[16 + g, :, c * CL:(c + 1) * CL]), 16)

            @b.tensor
            def _(t, c=c, g=g, gp4=gp4, cpar=cpar):
                if c >= 2 and c % 2 == 0 and g == 0:
                    t.wait_ge(s_ev.h, op_ev[(c - 2) // 2])
                t.wait_ge(s_ldcg[g % 4].h, p2_cg[(c, g)])
                t.wait_ge(s_ldx[cpar].h, p2_xbt[c])
                if (c, 2 * (g - 4) + 1) in p2_gl:
                    t.wait_ge(s_ev.h, p2_gl[(c, 2 * (g - 4) + 1)])
                elif c >= 1 and g < 4:
                    t.wait_ge(s_ev.h, p2_gl[(c - 1, 2 * (NG - 4 + g) + 1)])
                for st in range(2):
                    ins = t.matmul(Gb[gp4][:, st, :],
                                   bck[:, cpar, g // 4, st * 128:(st + 1) * 128],
                                   cgb[:, g % 4, :], start=True, stop=True)
                p2_G[(c, g)] = s_pe.inc(ins, 1)

            @b.scalar
            def _(s, c=c, g=g, pp=pp):
                s.wait_ge(s_ev.h, p2_seg[(c, g)])
                s.wait_ge(cx.s_lda[pp].h, cx._ea_ld[(c, g)])
                if c == 0 and g == 0:
                    s.wait_ge(s_ev.h, cx.dt_done_ev)
                for hh in range(2):
                    hgl = 2 * g + hh
                    for st in range(2):
                        ins = s.activation(Lb[:, pp, hh, st, :], segb[:, pp, hh, st, :], AF.Exp,
                                           bias=cx.biasTf[:, 2 * c + st, hgl:hgl + 1])
                p2_L[(c, g)] = s_sc.inc(ins, 1)

            @b.vector
            def _(v, c=c, g=g, pp=pp, gp4=gp4):
                v.wait_ge(s_sc.h, p2_L[(c, g)])
                v.wait_ge(s_pe.h, p2_G[(c, g)])
                v.tensor_tensor(csb[:, pp, :, :],
                                cgb[:, g % 4, :].unsqueeze(1).broadcast_to([128, 2, CL]),
                                expab[:, pp, :, :], ALU.mult)
                p2_gl[(c, 2 * g + 1)] = s_ev.inc(v.tensor_tensor(
                    glb[:, pp, :, :, :],
                    Gb[gp4].unsqueeze(1).broadcast_to([128, 2, 2, CL]),
                    Lb[:, pp, :, :, :], ALU.mult), 1)
                p2_gl2[(c, g)] = p2_gl[(c, 2 * g + 1)]

            for hh in range(2):
                h = 2 * g + hh
                xc0 = (h // 8) * 128 + (h % 2) * 64

                @b.tensor
                def _(t, c=c, g=g, h=h, hh=hh, pp=pp, cpar=cpar, xc0=xc0):
                    t.wait_ge(s_ev.h, p2_gl[(c, 2 * g + 1)])
                    if (c, h - 2) in p2_yev:
                        t.wait_ge(s_ev.h, p2_yev[(c, h - 2)])
                    elif c >= 1 and h < 2:
                        t.wait_ge(s_ev.h, p2_yev[(c - 1, NH - 2 + h)])
                    t.matmul(Yb[hh], xbt[:, 0, 0, xc0:xc0 + 64], glb[:, pp, hh, 0, :],
                             start=True, stop=False)
                    t.matmul(Yb[hh], xbt[:, 0, 1, xc0:xc0 + 64], glb[:, pp, hh, 1, :],
                             start=False, stop=False)
                    p2_Y[(c, h)] = s_pe.inc(
                        t.matmul(Yb[hh], carry_bf[:, h * 64:(h + 1) * 64], csb[:, pp, hh, :],
                                 start=False, stop=True), 1)

                @b.vector
                def _(v, c=c, g=g, h=h, hh=hh, gp=gp, cpar=cpar):
                    v.wait_ge(s_pe.h, p2_Y[(c, h)])
                    po = (h % 2) * 64
                    v.scalar_tensor_tensor(ygc[hh * 64:hh * 64 + 64, gp, :],
                                           xfm[po:po + 64, cpar, h // 8, :],
                                           cx.d_rep[po:po + 64, h:h + 1],
                                           Yb[hh], ALU.mult, ALU.add)
                    ssync(v, s_ev, v.scalar_tensor_tensor(
                        carry[:, h * 64:(h + 1) * 64], carry[:, h * 64:(h + 1) * 64],
                        cx.dbcc[:, c, h:h + 1], states[:, c, h * 64:(h + 1) * 64],
                        ALU.mult, ALU.add))
                    p2_yev[(c, h)] = s_ev.inc(
                        v.tensor_copy(carry_bf[:, h * 64:(h + 1) * 64],
                                      carry[:, h * 64:(h + 1) * 64]), 1)

            @b.scalar
            def _(s, c=c, g=g, gp=gp):
                if (c, g - 2) in p2_sil:
                    s.wait_ge(s_ev.h, p2_y2[(c, g - 2)])
                elif c >= 1 and g < 2:
                    s.wait_ge(s_ev.h, p2_y2[(c - 1, NG - 2 + g)])
                p2_sil[(c, g)] = cx.s_ld2[gp].inc(
                    s.dma_start(out=zgb[:, gp, :],
                                in_=cx.z_raw[g, :, c * CL:(c + 1) * CL]), 16)

            @b.vector
            def _(v, c=c, g=g, gp=gp):
                v.wait_ge(cx.s_ld2[gp].h, p2_sil[(c, g)])
                v.wait_ge(s_ev.h, p2_yev[(c, 2 * g + 1)])
                ssync(v, s_ev, v.tensor_tensor(y2[:, gp, :], ygc[:, gp, :], zgb[:, gp, :], ALU.mult))
                p2_sq[(c, g)] = s_ev.inc(
                    v.tensor_tensor(y2q[:, gp, :], y2[:, gp, :], y2[:, gp, :], ALU.mult), 1)
                p2_y2[(c, g)] = p2_sq[(c, g)]

            @b.tensor
            def _(t, c=c, g=g, gp=gp):
                t.wait_ge(s_ev.h, p2_sq[(c, g)])
                if (c, g - 1) in p2_rr:
                    t.wait_ge(s_sc.h, p2_rr[(c, g - 1)])
                elif c >= 1 and g == 0:
                    t.wait_ge(s_sc.h, p2_rr[(c - 1, NG - 1)])
                p2_ms[(c, g)] = s_pe.inc(
                    t.matmul(ps[:, 3072:3328][0:1, :], cx.ones_col[:, :], y2q[:, gp, :],
                             start=True, stop=True), 1)

            @b.scalar
            def _(s, c=c, g=g, gp=gp):
                s.wait_ge(s_pe.h, p2_ms[(c, g)])
                s.wait_ge(s_sc.h, s_sc.inc(
                    s.activation(rrow[:, gp, :], ps[:, 3072:3328][0:1, :], AF.Ln,
                                 scale=1.0 / 128.0, bias=cx.eps_col[0:1, :]), 1))
                p2_rr[(c, g)] = s_sc.inc(
                    s.activation(rrow[:, gp, :], rrow[:, gp, :], AF.Exp, scale=-0.5), 1)

            @b.tensor
            def _(t, c=c, g=g, gp=gp):
                t.wait_ge(s_sc.h, p2_rr[(c, g)])
                if (c, g - 1) in p2_yn:
                    t.wait_ge(s_ev.h, p2_yn[(c, g - 1)])
                elif c >= 1 and g == 0:
                    t.wait_ge(s_ev.h, p2_yn[(c - 1, NG - 1)])
                p2_rb[(c, g)] = s_pe.inc(
                    t.matmul(ps[:, 3584:3840][:, 0:256], cx.ones_row[:, :], rrow[0:1, gp, :],
                             start=True, stop=True), 1)

            @b.vector
            def _(v, c=c, g=g, gp=gp, cpar=cpar):
                v.wait_ge(s_pe.h, p2_rb[(c, g)])
                p2_yn[(c, g)] = s_ev.inc(
                    v.scalar_tensor_tensor(ynormF[:, g, cpar * CL:(cpar + 1) * CL],
                                           ps[:, 3584:3840][:, 0:256], cx.norm_sb[:, g:g + 1],
                                           y2[:, gp, :], ALU.mult, ALU.mult), 1)

        if c % 2 == 1:
            g2 = c // 2
            pallv = ps.rearrange("p (a b) -> p a b", a=2)
            yn_done = p2_yn[(c, NG - 1)]
            pe_op = {}
            ev_op = {}
            wld = {}
            x1_ld = {}
            pe0 = s_pe.v
            kk = 0
            for jp in range(2):
                for hc in range(NG):
                    pe_op[(jp, hc)] = pe0 + kk + 1
                    kk += 1
            if not hasattr(cx, "_op_pe_last"):
                cx._op_pe_last = {}
            cx._op_pe_last[g2] = pe0 + kk

            @b.gpsimd
            def _(g_, g2=g2, c=c):
                g_.wait_ge(s_pe.h, p2_Y[(c, NH - 1)])
                for jp in range(2):
                    for hc in range(NG):
                        wpar = hc % 2
                        if hc >= 2:
                            g_.wait_ge(s_pe.h, pe_op[(jp, hc - 2)])
                        elif jp >= 1:
                            g_.wait_ge(s_pe.h, pe_op[(jp - 1, NG - 2 + hc)])
                        elif g2 >= 1:
                            g_.wait_ge(s_pe.h, cx._op_pe_last[g2 - 1])
                        else:
                            g_.wait_ge(cx.s_wcv[0].h, bf["wout_conv_done"][0])
                            g_.wait_ge(cx.s_wcv[1].h, bf["wout_conv_done"][1])
                        wld[(jp, hc)] = s_ldw[wpar].inc(
                            g_.dma_start(out=woutb[:, wpar, :], in_=cx.wout_bfd[hc]), 16)

            for jp in range(2):
                for tm in (2 * jp, 2 * jp + 1):
                    tpar = tm % 2

                    @b.sync
                    def _(sy, tm=tm, tpar=tpar, g2=g2):
                        if tm >= 2:
                            sy.wait_ge(s_ev.h, ev_op[tm - 2])
                            sy.wait_ge(s_st[tpar].h, op_st[(g2, tm - 2)])
                        elif g2 >= 1:
                            sy.wait_ge(s_ev.h, op_ev[g2 - 1])
                            sy.wait_ge(s_st[tpar].h, op_st[(g2 - 1, 2 + tm)])
                        else:
                            sy.wait_ge(s_ev.h, p2_yev[(1, NH - 1)])
                        row0 = g2 * 512 + tm * 128
                        x1_ld[tm] = s_ld[tpar].inc(
                            sy.dma_start(out=x1t[:, tpar, :], in_=cx.x1[row0:row0 + 128, :]), 16)

                @b.tensor
                def _(t, jp=jp, g2=g2, yn_done=yn_done):
                    if jp == 0:
                        t.wait_ge(s_ev.h, yn_done)
                        if g2 >= 1:
                            t.wait_ge(s_ev.h, op_ev[g2 - 1])
                    else:
                        t.wait_ge(s_ev.h, ev_op[0])
                        t.wait_ge(s_ev.h, ev_op[1])
                    for hc in range(NG):
                        wpar = hc % 2
                        t.wait_ge(s_ldw[wpar].h, wld[(jp, hc)])
                        for tm in (2 * jp, 2 * jp + 1):
                            tpar = tm % 2
                            for hb in range(NHB):
                                ins = t.matmul(pallv[:, tpar, hb * 512:(hb + 1) * 512],
                                               ynormF[:, hc, tm * 128:(tm + 1) * 128],
                                               woutb[:, wpar, hb * 512:(hb + 1) * 512],
                                               start=(hc == 0), stop=(hc == NG - 1))
                        val = s_pe.inc(ins, 1)
                        assert val == pe_op[(jp, hc)]

                for tm in (2 * jp, 2 * jp + 1):
                    tpar = tm % 2

                    @b.vector
                    def _(v, jp=jp, tm=tm, tpar=tpar, g2=g2):
                        v.wait_ge(s_pe.h, pe_op[(jp, NG - 1)])
                        v.wait_ge(s_ld[tpar].h, x1_ld[tm])
                        if tm >= 2:
                            v.wait_ge(s_st[tpar].h, op_st[(g2, tm - 2)])
                        ev_op[tm] = s_ev.inc(
                            v.tensor_tensor(x2t[:, tpar, :], pallv[:, tpar, :],
                                            x1t[:, tpar, :], ALU.add), 1)

                    @b.sync
                    def _(sy, tm=tm, tpar=tpar, g2=g2):
                        sy.wait_ge(s_ev.h, ev_op[tm])
                        row0 = g2 * 512 + tm * 128
                        op_st[(g2, tm)] = s_st[tpar].inc(
                            sy.dma_start(out=cx.x2[row0:row0 + 128, :], in_=x2t[:, tpar, :]), 16)
            op_ev[g2] = s_ev.v

    cx.attn_done_st = [s_st[0].v, s_st[1].v]


def emit_final_wait(cx):
    b = cx.block

    @b.sync
    def _(sy):
        sy.wait_ge(cx.s_st[0].h, cx.s_st[0].v)
        sy.wait_ge(cx.s_st[1].h, cx.s_st[1].v)


def kernel(**inputs):
    per_core = _prep_core_inputs(inputs)
    cxp = build_program()
    res = run_bass_kernel_spmd(cxp.nc, per_core, core_ids=list(range(NCORES)))
    outs = [res.results[c]["out"] for c in range(NCORES)]
    return np.concatenate(outs, axis=0).reshape(B, L, D)


# version 10 (final)
# speedup vs baseline: 1.0927x; 1.0465x over previous
import contextlib
import numpy as np
import ml_dtypes
import concourse.bass as bass
import concourse.mybir as mybir
from concourse.bass_utils import run_bass_kernel_spmd

F32 = mybir.dt.float32
BF16 = mybir.dt.bfloat16
AF = mybir.ActivationFunctionType
ALU = mybir.AluOpType

NCORES = 8
B, L, D = 4, 4096, 2048
T = 2048
NTT = T // 128
HC = D // 128
NE, I = 16, 1792
NIM = I // 128
NHB = D // 512
HTOK = T // 2
NTM = HTOK // 128
DIN = 4096
DXB = 1024
CONVD = 6144
NFC = CONVD // 128
FIN = 10304
NFB = 81
NH = 64
CH, CL = 8, 256
NG = 32
EPS = 1e-5

STAGE = 3
SIM_SAFE = False
RGROUPS = [[0, 1], [2, 3], [4, 5], [6, 7]]



def _prep_core_inputs(inputs):
    f32 = np.float32
    gate = np.asarray(inputs["L0_ffn_gate_proj"], f32)
    up = np.asarray(inputs["L0_ffn_up_proj"], f32)
    down = np.asarray(inputs["L0_ffn_down_proj"], f32)
    W_in = np.asarray(inputs["L0_attn_W_in"], f32)
    W_out = np.asarray(inputs["L0_attn_W_out"], f32)
    router = np.asarray(inputs["L0_ffn_router_w"], f32)
    ebias = np.asarray(inputs["L0_ffn_expert_bias"], f32)
    conv_w = np.asarray(inputs["L0_attn_conv_w"], f32)
    conv_b = np.asarray(inputs["L0_attn_conv_b"], f32)
    dt_bias = np.asarray(inputs["L0_attn_dt_bias"], f32)
    A_log = np.asarray(inputs["L0_attn_A_log"], f32)
    Dv = np.asarray(inputs["L0_attn_D"], f32)
    nw = np.asarray(inputs["L0_attn_norm_weight"], f32)
    u = np.asarray(inputs["L0_attn_u"], f32).reshape(B * L, D)

    sh = {}
    sh["gate_blk"] = np.ascontiguousarray(
        gate.reshape(NE, HC, 128, NIM, 128).transpose(0, 3, 2, 1, 4).reshape(NE * NIM, 128, HC * 128))
    sh["up_blk"] = np.ascontiguousarray(
        up.reshape(NE, HC, 128, NIM, 128).transpose(0, 3, 2, 1, 4).reshape(NE * NIM, 128, HC * 128))
    sh["down_nat"] = np.ascontiguousarray(down.reshape(NE * NIM, 128, D))
    WinT = np.zeros((D, NFB * 128), f32)
    WinT[:, :FIN] = W_in.T
    sh["win_blk"] = np.ascontiguousarray(WinT.reshape(HC, 128, NFB, 128).transpose(2, 1, 0, 3))
    sh["wout_nat"] = np.ascontiguousarray(W_out.T.reshape(NG, 128, D))
    sh["router_t"] = np.ascontiguousarray(router.T.reshape(HC, 128, NE).transpose(1, 0, 2))
    sh["bias_row"] = np.ascontiguousarray(ebias.reshape(1, NE))
    sh["convw_t"] = np.ascontiguousarray(conv_w.reshape(NFC, 128, 4).transpose(1, 0, 2))
    sh["convb_t"] = np.ascontiguousarray(conv_b.reshape(NFC, 128).T)
    sh["dtb_col"] = np.ascontiguousarray(dt_bias.reshape(NH, 1))
    sh["alog_col"] = np.ascontiguousarray(A_log.reshape(NH, 1))
    sh["d_rep"] = np.ascontiguousarray(np.broadcast_to(Dv[None, :], (128, NH)).copy())
    sh["norm_t"] = np.ascontiguousarray(nw.reshape(NG, 128).T)
    sh["ident32"] = np.eye(128, dtype=f32)
    sh["identb"] = np.eye(128, dtype=f32).astype(ml_dtypes.bfloat16)
    sh["ones_row"] = np.ones((1, 128), f32)
    sh["ones_mat"] = np.ones((128, 128), f32)
    sh["ones_col"] = np.ones((128, 1), f32)
    sh["eps_col"] = np.full((128, 1), EPS, f32)
    tidx = np.arange(CL)[None, None, :]
    sidx = np.arange(128)[:, None, None] + 128 * np.arange(2)[None, :, None]
    sh["negbig"] = np.where(tidx >= sidx, 0.0, -1e9).astype(f32)

    per_core = []
    for c in range(NCORES):
        d = dict(sh)
        d["u_slice"] = np.ascontiguousarray(u[c * T:(c + 1) * T, :])
        half = c % 2
        d["fsend"] = np.full((128, 1), 1.0 if half == 0 else 0.0, f32)
        d["frecv"] = np.full((128, 1), 1.0 if half == 1 else 0.0, f32)
        per_core.append(d)
    return per_core



class Sem:
    __slots__ = ("h", "v", "name")

    def __init__(self, es, nc, name):
        self.h = es.enter_context(nc.semaphore(name))
        self.v = 0
        self.name = name

    def inc(self, ins, n=1):
        ins.then_inc(self.h, n)
        self.v += n
        return self.v


class Arena:
    def __init__(self, ap_u8, nbytes):
        self.ap = ap_u8
        self.nbytes = nbytes
        self.off = 0
        self.marks = []

    def push(self):
        self.marks.append(self.off)

    def pop(self):
        self.off = self.marks.pop()

    def _view(self, off, shape, dtype):
        esz = 2 if dtype == BF16 else 4
        n = 1
        for s in shape[1:]:
            n *= s
        nb = n * esz
        assert off + nb <= self.nbytes, f"arena overflow: {off}+{nb}>{self.nbytes}"
        v = self.ap[:, off:off + nb].bitcast(dtype)
        if len(shape) > 2:
            dims = shape[1:]
            pat_in = "p (" + " ".join(f"d{i}" for i in range(len(dims))) + ")"
            pat_out = "p " + " ".join(f"d{i}" for i in range(len(dims)))
            v = v.rearrange(pat_in + " -> " + pat_out, **{f"d{i}": dims[i] for i in range(len(dims))})
        if shape[0] != 128:
            v = v[0:shape[0]]
        return v, off + nb

    def alloc(self, shape, dtype):
        off = (self.off + 3) & ~3
        v, end = self._view(off, shape, dtype)
        self.off = end
        self.last_off = off
        return v

    def at(self, off_bytes, shape, dtype):
        v, _ = self._view(off_bytes, shape, dtype)
        return v


class Ctx:
    pass


def ssync(eng, sem, ins):
    if SIM_SAFE:
        eng.wait_ge(sem.h, sem.inc(ins, 1))
    return ins


def fence(cx, engines=("sync", "tensor", "vector", "scalar", "gpsimd")):
    b = cx.block
    sems = ([cx.s_pe, cx.s_ev, cx.s_sc, cx.s_ldc, cx.s_wd, cx.s_cc] + cx.s_st + cx.s_ld + cx.s_ld2 +
            cx.s_wg + cx.s_wu + cx.s_gx + getattr(cx, 's_ldx', []) + getattr(cx, 's_ldw', []) +
            getattr(cx, 's_stc', []) + getattr(cx, 's_ldcg', []) + getattr(cx, 's_lda', []) +
            getattr(cx, 's_cin', []) + getattr(cx, 's_wcv', []) + [x for x in [getattr(cx, 's_dt', None), getattr(cx, 's_po', None)] if x])
    vals = [(s.h, s.v) for s in sems if s.v > 0]

    for name in engines:
        @getattr(b, name)
        def _(e, vals=vals):
            for h, v in vals:
                e.wait_ge(h, v)



def build_program(stage=None):
    if stage is None:
        stage = STAGE
    nc = bass.Bass("TRN2", target_bir_lowering=False)
    cx = Ctx()
    cx.nc = nc
    cx.stage = stage

    def din(name, shape, dtype=F32):
        return nc.dram_tensor(name, shape, dtype, kind="ExternalInput").ap()

    cx.u = din("u_slice", [T, D])
    cx.gate_blkd = din("gate_blk", [NE * NIM, 128, HC * 128])
    cx.up_blkd = din("up_blk", [NE * NIM, 128, HC * 128])
    cx.down_natd = din("down_nat", [NE * NIM, 128, D])
    cx.win_blkd = din("win_blk", [NFB, 128, HC, 128])
    cx.wout_natd = din("wout_nat", [NG, 128, D])
    cx.router_td = din("router_t", [128, HC, NE])
    cx.bias_rowd = din("bias_row", [1, NE])
    cx.convw_td = din("convw_t", [128, NFC, 4])
    cx.convb_td = din("convb_t", [128, NFC])
    cx.dtb_cold = din("dtb_col", [NH, 1])
    cx.alog_cold = din("alog_col", [NH, 1])
    cx.d_repd = din("d_rep", [128, NH])
    cx.norm_td = din("norm_t", [128, NG])
    cx.ident32d = din("ident32", [128, 128])
    cx.identbd = din("identb", [128, 128], BF16)
    cx.ones_rowd = din("ones_row", [1, 128])
    cx.ones_matd = din("ones_mat", [128, 128])
    cx.ones_cold = din("ones_col", [128, 1])
    cx.eps_cold = din("eps_col", [128, 1])
    cx.negbigd = din("negbig", [128, 2, CL])
    cx.fsendd = din("fsend", [128, 1])
    cx.frecvd = din("frecv", [128, 1])

    cx.out = nc.dram_tensor("out", [T, D], F32, kind="ExternalOutput").ap()

    def dsc(name, shape, dtype):
        return nc.dram_tensor(name, shape, dtype).ap()

    cx.xln1 = dsc("xln1", [T, D], BF16)
    cx.x1 = dsc("x1", [T, D], F32)
    cx.xln2 = dsc("xln2", [T, D], BF16)
    cx.x2 = dsc("x2", [T, D], F32)
    cx.xbc_raw = dsc("xbc_raw", [NFC, 128, T], BF16)
    cx.z_raw = dsc("z_raw", [NG, 128, T], BF16)
    cx.xbc_conv = dsc("xbc_conv", [NFC, 128, T], BF16)
    cx.acs_dram = dsc("acs_d", [NH, T], F32)
    cx.xbt_dram = dsc("xbt_dram2", [16, 128, 2048], BF16)
    cx.expacs_dram = dsc("expacs_d", [NH, T], BF16)
    cx.dall_dram = dsc("dall_d", [1, NH * CH], F32)
    cx.wout_bfd = dsc("wout_bf", [NG, 128, D], BF16)
    cx.ex1_in = dsc("ex1_in", [128, NFC * 3], F32)
    cx.ex1_out = dsc("ex1_out", [128, NFC * 3], F32)
    cx.ex2_in = dsc("ex2_in", [128, DIN], F32)
    cx.ex2_out = dsc("ex2_out", [128, DIN], F32)

    es = contextlib.ExitStack()
    cx.es = es
    es.__enter__()

    sbt = lambda name, shape, dt=F32: es.enter_context(nc.sbuf_tensor(name, shape, dt))
    cx.ident32 = sbt("c_id32", [128, 128])
    cx.identb = sbt("c_idb", [128, 128], BF16)
    cx.ones_row = sbt("c_ones_r", [1, 128])
    cx.ones_mat = sbt("c_ones_m", [128, 128])
    cx.ones_col = sbt("c_ones_c", [128, 1])
    cx.eps_col = sbt("c_eps_c", [128, 1])
    cx.router_t = sbt("c_router", [128, HC, NE])
    cx.biasr = sbt("c_biasr", [1, NE])
    cx.alpha = sbt("alpha", [128, NTT, NE])
    cx.dtbc = sbt("c_dtbc", [NH, 1])
    cx.alogc = sbt("c_alogc", [NH, 1])
    cx.d_rep = sbt("c_drep", [128, NH])
    cx.norm_sb = sbt("c_norm", [128, NG])
    cx.flags = sbt("c_flags", [128, 2])
    cx.statw = sbt("statw", [128, 8])
    cx.acol = sbt("acolA", [NH, 1])

    AR_BYTES = 206848
    cx.arena_t = es.enter_context(nc.sbuf_tensor("arena", [128, AR_BYTES], mybir.dt.uint8))
    cx.ar = Arena(cx.arena_t.ap(), AR_BYTES)

    cx.psum_t = es.enter_context(nc.psum_tensor("psum", [128, 4096], F32))
    cx.psum = cx.psum_t.ap()

    S = lambda n: Sem(es, nc, n)
    cx.s_ldc = S("s_ldc")
    cx.s_ld = [S("s_ld0"), S("s_ld1")]
    cx.s_st = [S("s_st0"), S("s_st1")]
    cx.s_ld2 = [S("s_ld2a"), S("s_ld2b")]
    cx.s_wg = [S("s_wg0"), S("s_wg1")]
    cx.s_wu = [S("s_wu0"), S("s_wu1")]
    cx.s_wd = S("s_wd")
    cx.s_gx = [S("s_gx0"), S("s_gx1")]
    cx.s_pe = S("s_pe")
    cx.s_ev = S("s_ev")
    cx.s_sc = S("s_sc")
    cx.s_po = S("s_po")
    cx.s_cc = S("s_cc")
    cx.s_ldx = [S("s_ldx0"), S("s_ldx1")]
    cx.s_ldw = [S("s_ldw0"), S("s_ldw1")]
    cx.s_stc = [S("s_stc0"), S("s_stc1")]
    cx.s_ldcg = [S("s_ldcg0"), S("s_ldcg1"), S("s_ldcg2"), S("s_ldcg3")]
    cx.s_lda = [S("s_lda0"), S("s_lda1")]
    cx.s_dt = S("s_dt")
    cx.s_wcv = [S("s_wcv0"), S("s_wcv1")]
    cx.s_cin = [S("s_cin0"), S("s_cin1")]
    cx.s_abc = [S("s_abc0"), S("s_abc1"), S("s_abc2"), S("s_abc3")]

    cx.block = es.enter_context(nc.Block())

    emit_const_loads(cx)
    emit_ln_router(cx, src=cx.u, xln=cx.xln1)
    emit_ffn(cx, xln=cx.xln1, resid=cx.u, dst=(cx.x1 if stage >= 2 else cx.out))
    if stage >= 2:
        emit_attn(cx)
    if stage == 2:
        bb = cx.block

        @bb.sync
        def _(sy):
            sy.wait_ge(cx.s_st[0].h, cx.s_st[0].v)
            sy.wait_ge(cx.s_st[1].h, cx.s_st[1].v)
            cx.s_st[0].inc(sy.dma_start(out=cx.out, in_=cx.x2), 16)
    if stage >= 3:
        emit_ln_router(cx, src=cx.x2, xln=cx.xln2)
        emit_ffn(cx, xln=cx.xln2, resid=cx.x2, dst=cx.out)
    emit_final_wait(cx)
    return cx



def preload_nlexp(cx, eng):
    from concourse.hw_specs import get_activation_tables
    import concourse.mybir as _mb
    for idx, (name, funcs) in enumerate(get_activation_tables(cx.nc.m.arch).items()):
        if name == "natural_log_exp_and_others":
            return eng.add_instruction(
                _mb.InstLoadActFuncSet(
                    name=cx.nc.get_next_instruction_name(),
                    act_func_set_id=idx, ins=[], outs=[]))
    raise ValueError


def emit_const_loads(cx):
    b = cx.block
    s = cx.s_ldc

    @b.sync
    def _(sy):
        pairs = [
            (cx.ident32[:, :], cx.ident32d), (cx.identb[:, :], cx.identbd),
            (cx.ones_row[:, :], cx.ones_rowd), (cx.ones_col[:, :], cx.ones_cold),
            (cx.ones_mat[:, :], cx.ones_matd),
            (cx.eps_col[:, :], cx.eps_cold),
            (cx.router_t[:, :, :], cx.router_td),
            (cx.biasr[:, :], cx.bias_rowd),
            (cx.dtbc[:, :], cx.dtb_cold),
            (cx.alogc[:, :], cx.alog_cold), (cx.d_rep[:, :], cx.d_repd),
            (cx.norm_sb[:, :], cx.norm_td),
            (cx.flags[:, 0:1], cx.fsendd), (cx.flags[:, 1:2], cx.frecvd),
        ]
        for dst, src in pairs:
            s.inc(sy.dma_start(out=dst, in_=src), 16)
    cx.c_const = s.v



def emit_ln_router(cx, src, xln):
    b, ar, ps = cx.block, cx.ar, cx.psum
    fence(cx)
    ar.push()
    xt = ar.alloc([128, 2, D], F32)
    scr32 = ar.alloc([128, 2, D], F32)
    xlnb = ar.alloc([128, 2, D], BF16)
    xf32 = ar.alloc([128, 2, 128], F32)
    jnkA = ar.alloc([128, D], F32)
    jnkB = ar.alloc([128, D], F32)
    logits = ar.alloc([128, NTT, NE], F32)
    lwork = ar.alloc([128, NTT, NE], F32)
    lwork2 = ar.alloc([128, NTT, NE], F32)
    awork = ar.alloc([128, NTT, 2], F32)

    s_ld, s_st = cx.s_ld, cx.s_st
    s_pe, s_ev, s_sc = cx.s_pe, cx.s_ev, cx.s_sc
    pslot = [ps[:, 2048:2176], ps[:, 2560:2688]]
    plog = ps[:, 3584:3600]

    ld_done = [None] * NTT
    st_done = [None] * NTT
    sc_sq = [None] * NTT
    sc_apply = [None] * NTT
    ev_stats = [None] * NTT
    ev_cast = [None] * NTT
    ev_evac = [[None] * HC for _ in range(NTT)]
    pe_tr = [[None] * HC for _ in range(NTT)]
    pe_mm = [None] * NTT
    lg = [None] * NTT

    M, SQ, NV, RS, NM, RST, NMR = 0, 1, 2, 3, 4, 5, 6

    for tt in range(NTT):
        par = tt % 2

        @b.sync
        def _(sy, tt=tt, par=par):
            if tt >= 2:
                sy.wait_ge(s_sc.h, sc_apply[tt - 2])
            ld_done[tt] = s_ld[par].inc(
                sy.dma_start(out=xt[:, par, :], in_=src[tt * 128:(tt + 1) * 128, :]), 16)

        @b.scalar
        def _(s, tt=tt, par=par):
            s.wait_ge(s_ld[par].h, ld_done[tt])
            if tt >= 2:
                s.wait_ge(s_pe.h, pe_tr[tt - 2][HC - 1])
            sc_sq[tt] = s_sc.inc(
                s.activation(jnkB[:, :], xt[:, par, :], AF.Square,
                             accum_out=cx.statw[:, SQ:SQ + 1]), 1)

        @b.vector
        def _(v, tt=tt, par=par):
            v.wait_ge(s_ld[par].h, ld_done[tt])
            acv = s_ev.inc(v.tensor_scalar(jnkA[:, :], xt[:, par, :], 1.0, None, ALU.mult,
                                           ALU.add, accum_out=cx.statw[:, M:M + 1]), 1)
            v.wait_ge(s_ev.h, acv)
            v.wait_ge(s_sc.h, sc_sq[tt])

            def vstep(ins):
                v.wait_ge(s_ev.h, s_ev.inc(ins, 1))

            vstep(v.tensor_scalar(cx.statw[:, M:M + 1], cx.statw[:, M:M + 1],
                                  1.0 / D, None, ALU.mult))
            vstep(v.tensor_scalar(cx.statw[:, SQ:SQ + 1], cx.statw[:, SQ:SQ + 1],
                                  1.0 / D, None, ALU.mult))
            vstep(v.scalar_tensor_tensor(cx.statw[:, NV:NV + 1], cx.statw[:, M:M + 1],
                                         cx.statw[:, M:M + 1], cx.statw[:, SQ:SQ + 1],
                                         ALU.mult, ALU.subtract))
            vstep(v.tensor_scalar(cx.statw[:, NV:NV + 1], cx.statw[:, NV:NV + 1], -1.0, EPS,
                                  ALU.mult, ALU.add))
            vstep(v.tensor_scalar(cx.statw[:, NM:NM + 1], cx.statw[:, M:M + 1], -1.0, None, ALU.mult))
            ev_stats[tt] = s_ev.inc(v.reciprocal(cx.statw[:, RS:RS + 1], cx.statw[:, NV:NV + 1]), 1)

        @b.scalar
        def _(s, tt=tt, par=par):
            s.wait_ge(s_ev.h, ev_stats[tt])
            sv = s_sc.inc(s.activation(cx.statw[:, RST:RST + 1], cx.statw[:, RS:RS + 1], AF.Sqrt), 1)
            s.wait_ge(s_sc.h, sv)
            sv2 = s_sc.inc(s.activation(cx.statw[:, NMR:NMR + 1], cx.statw[:, NM:NM + 1], AF.Identity,
                                        scale=cx.statw[:, RST:RST + 1]), 1)
            s.wait_ge(s_sc.h, sv2)
            sc_apply[tt] = s_sc.inc(
                s.activation(scr32[:, par, :], xt[:, par, :], AF.Identity,
                             scale=cx.statw[:, RST:RST + 1], bias=cx.statw[:, NMR:NMR + 1]), 1)

        @b.vector
        def _(v, tt=tt, par=par):
            v.wait_ge(s_sc.h, sc_apply[tt])
            if tt >= 2:
                v.wait_ge(s_st[par].h, st_done[tt - 2])
            ev_cast[tt] = s_ev.inc(v.tensor_copy(xlnb[:, par, :], scr32[:, par, :]), 1)

        @b.sync
        def _(sy, tt=tt, par=par):
            sy.wait_ge(s_ev.h, ev_cast[tt])
            st_done[tt] = s_st[par].inc(
                sy.dma_start(out=xln[tt * 128:(tt + 1) * 128, :], in_=xlnb[:, par, :]), 16)

        peb = s_pe.v
        evb = s_ev.v
        for hc in range(HC):
            pe_tr[tt][hc] = peb + hc + 1
            ev_evac[tt][hc] = evb + hc + 1
        pe_mm[tt] = peb + HC + 1
        lg[tt] = evb + HC + 1

        @b.tensor
        def _(t, tt=tt, par=par):
            if tt == 0:
                t.wait_ge(cx.s_ldc.h, cx.c_const)
            t.wait_ge(s_sc.h, sc_apply[tt])
            for hc in range(HC):
                sl = hc % 2
                if hc >= 2:
                    t.wait_ge(s_ev.h, ev_evac[tt][hc - 2])
                elif tt >= 1:
                    t.wait_ge(s_ev.h, ev_evac[tt - 1][HC - 2 + hc])
                t.transpose(pslot[sl], scr32[:, par, hc * 128:(hc + 1) * 128],
                            cx.ident32[:, :]).then_inc(s_pe.h, 1)
                s_pe.v += 1
                if hc >= 1:
                    t.wait_ge(s_ev.h, ev_evac[tt][hc - 1])
                    if hc == 1 and tt >= 1:
                        t.wait_ge(s_ev.h, lg[tt - 1])
                    t.matmul(plog, xf32[:, (hc - 1) % 2, :], cx.router_t[:, hc - 1, :],
                             start=(hc == 1), stop=False)
            t.wait_ge(s_ev.h, ev_evac[tt][HC - 1])
            t.matmul(plog, xf32[:, (HC - 1) % 2, :], cx.router_t[:, HC - 1, :],
                     start=False, stop=False)
            t.matmul(plog, cx.ones_row[:, :], cx.biasr[:, :],
                     start=False, stop=True).then_inc(s_pe.h, 1)
            s_pe.v += 1

        @b.vector
        def _(v, tt=tt, par=par):
            for hc in range(HC):
                sl = hc % 2
                v.wait_ge(s_pe.h, pe_tr[tt][hc])
                v.tensor_copy(xf32[:, sl, :], pslot[sl]).then_inc(s_ev.h, 1)
                s_ev.v += 1
            v.wait_ge(s_pe.h, pe_mm[tt])
            v.tensor_copy(logits[:, tt, :], plog).then_inc(s_ev.h, 1)
            s_ev.v += 1

        assert s_pe.v == pe_mm[tt] and s_ev.v == lg[tt]

    BIG = 1e9
    m_ = awork[:, :, 0:1]
    d_ = awork[:, :, 1:2]
    mbc = awork[:, :, 0:1].broadcast_to([128, NTT, NE])
    dbc = awork[:, :, 1:2].broadcast_to([128, NTT, NE])
    sc_exp = [None]
    ev_pre = [None]

    lg_all = s_ev.v

    @b.vector
    def _(v):
        def vstep(ins):
            v.wait_ge(s_ev.h, s_ev.inc(ins, 1))

        v.wait_ge(s_ev.h, lg_all)
        vstep(v.tensor_copy(lwork[:, :, :], logits[:, :, :]))
        vstep(v.tensor_reduce(m_, lwork[:, :, :], mybir.AxisListType.X, ALU.max))
        for r in range(3):
            vstep(v.tensor_tensor(lwork2[:, :, :], lwork[:, :, :], mbc, ALU.is_ge))
            vstep(v.tensor_scalar(lwork2[:, :, :], lwork2[:, :, :], -BIG, None, ALU.mult))
            vstep(v.tensor_tensor(lwork[:, :, :], lwork[:, :, :], lwork2[:, :, :], ALU.add))
            vstep(v.tensor_reduce(m_, lwork[:, :, :], mybir.AxisListType.X, ALU.max))
        vstep(v.tensor_tensor(lwork2[:, :, :], logits[:, :, :], mbc, ALU.is_ge))
        vstep(v.tensor_reduce(d_, logits[:, :, :], mybir.AxisListType.X, ALU.max))
        ev_pre[0] = s_ev.inc(
            v.tensor_tensor(lwork[:, :, :], logits[:, :, :], dbc, ALU.subtract), 1)

    @b.scalar
    def _(s):
        s.wait_ge(s_ev.h, ev_pre[0])
        sc_exp[0] = s_sc.inc(s.activation(lwork[:, :, :], lwork[:, :, :], AF.Exp), 1)

    @b.vector
    def _(v):
        def vstep(ins):
            v.wait_ge(s_ev.h, s_ev.inc(ins, 1))

        v.wait_ge(s_sc.h, sc_exp[0])
        vstep(v.tensor_tensor(lwork[:, :, :], lwork[:, :, :], lwork2[:, :, :], ALU.mult))
        vstep(v.tensor_reduce(d_, lwork[:, :, :], mybir.AxisListType.X, ALU.add))
        vstep(v.reciprocal(d_, d_))
        vstep(v.tensor_tensor(cx.alpha[:, :, :], lwork[:, :, :], dbc, ALU.mult))
        cx.alpha_done = s_ev.inc(
            v.tensor_scalar(cx.alpha[:, :, :], cx.alpha[:, :, :], 0.5, None, ALU.mult), 1)

    ar.pop()



def emit_ffn(cx, xln, resid, dst):
    b, ar, ps = cx.block, cx.ar, cx.psum
    fence(cx)
    ar.push()
    xlf = ar.alloc([128, HC, HTOK], BF16)
    acc = ar.alloc([128, NTM, D], F32)
    act = ar.alloc([128, NIM, HTOK], BF16)
    act_off0 = ar.last_off
    wg = ar.alloc([128, 2, HC, 128], BF16)
    wu = ar.alloc([128, 2, HC, 128], BF16)
    wd = ar.alloc([128, NIM, D], BF16)
    wd_off = ar.last_off
    xtile = ar.at(act_off0, [128, 2, D], BF16)
    rt = ar.at(wd_off, [128, 2, D], F32)
    ot = ar.at(wd_off + 16384, [128, 2, D], F32)

    s_ld, s_st = cx.s_ld, cx.s_st
    s_pe, s_ev, s_sc = cx.s_pe, cx.s_ev, cx.s_sc
    pall = ps.rearrange("p (a b) -> p a b", a=2)
    ptrb = [ps[:, 0:1024].bitcast(BF16).rearrange("p (a b) -> p a b", a=HC),
            ps[:, 1024:2048].bitcast(BF16).rearrange("p (a b) -> p a b", a=HC)]

    for hf in range(2):
        t0 = hf * HTOK
        ld_done = [None] * NTM
        tr_done = [None] * NTM
        evx = [None] * NTM
        pe_frontier = s_pe.v
        ev_frontier = s_ev.v
        for tm in range(NTM):
            par = tm % 2

            @b.sync
            def _(sy, tm=tm, par=par):
                if tm >= 2:
                    sy.wait_ge(s_pe.h, tr_done[tm - 2])
                else:
                    sy.wait_ge(s_pe.h, pe_frontier)
                ld_done[tm] = s_ld[par].inc(
                    sy.dma_start(out=xtile[:, par, :],
                                 in_=xln[t0 + tm * 128:t0 + (tm + 1) * 128, :]), 16)

            @b.tensor
            def _(t, tm=tm, par=par):
                t.wait_ge(s_ld[par].h, ld_done[tm])
                if tm >= 2:
                    t.wait_ge(s_ev.h, evx[tm - 2])
                else:
                    t.wait_ge(s_ev.h, ev_frontier)
                for hc in range(HC):
                    ins = t.transpose(ptrb[par][:, hc, :],
                                      xtile[:, par, hc * 128:(hc + 1) * 128],
                                      cx.identb[:, :])
                tr_done[tm] = s_pe.inc(ins, 1)

            @b.vector
            def _(v, tm=tm, par=par):
                v.wait_ge(s_pe.h, tr_done[tm])
                evx[tm] = s_ev.inc(
                    v.tensor_copy(xlf[:, :, tm * 128:(tm + 1) * 128], ptrb[par][:, :, :]), 1)

        xlf_ready = s_ev.v

        pe_base = s_pe.v
        ev_base = s_ev.v
        sc_base = s_sc.v
        pe_gu = [None] * (NE * NIM)
        pe_dn = [[None] * NTM for _ in range(NE)]
        ev_act = [None] * (NE * NIM)
        ev_stt = [[None] * NTM for _ in range(NE)]
        sc_silu = [None] * (NE * NIM)
        kk = 0
        for e in range(NE):
            for im in range(NIM):
                pe_gu[e * NIM + im] = pe_base + kk + 1
                ev_act[e * NIM + im] = ev_base + kk + 1
                kk += 1
            for tm in range(NTM):
                pe_dn[e][tm] = pe_base + kk + 1
                ev_stt[e][tm] = ev_base + kk + 1
                kk += 1
        for k in range(NE * NIM):
            sc_silu[k] = sc_base + k + 1

        w_done = [None] * (NE * NIM)
        wd_done = [None] * NE

        pe_entry = pe_base

        @b.gpsimd
        def _(g):
            for e in range(NE):
                for im in range(NIM):
                    k = e * NIM + im
                    par = k % 2
                    if k >= 2:
                        g.wait_ge(s_pe.h, pe_gu[k - 2])
                    else:
                        g.wait_ge(s_pe.h, pe_entry)
                    w_done[k] = cx.s_wg[par].inc(
                        g.dma_start(out=wg[:, par, :, :], in_=cx.gate_blkd[e * NIM + im]), 16)
                    cx.s_wu[par].inc(
                        g.dma_start(out=wu[:, par, :, :], in_=cx.up_blkd[e * NIM + im]), 16)
                if e >= 1:
                    g.wait_ge(s_pe.h, pe_dn[e - 1][NTM - 1])
                else:
                    g.wait_ge(s_st[0].h, s_st[0].v)
                    g.wait_ge(s_st[1].h, s_st[1].v)
                for ic in range(NIM):
                    wd_done[e] = cx.s_wd.inc(
                        g.dma_start(out=wd[:, ic, :], in_=cx.down_natd[e * NIM + ic]), 16)

        @b.tensor
        def _(t):
            t.wait_ge(s_ev.h, xlf_ready)
            for e in range(NE):
                for im in range(NIM):
                    k = e * NIM + im
                    par = k % 2
                    t.wait_ge(cx.s_wg[par].h, w_done[k])
                    t.wait_ge(cx.s_wu[par].h, w_done[k])
                    if k >= 2:
                        t.wait_ge(s_ev.h, ev_act[k - 2])
                    if im < 2 and e >= 1:
                        t.wait_ge(s_ev.h, ev_stt[e - 1][NTM - 2 + par])
                    for hc in range(HC):
                        t.matmul(pall[:, par, 0:512], wg[:, par, hc, :], xlf[:, hc, 0:512],
                                 start=(hc == 0), stop=(hc == HC - 1))
                        t.matmul(pall[:, par, 512:1024], wg[:, par, hc, :], xlf[:, hc, 512:1024],
                                 start=(hc == 0), stop=(hc == HC - 1))
                    for hc in range(HC):
                        t.matmul(pall[:, par, 1024:1536], wu[:, par, hc, :], xlf[:, hc, 0:512],
                                 start=(hc == 0), stop=(hc == HC - 1))
                        ins = t.matmul(pall[:, par, 1536:2048], wu[:, par, hc, :],
                                       xlf[:, hc, 512:1024],
                                       start=(hc == 0), stop=(hc == HC - 1))
                    ins.then_inc(s_pe.h, 1)
                    s_pe.v += 1
                t.wait_ge(cx.s_wd.h, wd_done[e])
                t.wait_ge(s_ev.h, ev_act[e * NIM + NIM - 1])
                for tm in range(NTM):
                    par = tm % 2
                    if tm >= 2:
                        t.wait_ge(s_ev.h, ev_stt[e][tm - 2])
                    elif e >= 1:
                        t.wait_ge(s_ev.h, ev_stt[e - 1][NTM - 2 + tm])
                    for ic in range(NIM):
                        for hb in range(NHB):
                            ins = t.matmul(pall[:, par, hb * 512:(hb + 1) * 512],
                                           act[:, ic, tm * 128:(tm + 1) * 128],
                                           wd[:, ic, hb * 512:(hb + 1) * 512],
                                           start=(ic == 0), stop=(ic == NIM - 1))
                    ins.then_inc(s_pe.h, 1)
                    s_pe.v += 1

        @b.scalar
        def _(s):
            for e in range(NE):
                for im in range(NIM):
                    k = e * NIM + im
                    par = k % 2
                    s.wait_ge(s_pe.h, pe_gu[k])
                    s.activation(act[:, im, :], pall[:, par, 0:1024], AF.Silu).then_inc(s_sc.h, 1)
                    s_sc.v += 1

        @b.vector
        def _(v):
            v.wait_ge(s_ev.h, cx.alpha_done)
            for e in range(NE):
                for im in range(NIM):
                    k = e * NIM + im
                    par = k % 2
                    v.wait_ge(s_sc.h, sc_silu[k])
                    v.tensor_tensor(act[:, im, :], act[:, im, :],
                                    pall[:, par, 1024:2048], ALU.mult).then_inc(s_ev.h, 1)
                    s_ev.v += 1
                for tm in range(NTM):
                    par = tm % 2
                    v.wait_ge(s_pe.h, pe_dn[e][tm])
                    aco = cx.alpha[:, hf * NTM + tm, e:e + 1]
                    for hb in range(NHB):
                        dstv = acc[:, tm, hb * 512:(hb + 1) * 512]
                        srcv = pall[:, par, hb * 512:(hb + 1) * 512]
                        if e == 0:
                            ins = v.tensor_scalar(dstv, srcv, aco, None, ALU.mult)
                        else:
                            ins = v.scalar_tensor_tensor(dstv, srcv, aco, dstv, ALU.mult, ALU.add)
                    ins.then_inc(s_ev.h, 1)
                    s_ev.v += 1

        assert s_pe.v == pe_base + NE * (NIM + NTM)
        assert s_ev.v == ev_base + NE * (NIM + NTM)
        assert s_sc.v == sc_base + NE * NIM

        rld = [None] * NTM
        evo = [None] * NTM
        sto = [None] * NTM
        pe_all_done = s_pe.v
        for tm in range(NTM):
            par = tm % 2

            @b.sync
            def _(sy, tm=tm, par=par):
                if tm >= 2:
                    sy.wait_ge(s_ev.h, evo[tm - 2])
                else:
                    sy.wait_ge(s_pe.h, pe_all_done)
                rld[tm] = s_ld[par].inc(
                    sy.dma_start(out=rt[:, par, :],
                                 in_=resid[t0 + tm * 128:t0 + (tm + 1) * 128, :]), 16)

            @b.vector
            def _(v, tm=tm, par=par):
                v.wait_ge(s_ld[par].h, rld[tm])
                v.wait_ge(s_ev.h, ev_stt[NE - 1][tm])
                if tm >= 2:
                    v.wait_ge(s_st[par].h, sto[tm - 2])
                evo[tm] = s_ev.inc(
                    v.tensor_tensor(ot[:, par, :], acc[:, tm, :], rt[:, par, :], ALU.add), 1)

            @b.sync
            def _(sy, tm=tm, par=par):
                sy.wait_ge(s_ev.h, evo[tm])
                sto[tm] = s_st[par].inc(
                    sy.dma_start(out=dst[t0 + tm * 128:t0 + (tm + 1) * 128, :],
                                 in_=ot[:, par, :]), 16)
    ar.pop()



def emit_attn(cx):
    b, ar, ps = cx.block, cx.ar, cx.psum
    fence(cx)
    ar.push()
    cx.negbig = ar.alloc([128, 2, CL], F32)
    cx.convw = ar.alloc([128, NFC, 4], F32)
    cx.convb = ar.alloc([128, NFC], F32)
    cx.dbcc = ar.alloc([128, CH, NH], F32)
    cx.ddTf = ar.alloc([128, 16, NH], F32)
    cx.biasTf = ar.alloc([128, 16, NH], F32)

    s_ld, s_st, s_ld2 = cx.s_ld, cx.s_st, cx.s_ld2
    s_pe, s_ev, s_sc, s_cc = cx.s_pe, cx.s_ev, cx.s_sc, cx.s_cc
    s_gx, s_ldx, s_ldw, s_stc, s_ldcg = cx.s_gx, cx.s_ldx, cx.s_ldw, cx.s_stc, cx.s_ldcg

    @b.sync
    def _(sy):
        for dst, src in [(cx.negbig[:, :, :], cx.negbigd), (cx.convw[:, :, :], cx.convw_td),
                         (cx.convb[:, :], cx.convb_td)]:
            cx.s_ldc.inc(sy.dma_start(out=dst, in_=src), 16)
    cx.attn_const = cx.s_ldc.v

    ar.push()
    dtA = ar.alloc([NH, T], F32)
    dtB = ar.alloc([NH, T], F32)
    dtC = ar.alloc([NH, T], F32)
    dtD = ar.alloc([NH, T], F32)
    dcol = ar.alloc([NH, CH], F32)
    drow = ar.alloc([1, NH * CH], F32)
    uaf = ar.alloc([128, HC, T], BF16)
    uaf_off = ar.last_off
    xt = ar.alloc([128, 2, D], F32)
    xlnb = ar.alloc([128, 2, D], BF16)
    xlnb_off = ar.last_off
    wbuf = ar.alloc([128, 2, HC, 128], BF16)
    bounce = ar.alloc([128, 2, D], BF16)
    bounce_off = ar.last_off
    ex1sb = ar.alloc([128, NFC, 3], F32)
    tailr = ar.alloc([128, NFC * 3], F32)
    cin = ar.alloc([128, 2, T + 3], BF16)
    cv = ar.alloc([128, 2, T], F32)
    cv_off = ar.last_off
    cout = ar.alloc([128, 2, T], BF16)
    cout_off = ar.last_off
    wcv = ar.alloc([128, 2, D], BF16)
    jnkB = ar.at(cout_off, [128, D], F32)
    jnkA = ar.at(cv_off, [128, D], F32)
    jnkdt = ar.at(xlnb_off, [128, D], F32)

    M, SQ, NV, RS, NM, RST, NMR = 0, 1, 2, 3, 4, 5, 6
    pslotb = [ps[:, 0:1024].bitcast(BF16).rearrange("p (a b) -> p a b", a=HC),
              ps[:, 1024:2048].bitcast(BF16).rearrange("p (a b) -> p a b", a=HC)]

    ld_done = [None] * NTT
    sc_apply = [None] * NTT
    tr_done = [None] * NTT
    evx = [None] * NTT
    ev_entry0 = s_ev.v
    for tt in range(NTT):
        par = tt % 2

        @b.sync
        def _(sy, tt=tt, par=par):
            if tt >= 2:
                sy.wait_ge(s_sc.h, sc_apply[tt - 2])
            ld_done[tt] = s_ld[par].inc(
                sy.dma_start(out=xt[:, par, :], in_=cx.x1[tt * 128:(tt + 1) * 128, :]), 16)

        @b.scalar
        def _(s, tt=tt, par=par):
            s.wait_ge(s_ld[par].h, ld_done[tt])
            if tt >= 2:
                s.wait_ge(s_pe.h, tr_done[tt - 2])
            s.activation(jnkB[:, :], xt[:, par, :], AF.Square,
                         accum_out=cx.statw[:, SQ:SQ + 1]).then_inc(s_sc.h, 1)
            s_sc.v += 1
        sc_sq = s_sc.v

        @b.vector
        def _(v, tt=tt, par=par, sc_sq=sc_sq):
            def vstep(ins):
                v.wait_ge(s_ev.h, s_ev.inc(ins, 1))

            v.wait_ge(s_ld[par].h, ld_done[tt])
            vstep(v.tensor_scalar(jnkA[:, :], xt[:, par, :], 1.0, None, ALU.mult,
                                  ALU.add, accum_out=cx.statw[:, M:M + 1]))
            v.wait_ge(s_sc.h, sc_sq)
            vstep(v.tensor_scalar(cx.statw[:, M:M + 1], cx.statw[:, M:M + 1],
                                  1.0 / D, None, ALU.mult))
            vstep(v.tensor_scalar(cx.statw[:, SQ:SQ + 1], cx.statw[:, SQ:SQ + 1],
                                  1.0 / D, None, ALU.mult))
            vstep(v.scalar_tensor_tensor(cx.statw[:, NV:NV + 1], cx.statw[:, M:M + 1],
                                         cx.statw[:, M:M + 1], cx.statw[:, SQ:SQ + 1],
                                         ALU.mult, ALU.subtract))
            vstep(v.tensor_scalar(cx.statw[:, NV:NV + 1], cx.statw[:, NV:NV + 1], -1.0, EPS,
                                  ALU.mult, ALU.add))
            vstep(v.tensor_scalar(cx.statw[:, NM:NM + 1], cx.statw[:, M:M + 1], -1.0, None,
                                  ALU.mult))
            s_ev.inc(v.reciprocal(cx.statw[:, RS:RS + 1], cx.statw[:, NV:NV + 1]), 1)
        ev_stats = s_ev.v

        @b.scalar
        def _(s, tt=tt, par=par, ev_stats=ev_stats):
            s.wait_ge(s_ev.h, ev_stats)
            s.wait_ge(s_sc.h, s_sc.inc(
                s.activation(cx.statw[:, RST:RST + 1], cx.statw[:, RS:RS + 1], AF.Sqrt), 1))
            s.wait_ge(s_sc.h, s_sc.inc(
                s.activation(cx.statw[:, NMR:NMR + 1], cx.statw[:, NM:NM + 1], AF.Identity,
                             scale=cx.statw[:, RST:RST + 1]), 1))
            sc_apply[tt] = s_sc.inc(
                s.activation(xlnb[:, par, :], xt[:, par, :], AF.Identity,
                             scale=cx.statw[:, RST:RST + 1], bias=cx.statw[:, NMR:NMR + 1]), 1)

        @b.tensor
        def _(t, tt=tt, par=par):
            t.wait_ge(s_sc.h, sc_apply[tt])
            if tt >= 2:
                t.wait_ge(s_ev.h, evx[tt - 2])
            else:
                t.wait_ge(s_ev.h, ev_entry0)
            for hc in range(HC):
                ins = t.transpose(pslotb[par][:, hc, :], xlnb[:, par, hc * 128:(hc + 1) * 128],
                                  cx.identb[:, :])
            tr_done[tt] = s_pe.inc(ins, 1)

        @b.vector
        def _(v, tt=tt, par=par):
            v.wait_ge(s_pe.h, tr_done[tt])
            evx[tt] = s_ev.inc(
                v.tensor_copy(uaf[:, :, tt * 128:(tt + 1) * 128], pslotb[par][:, :, :]), 1)

    uaf_ready = s_ev.v

    wcv_ld = [None] * NG
    wcv_st = [None] * NG
    for i in range(NG):
        par = i % 2
        wcv_st[i] = cx.s_wcv[par].v + 16 * (i // 2 + 1)

    @b.gpsimd
    def _(g):
        for i in range(NG):
            par = i % 2
            if i >= 2:
                g.wait_ge(cx.s_wcv[par].h, wcv_st[i - 2])
            else:
                g.wait_ge(s_ev.h, ev_entry0)
            wcv_ld[i] = s_ldw[par].inc(
                g.dma_start(out=wcv[:, par, :], in_=cx.wout_natd[i]), 16)

    @b.sync
    def _(sy):
        for i in range(NG):
            par = i % 2
            sy.wait_ge(s_ldw[par].h, wcv_ld[i])
            val = cx.s_wcv[par].inc(sy.dma_start(out=cx.wout_bfd[i], in_=wcv[:, par, :]), 16)
            assert val == wcv_st[i]
    wout_conv_done = [cx.s_wcv[0].v, cx.s_wcv[1].v]

    fb_order = list(range(32, NFB)) + list(range(0, 32))
    pall = ps.rearrange("p (a b) -> p a b", a=2)
    w_ld = {}
    pe_fb = {}
    ev_fb = {}
    st_fb = {}
    pe_base = s_pe.v
    ev_base = s_ev.v
    for j, fb in enumerate(fb_order):
        pe_fb[fb] = pe_base + j + 1
        ev_fb[fb] = ev_base + j + 1
    stv = [cx.s_st[0].v, cx.s_st[1].v]
    for j, fb in enumerate(fb_order):
        par = j % 2
        if fb != 80:
            stv[par] += 16
            st_fb[fb] = stv[par]
    evv = ev_base
    for j, fb in enumerate(fb_order):
        if 32 <= fb < 80:
            evv += 2
        elif fb == 80:
            evv += 1
        ev_fb[fb] = evv
    sc_fb = {}
    scv = s_sc.v
    for j, fb in enumerate(fb_order):
        if fb < 32:
            scv += 1
            sc_fb[fb] = scv

    @b.gpsimd
    def _(g):
        for j, fb in enumerate(fb_order):
            par = j % 2
            if j >= 2:
                g.wait_ge(s_pe.h, pe_fb[fb_order[j - 2]])
            else:
                g.wait_ge(s_ev.h, uaf_ready)
            w_ld[fb] = s_gx[par].inc(
                g.dma_start(out=wbuf[:, par, :, :], in_=cx.win_blkd[fb]), 16)

    @b.tensor
    def _(t):
        t.wait_ge(s_ev.h, uaf_ready)
        for j, fb in enumerate(fb_order):
            par = j % 2
            t.wait_ge(s_gx[par].h, w_ld[fb])
            if j >= 2:
                pfb = fb_order[j - 2]
                if pfb < 32:
                    t.wait_ge(s_sc.h, sc_fb[pfb])
                else:
                    t.wait_ge(s_ev.h, ev_fb[pfb])
            for hc in range(HC):
                for tq in range(4):
                    ins = t.matmul(pall[:, par, tq * 512:(tq + 1) * 512],
                                   wbuf[:, par, hc, :], uaf[:, hc, tq * 512:(tq + 1) * 512],
                                   start=(hc == 0), stop=(hc == HC - 1))
            val = s_pe.inc(ins, 1)
            assert val == pe_fb[fb]

    @b.vector
    def _(v):
        for j, fb in enumerate(fb_order):
            if fb < 32:
                continue
            par = j % 2
            v.wait_ge(s_pe.h, pe_fb[fb])
            jj = j - 2
            while jj >= 0 and fb_order[jj] == 80:
                jj -= 2
            if jj >= 0:
                v.wait_ge(s_st[par].h, st_fb[fb_order[jj]])
            if fb == 80:
                v.tensor_copy(dtA[:, :], pall[0:NH, par, :]).then_inc(s_ev.h, 1)
                s_ev.v += 1
            else:
                cpv = s_ev.inc(v.tensor_copy(bounce[:, par, :], pall[:, par, :]), 1)
                if SIM_SAFE:
                    v.wait_ge(s_ev.h, cpv)
                v.tensor_scalar(ex1sb[:, fb - 32, :], bounce[:, par, 2045:2048],
                                cx.flags[:, 0:1], None, ALU.mult).then_inc(s_ev.h, 1)
                s_ev.v += 1
            assert s_ev.v == ev_fb[fb]

    @b.scalar
    def _(sc):
        for j, fb in enumerate(fb_order):
            if fb >= 32:
                continue
            par = j % 2
            sc.wait_ge(s_pe.h, pe_fb[fb])
            jj = j - 2
            while jj >= 0 and fb_order[jj] == 80:
                jj -= 2
            if jj >= 0:
                sc.wait_ge(s_st[par].h, st_fb[fb_order[jj]])
            sc.activation(bounce[:, par, :], pall[:, par, :], AF.Silu).then_inc(s_sc.h, 1)
            s_sc.v += 1
            assert s_sc.v == sc_fb[fb]
    dt_ready = ev_fb[80]

    @b.sync
    def _(sy):
        for j, fb in enumerate(fb_order):
            if fb == 80:
                continue
            par = j % 2
            if 32 <= fb < 80:
                sy.wait_ge(s_ev.h, ev_fb[fb])
                val = s_st[par].inc(sy.dma_start(out=cx.xbc_raw[fb - 32], in_=bounce[:, par, :]), 16)
            else:
                sy.wait_ge(s_sc.h, sc_fb[fb])
                val = s_st[par].inc(sy.dma_start(out=cx.z_raw[fb], in_=bounce[:, par, :]), 16)
            assert val == st_fb[fb]

    xbc_export_ev = ev_fb[79]
    inproj_st_frontier = [cx.s_st[0].v, cx.s_st[1].v]

    @b.gpsimd
    def _(g):
        g.wait_ge(s_ev.h, xbc_export_ev)
        g.dma_start(out=cx.ex1_in, in_=ex1sb.rearrange("p a b -> p (a b)")).then_inc(cx.s_po.h, 16)
        cx.s_po.v += 16
        g.wait_ge(cx.s_po.h, cx.s_po.v)
        g.collective_compute(
            "AllReduce", ALU.add,
            replica_groups=RGROUPS,
            ins=[cx.ex1_in.opt()], outs=[cx.ex1_out.opt()],
        ).then_inc(s_cc.h, 1)
        s_cc.v += 1
        g.wait_ge(s_cc.h, s_cc.v)
        g.dma_start(out=tailr[:, :], in_=cx.ex1_out).then_inc(cx.s_po.h, 16)
        cx.s_po.v += 16
    tail_ld = cx.s_po.v

    tail_ev = [None]

    @b.vector
    def _(v):
        v.wait_ge(cx.s_po.h, tail_ld)
        v.wait_ge(cx.s_ldc.h, cx.attn_const)
        tail_ev[0] = s_ev.inc(
            v.tensor_scalar(tailr[:, :], tailr[:, :], cx.flags[:, 1:2], None, ALU.mult), 1)

    dt_prep2(cx, dtA, dtB, dtC, dtD, dcol, drow, jnkdt, dt_ready)

    tailr3 = tailr.rearrange("p (f k) -> p f k", f=NFC)
    cv_ev = {}
    cv_st = {}
    for fc in range(NFC):
        par = fc % 2

        @b.gpsimd
        def _(gq, fc=fc, par=par):
            if fc >= 2:
                gq.wait_ge(s_ev.h, cv_ev[fc - 2])
                sem = s_st if fc - 2 < 16 else s_stc
                gq.wait_ge(sem[par].h, cv_st[fc - 2])
            gq.wait_ge(s_st[0].h, inproj_st_frontier[0])
            gq.wait_ge(s_st[1].h, inproj_st_frontier[1])
            cx.s_cin[par].inc(gq.dma_start(out=cin[:, par, 3:3 + T], in_=cx.xbc_raw[fc]), 16)
        cin_ld = cx.s_cin[par].v

        @b.vector
        def _(v, fc=fc, par=par, cin_ld=cin_ld):
            v.wait_ge(cx.s_cin[par].h, cin_ld)
            v.wait_ge(s_ev.h, tail_ev[0])
            ssync(v, s_ev, v.tensor_copy(cin[:, par, 0:3], tailr3[:, fc, :]))
            ssync(v, s_ev, v.tensor_scalar(cv[:, par, :], cin[:, par, 0:T], cx.convw[:, fc, 0:1],
                                           cx.convb[:, fc:fc + 1], ALU.mult, ALU.add))
            for k in range(1, 4):
                ins = v.scalar_tensor_tensor(cv[:, par, :], cin[:, par, k:k + T],
                                             cx.convw[:, fc, k:k + 1], cv[:, par, :],
                                             ALU.mult, ALU.add)
                if k < 3:
                    ssync(v, s_ev, ins)
            cv_ev[fc] = s_ev.inc(ins, 1)

        @b.scalar
        def _(s, fc=fc, par=par):
            s.wait_ge(s_ev.h, cv_ev[fc])
            s.activation(cout[:, par, :], cv[:, par, :], AF.Silu).then_inc(s_sc.h, 1)
            s_sc.v += 1
        sil_done = s_sc.v

        @b.sync
        def _(sy, fc=fc, par=par, sil_done=sil_done):
            sy.wait_ge(s_sc.h, sil_done)
            sem = s_st if fc < 16 else s_stc
            cv_st[fc] = sem[par].inc(
                sy.dma_start(out=cx.xbc_conv[fc], in_=cout[:, par, :]), 16)

    xb_conv_st = [s_st[0].v, s_st[1].v]
    conv_all_st = [s_stc[0].v, s_stc[1].v]
    conv_done_ev = s_ev.v


    emit_xbt(cx, uaf_off, bounce_off, xb_conv_st)

    ar.pop()

    ar.push()
    bufs = {}
    bufs["carry"] = ar.alloc([128, DIN], F32)
    bufs["carry_bf"] = ar.alloc([128, DIN], BF16)
    bufs["states"] = ar.alloc([128, CH, DIN], BF16)
    states_off = ar.last_off
    bufs["xbt"] = ar.alloc([128, 1, 2, T], BF16)
    bufs["abc"] = ar.alloc([128, 4, 2, CL], F32)
    bufs["bck"] = ar.alloc([128, 2, 8, CL], BF16)
    bufs["xfm"] = ar.alloc([128, 2, 8, CL], BF16)
    bufs["xdd"] = ar.alloc([128, 4, 2, 64], BF16)
    bufs["segb"] = ar.alloc([128, 2, 2, 2, CL], F32)
    segb_off = ar.last_off
    bufs["Lb"] = ar.alloc([128, 2, 2, 2, CL], BF16)
    bufs["glb"] = ar.alloc([128, 2, 2, 2, CL], BF16)
    bufs["ebcb"] = ar.alloc([128, 2, 2, CL], BF16)
    bufs["csb"] = ar.alloc([128, 2, 2, CL], BF16)
    bufs["cgb"] = ar.alloc([128, 4, CL], BF16)
    bufs["zgb"] = ar.alloc([128, 2, CL], BF16)
    bufs["ygc"] = ar.alloc([128, 2, CL], F32)
    bufs["y2"] = ar.alloc([128, 2, CL], F32)
    bufs["rstd"] = ar.alloc([128, 2, CL], F32)
    bufs["ynormF"] = ar.alloc([128, NG, 2 * CL], BF16)
    ynorm_off = ar.last_off
    bufs["exc"] = ar.at(ynorm_off, [128, DIN], F32)
    bufs["x1t"] = ar.at(states_off, [128, 2, D], F32)
    bufs["woutb"] = ar.at(segb_off, [128, 2, D], BF16)
    bufs["xb_conv_st"] = xb_conv_st
    bufs["conv_all_st"] = conv_all_st
    bufs["wout_conv_done"] = wout_conv_done

    emit_ssd2(cx, bufs)
    ar.pop()
    ar.pop()



def emit_xbt(cx, xbfm_off, stage_off, conv_xb_st):
    b, ar, ps = cx.block, cx.ar, cx.psum
    s_pe, s_ev = cx.s_pe, cx.s_ev
    xbfm = ar.at(xbfm_off, [128, 16, T], BF16)
    stage = ar.at(stage_off, [128, 2, 16, 128], BF16)
    ld = [None] * 16
    trd = {}
    evd = {}
    std = [None] * 16
    ptr = [ps[:, 0:512].bitcast(BF16).rearrange("p (a b) -> p a b", a=8),
           ps[:, 512:1024].bitcast(BF16).rearrange("p (a b) -> p a b", a=8)]

    @b.sync
    def _(sy):
        sy.wait_ge(cx.s_st[0].h, conv_xb_st[0])
        sy.wait_ge(cx.s_st[1].h, conv_xb_st[1])
        for fc in range(16):
            ld[fc] = cx.s_ldx[fc % 2].inc(
                sy.dma_start(out=xbfm[:, fc, :], in_=cx.xbc_conv[fc]), 16)
    ld_all = [cx.s_ldx[0].v, cx.s_ldx[1].v]

    for tc in range(16):
        par = tc % 2
        for half in range(2):
            bk = (2 * tc + half) % 2

            @b.tensor
            def _(t, tc=tc, half=half, bk=bk):
                if tc == 0 and half == 0:
                    t.wait_ge(cx.s_ldx[0].h, ld_all[0])
                    t.wait_ge(cx.s_ldx[1].h, ld_all[1])
                key = (tc, half - 1) if half == 1 else (tc - 1, 1)
                if key in evd:
                    t.wait_ge(s_ev.h, evd[key])
                for i in range(8):
                    fc = half * 8 + i
                    ins = t.transpose(ptr[bk][:, i, :], xbfm[:, fc, tc * 128:(tc + 1) * 128],
                                      cx.identb[:, :])
                trd[(tc, half)] = s_pe.inc(ins, 1)

            @b.vector
            def _(v, tc=tc, half=half, bk=bk, par=par):
                v.wait_ge(s_pe.h, trd[(tc, half)])
                if tc >= 2:
                    v.wait_ge(cx.s_st[par].h, std[tc - 2])
                evd[(tc, half)] = s_ev.inc(
                    v.tensor_copy(stage[:, par, half * 8:(half + 1) * 8, :],
                                  ptr[bk][:, :, :]), 1)

        @b.sync
        def _(sy, tc=tc, par=par):
            sy.wait_ge(s_ev.h, evd[(tc, 1)])
            std[tc] = cx.s_st[par].inc(
                sy.dma_start(out=cx.xbt_dram[tc],
                             in_=stage[:, par, :, :].rearrange("p a b -> p (a b)")), 16)
    cx.xbt_st = [cx.s_st[0].v, cx.s_st[1].v]


def dt_prep2(cx, dtA, dtB, dtC, dtD, dcol, drow, jnk, dt_ready):
    b, ps = cx.block, cx.psum
    s_ev, s_sc, s_pe = cx.s_ev, cx.s_sc, cx.s_pe
    jnk64 = jnk[0:NH, :]
    ev0 = [None] * 8

    @b.vector
    def _(v):
        def vstep(ins):
            v.wait_ge(s_ev.h, s_ev.inc(ins, 1))

        v.wait_ge(s_ev.h, dt_ready)
        vstep(v.tensor_scalar(dtA[:, :], dtA[:, :], cx.dtbc[:, :], None, ALU.add))
        ev0[0] = s_ev.inc(v.tensor_scalar(dtB[:, :], dtA[:, :], 0.0, None, ALU.max), 1)

    @b.scalar
    def _(s):
        s.wait_ge(s_ev.h, ev0[0])
        s.wait_ge(s_sc.h, s_sc.inc(s.activation(jnk64[:, :], dtA[:, :], AF.Abs), 1))
        s.wait_ge(s_sc.h, s_sc.inc(s.activation(jnk64[:, :], jnk64[:, :], AF.Exp, scale=-1.0), 1))
        s.wait_ge(s_sc.h, s_sc.inc(s.activation(jnk64[:, :], jnk64[:, :], AF.Ln, bias=1.0), 1))
        ev0[1] = s_sc.inc(s.activation(cx.acol[:, :], cx.alogc[:, :], AF.Exp), 1)

    @b.vector
    def _(v):
        def vstep(ins):
            v.wait_ge(s_ev.h, s_ev.inc(ins, 1))

        v.wait_ge(s_sc.h, ev0[1])
        vstep(v.tensor_tensor(dtA[:, :], dtB[:, :], jnk64[:, :], ALU.add))
        vstep(v.tensor_scalar(dtA[:, :], dtA[:, :], 1e-30, None, ALU.max))
        ev0[2] = s_ev.inc(v.tensor_scalar(cx.acol[:, :], cx.acol[:, :], -1.0, None, ALU.mult), 1)

    @b.scalar
    def _(s):
        s.wait_ge(s_ev.h, ev0[2])
        ev0[3] = s_sc.inc(s.activation(dtD[:, :], dtA[:, :], AF.Ln), 1)

    @b.vector
    def _(v):
        def vstep(ins):
            v.wait_ge(s_ev.h, s_ev.inc(ins, 1))

        v.wait_ge(s_sc.h, ev0[3])
        vstep(v.tensor_scalar(dtC[:, :], dtA[:, :], cx.acol[:, :], None, ALU.mult))
        src, dst = dtC, jnk64
        sh = 1
        while sh < CL:
            sv = src.rearrange("p (c t) -> p c t", t=CL)
            dv = dst.rearrange("p (c t) -> p c t", t=CL)
            ssync(v, s_ev, v.tensor_copy(dv[:, :, 0:sh], sv[:, :, 0:sh]))
            ssync(v, s_ev, v.tensor_tensor(dv[:, :, sh:CL], sv[:, :, sh:CL],
                                           sv[:, :, 0:CL - sh], ALU.add))
            src, dst = dst, src
            sh *= 2
        assert src is dtC
        ev0[4] = s_ev.inc(v.tensor_tensor(dtD[:, :], dtD[:, :], dtC[:, :], ALU.subtract), 1)

    @b.scalar
    def _(s):
        s.wait_ge(s_ev.h, ev0[4])
        c3 = dtC.rearrange("p (c t) -> p c t", t=CL)
        b3 = dtB.rearrange("p (c t) -> p c t", t=CL)
        for c in range(CH):
            ins = s.activation(b3[:, c, :], c3[:, c, :], AF.Exp, scale=-1.0,
                               bias=c3[:, c, CL - 1:CL])
        s.wait_ge(s_sc.h, s_sc.inc(ins, 1))
        dc3 = dcol.unsqueeze(2)
        ev0[5] = s_sc.inc(s.activation(dc3[:, :, :], c3[:, :, CL - 1:CL], AF.Exp), 1)

    @b.vector
    def _(v):
        v.wait_ge(s_sc.h, ev0[5])
        ev0[6] = s_ev.inc(v.tensor_tensor(dtB[:, :], dtA[:, :], dtB[:, :], ALU.mult), 1)


    ddv = cx.ddTf
    bsv = cx.biasTf
    trv = {}
    evv = {}
    for i, (srcT, dstT, rdy, sname) in enumerate(
            [(dtB, ddv, ev0[6], "dd"), (dtD, bsv, ev0[4], "bs")]):
        for tc in range(16):
            sl = tc % 2

            @b.tensor
            def _(t, i=i, tc=tc, sl=sl, srcT=srcT, rdy=rdy):
                t.wait_ge(s_ev.h, rdy)
                if (i, tc - 2) in evv:
                    t.wait_ge(s_ev.h, evv[(i, tc - 2)])
                elif i == 1 and tc < 2:
                    t.wait_ge(s_ev.h, evv[(0, 14 + tc)])
                poff = 3072 if sl == 0 else 3584
                trv[(i, tc)] = s_pe.inc(
                    t.transpose(ps[:, poff:poff + 64],
                                srcT[:, tc * 128:(tc + 1) * 128], cx.ident32[0:NH, 0:NH]), 1)

            @b.vector
            def _(v, i=i, tc=tc, sl=sl, dstT=dstT):
                v.wait_ge(s_pe.h, trv[(i, tc)])
                poff = 3072 if sl == 0 else 3584
                evv[(i, tc)] = s_ev.inc(
                    v.tensor_copy(dstT[:, tc, :], ps[:, poff:poff + 64]), 1)

    @b.tensor
    def _(t):
        t.wait_ge(s_sc.h, ev0[5])
        t.wait_ge(s_ev.h, evv[(1, 15)])
        ev0[7] = s_pe.inc(t.transpose(ps[:, 3584:3648][0:CH, :], dcol[:, :],
                                      cx.ident32[0:NH, 0:NH]), 1)

    dtr_ev = [None]

    @b.vector
    def _(v):
        v.wait_ge(s_pe.h, ev0[7])
        dtr_ev[0] = s_ev.inc(v.tensor_copy(jnk[0:CH, 0:NH], ps[:, 3584:3648][0:CH, :]), 1)

    dall_st = [None]
    dall_ld = [None]
    acs_st = [None]
    expa_sb = jnk.bitcast(BF16)[0:NH, 0:T]
    expa_ev = [None]

    @b.sync
    def _(sy):
        sy.wait_ge(s_ev.h, dtr_ev[0])
        dall_st[0] = cx.s_dt.inc(sy.dma_start(out=cx.dall_dram, in_=jnk[0:CH, 0:NH]), 16)
        sy.wait_ge(cx.s_dt.h, dall_st[0])
        dall_ld[0] = cx.s_dt.inc(sy.dma_start(out=drow[:, :], in_=cx.dall_dram), 16)
        sy.wait_ge(cx.s_dt.h, dall_ld[0])
        sy.wait_ge(s_ev.h, ev0[4])
        cx.s_dt.inc(sy.dma_start(out=cx.acs_dram, in_=dtC[:, :]), 16)

    dbv = cx.dbcc
    cx._dbc_ev = {}
    for c in range(CH):
        @b.tensor
        def _(t, c=c):
            if c == 0:
                t.wait_ge(cx.s_dt.h, dall_ld[0])
            else:
                t.wait_ge(s_ev.h, cx._dbc_ev[c - 1])
            cx._dbc_pe = s_pe.inc(
                t.matmul(ps[:, 3648:3712], cx.ones_row[:, :],
                         drow[0:1, c * NH:(c + 1) * NH], start=True, stop=True), 1)

        @b.vector
        def _(v, c=c):
            v.wait_ge(s_pe.h, cx._dbc_pe)
            cx._dbc_ev[c] = s_ev.inc(v.tensor_copy(dbv[:, c, :], ps[:, 3648:3712]), 1)
    @b.scalar
    def _(s):
        s.wait_ge(s_ev.h, ev0[4])
        s.wait_ge(cx.s_dt.h, dall_ld[0])
        expa_ev[0] = s_sc.inc(s.activation(expa_sb, dtC[:, :], AF.Exp), 1)

    @b.sync
    def _(sy):
        sy.wait_ge(s_sc.h, expa_ev[0])
        acs_st[0] = cx.s_dt.inc(sy.dma_start(out=cx.expacs_dram, in_=expa_sb), 16)

    cx.dt_done_ev = cx._dbc_ev[CH - 1]
    cx.acs_st = acs_st[0]


def emit_ssd2(cx, bf):
    b, ps = cx.block, cx.psum
    s_pe, s_ev, s_sc, s_cc = cx.s_pe, cx.s_ev, cx.s_sc, cx.s_cc
    s_ld, s_st, s_ld2 = cx.s_ld, cx.s_st, cx.s_ld2
    s_ldx, s_ldw, s_ldcg = cx.s_ldx, cx.s_ldw, cx.s_ldcg

    carry, carry_bf, states = bf["carry"], bf["carry_bf"], bf["states"]
    xbt, bck, xfm = bf["xbt"], bf["bck"], bf["xfm"]
    abc = bf["abc"]
    xdd = bf["xdd"]
    segb, Lb, glb = bf["segb"], bf["Lb"], bf["glb"]
    ebcb, csb, cgb = bf["ebcb"], bf["csb"], bf["cgb"]
    zgb, ygc, y2 = bf["zgb"], bf["ygc"], bf["y2"]
    y2q = ygc
    rstd = bf["rstd"]
    ynormF = bf["ynormF"]
    exc, x1t, woutb = bf["exc"], bf["x1t"], bf["woutb"]
    x2t = x1t

    xbt_ld = [None] * CH
    p1_mm = {}
    p1_xdd = {}
    p1_cu = {}
    st_banks = [ps[:, 0:64], ps[:, 512:576], ps[:, 1024:1088], ps[:, 1536:1600]]

    for c in range(CH):
        cpar = c % 2

        @b.sync
        def _(sy, c=c, cpar=cpar):
            if c >= 1:
                sy.wait_ge(s_pe.h, p1_mm[(c - 1, NH - 1)])
            else:
                sy.wait_ge(s_st[0].h, cx.xbt_st[0])
                sy.wait_ge(s_st[1].h, cx.xbt_st[1])
            xbt_ld[c] = s_ldx[cpar].inc(
                sy.dma_start(out=xbt[:, 0, :, :],
                             in_=cx.xbt_dram[2 * c:2 * c + 2].rearrange("a p f -> p a f")), 16)

        for h in range(NH):
            hp = h % 4
            xc0 = (h // 8) * 128 + (h % 2) * 64

            @b.vector
            def _(v, c=c, h=h, hp=hp, xc0=xc0, cpar=cpar):
                v.wait_ge(s_ldx[cpar].h, xbt_ld[c])
                if (c, h - 4) in p1_mm:
                    v.wait_ge(s_pe.h, p1_mm[(c, h - 4)])
                elif c >= 1 and h < 4:
                    v.wait_ge(s_pe.h, p1_mm[(c - 1, NH - 4 + h)])
                if c == 0 and h < 4:
                    v.wait_ge(s_ev.h, cx.dt_done_ev)
                    v.wait_ge(cx.s_dt.h, cx.acs_st)
                for st in range(2):
                    ins = v.tensor_scalar(xdd[:, hp, st, :], xbt[:, 0, st, xc0:xc0 + 64],
                                          cx.ddTf[:, 2 * c + st, h:h + 1], None, ALU.mult)
                p1_xdd[(c, h)] = s_ev.inc(ins, 1)

            @b.tensor
            def _(t, c=c, h=h, hp=hp, cpar=cpar):
                t.wait_ge(s_ev.h, p1_xdd[(c, h)])
                if (c, h - 4) in p1_cu:
                    t.wait_ge(s_ev.h, p1_cu[(c, h - 4)])
                elif c >= 1 and h < 4:
                    t.wait_ge(s_ev.h, p1_cu[(c - 1, NH - 4 + h)])
                bc0 = 1024 + (h // 8) * 128
                for st in range(2):
                    ins = t.matmul(st_banks[hp], xbt[:, 0, st, bc0:bc0 + 128],
                                   xdd[:, hp, st, :], start=(st == 0), stop=(st == 1))
                p1_mm[(c, h)] = s_pe.inc(ins, 1)

            @b.vector
            def _(v, c=c, h=h, hp=hp):
                v.wait_ge(s_pe.h, p1_mm[(c, h)])
                v.tensor_copy(states[:, c, h * 64:(h + 1) * 64], st_banks[hp])
                if c == 0:
                    ins = v.tensor_copy(carry[:, h * 64:(h + 1) * 64], st_banks[hp])
                else:
                    ins = v.scalar_tensor_tensor(
                        carry[:, h * 64:(h + 1) * 64], carry[:, h * 64:(h + 1) * 64],
                        cx.dbcc[:, c, h:h + 1], st_banks[hp], ALU.mult, ALU.add)
                p1_cu[(c, h)] = s_ev.inc(ins, 1)

    exp_ev = [None]
    cin_ev = [None]

    @b.vector
    def _(v):
        v.wait_ge(s_ev.h, p1_cu[(CH - 1, NH - 1)])
        v.wait_ge(cx.s_stc[0].h, bf["conv_all_st"][0])
        v.wait_ge(cx.s_stc[1].h, bf["conv_all_st"][1])
        v.wait_ge(cx.s_wcv[0].h, bf["wout_conv_done"][0])
        v.wait_ge(cx.s_wcv[1].h, bf["wout_conv_done"][1])
        exp_ev[0] = s_ev.inc(
            v.tensor_scalar(exc[:, :], carry[:, :], cx.flags[:, 0:1], None, ALU.mult), 1)

    @b.gpsimd
    def _(g):
        g.wait_ge(s_ev.h, exp_ev[0])
        g.dma_start(out=cx.ex2_in, in_=exc[:, :]).then_inc(cx.s_po.h, 16)
        cx.s_po.v += 16
        g.wait_ge(cx.s_po.h, cx.s_po.v)
        g.collective_compute(
            "AllReduce", ALU.add,
            replica_groups=RGROUPS,
            ins=[cx.ex2_in.opt()], outs=[cx.ex2_out.opt()],
        ).then_inc(s_cc.h, 1)
        s_cc.v += 1
        g.wait_ge(s_cc.h, s_cc.v)
        g.dma_start(out=exc[:, :], in_=cx.ex2_out).then_inc(cx.s_po.h, 16)
        cx.s_po.v += 16
    exc_ld = cx.s_po.v

    @b.vector
    def _(v):
        v.wait_ge(cx.s_po.h, exc_ld)
        v.wait_ge(s_ev.h, s_ev.inc(
            v.tensor_scalar(carry[:, :], exc[:, :], cx.flags[:, 1:2], None, ALU.mult), 1))
        cin_ev[0] = s_ev.inc(v.tensor_copy(carry_bf[:, :], carry[:, :]), 1)

    p2_xbt = [None] * CH
    p2_G = {}
    p2_seg = {}
    p2_gl2 = {}
    cx._abc_ld = {}
    cx._ea_ld = {}
    p2_L = {}
    p2_gl = {}
    p2_Y = {}
    p2_yev = {}
    p2_sil = {}
    p2_y2 = {}
    p2_sq = {}
    p2_ms = {}
    p2_rr = {}
    p2_rb = {}
    p2_yn = {}
    p2_cg = {}
    op_ev = {}
    op_st = {}
    segb, Lb, glb = bf["segb"], bf["Lb"], bf["glb"]
    ebcb, csb, cgb = bf["ebcb"], bf["csb"], bf["cgb"]
    expab = ebcb
    Gb = [ps[:, i * 512:(i + 1) * 512].rearrange("p (a b) -> p a b", a=2) for i in range(4)]
    Yb = [ps[:, 2048:2304][0:64, :], ps[:, 2560:2816][0:64, :]]

    for c in range(CH):
        cpar = c % 2

        @b.sync
        def _(sy, c=c, cpar=cpar):
            if c >= 1:
                sy.wait_ge(s_ev.h, p2_yev[(c - 1, NH - 1)])
            else:
                sy.wait_ge(s_ev.h, cin_ev[0])
            sy.dma_start(out=xbt[:, 0, :, :],
                         in_=cx.xbt_dram[2 * c:2 * c + 2].rearrange("a p f -> p a f")
                         ).then_inc(s_ldx[cpar].h, 16)
            s_ldx[cpar].v += 16
            sy.dma_start(out=bck[:, cpar, :, :],
                         in_=cx.xbc_conv[8:16, :, c * CL:(c + 1) * CL].rearrange(
                             "f p t -> p f t")).then_inc(s_ldx[cpar].h, 16)
            s_ldx[cpar].v += 16
            p2_xbt[c] = s_ldx[cpar].inc(
                sy.dma_start(out=xfm[:, cpar, :, :],
                             in_=cx.xbc_conv[0:8, :, c * CL:(c + 1) * CL].rearrange(
                                 "f p t -> p f t")), 16)

        for g in range(NG):
            gp4 = g % 4
            gp = g % 2
            pp = g % 2

            ab4 = g % 4

            @b.sync
            def _(sy, c=c, g=g, ab4=ab4):
                prev = (c, g - 4) if g >= 4 else ((c - 1, NG - 4 + g) if c >= 1 else None)
                if prev is not None:
                    sy.wait_ge(s_ev.h, p2_seg[prev])
                else:
                    sy.wait_ge(cx.s_dt.h, cx.acs_st)
                    sy.wait_ge(s_ev.h, cin_ev[0])
                p2_abc = cx.s_abc[ab4].inc(
                    sy.dma_start(out=abc[:, ab4, :, :],
                                 in_=cx.acs_dram[2 * g:2 * g + 2, c * CL:(c + 1) * CL]
                                 .unsqueeze(0).broadcast_to([128, 2, CL])), 16)
                if not hasattr(cx, "_abc_ld"):
                    cx._abc_ld = {}
                cx._abc_ld[(c, g)] = p2_abc

            @b.scalar
            def _(se, c=c, g=g, pp=pp):
                if (c, g - 2) in p2_gl2:
                    se.wait_ge(s_ev.h, p2_gl2[(c, g - 2)])
                elif c >= 1 and g < 2:
                    se.wait_ge(s_ev.h, p2_gl2[(c - 1, NG - 2 + g)])
                else:
                    se.wait_ge(cx.s_dt.h, cx.acs_st)
                    se.wait_ge(s_ev.h, cin_ev[0])
                p2_ea = cx.s_lda[pp].inc(
                    se.dma_start(out=expab[:, pp, :, :],
                                 in_=cx.expacs_dram[2 * g:2 * g + 2, c * CL:(c + 1) * CL]
                                 .unsqueeze(0).broadcast_to([128, 2, CL])), 16)
                cx._ea_ld[(c, g)] = p2_ea

            @b.vector
            def _(v, c=c, g=g, pp=pp, ab4=ab4):
                v.wait_ge(cx.s_abc[ab4].h, cx._abc_ld[(c, g)])
                if (c, g - 2) in p2_L:
                    v.wait_ge(s_sc.h, p2_L[(c, g - 2)])
                elif c >= 1 and g < 2:
                    v.wait_ge(s_sc.h, p2_L[(c - 1, NG - 2 + g)])
                elif c == 0 and g < 2:
                    v.wait_ge(cx.s_ldc.h, cx.attn_const)
                    v.wait_ge(s_ev.h, cin_ev[0])
                p2_seg[(c, g)] = s_ev.inc(v.tensor_tensor(
                    segb[:, pp, :, :, :],
                    abc[:, ab4, :, :].unsqueeze(2).broadcast_to([128, 2, 2, CL]),
                    cx.negbig.unsqueeze(1).broadcast_to([128, 2, 2, CL]), ALU.add), 1)

            @b.sync
            def _(sy, c=c, g=g, gp=gp):
                if (c, 2 * (g - 4) + 1) in p2_gl:
                    sy.wait_ge(s_ev.h, p2_gl[(c, 2 * (g - 4) + 1)])
                elif c >= 1 and g < 4:
                    sy.wait_ge(s_ev.h, p2_gl[(c - 1, 2 * (NG - 4 + g) + 1)])
                else:
                    sy.wait_ge(cx.s_stc[0].h, bf["conv_all_st"][0])
                    sy.wait_ge(cx.s_stc[1].h, bf["conv_all_st"][1])
                p2_cg[(c, g)] = s_ldcg[g % 4].inc(
                    sy.dma_start(out=cgb[:, g % 4, :],
                                 in_=cx.xbc_conv[16 + g, :, c * CL:(c + 1) * CL]), 16)

            @b.tensor
            def _(t, c=c, g=g, gp4=gp4, cpar=cpar):
                if c >= 2 and c % 2 == 0 and g == 0:
                    t.wait_ge(s_ev.h, op_ev[(c - 2) // 2])
                t.wait_ge(s_ldcg[g % 4].h, p2_cg[(c, g)])
                t.wait_ge(s_ldx[cpar].h, p2_xbt[c])
                if (c, 2 * (g - 4) + 1) in p2_gl:
                    t.wait_ge(s_ev.h, p2_gl[(c, 2 * (g - 4) + 1)])
                elif c >= 1 and g < 4:
                    t.wait_ge(s_ev.h, p2_gl[(c - 1, 2 * (NG - 4 + g) + 1)])
                for st in range(2):
                    ins = t.matmul(Gb[gp4][:, st, :],
                                   bck[:, cpar, g // 4, st * 128:(st + 1) * 128],
                                   cgb[:, g % 4, :], start=True, stop=True)
                p2_G[(c, g)] = s_pe.inc(ins, 1)

            @b.scalar
            def _(s, c=c, g=g, pp=pp):
                s.wait_ge(s_ev.h, p2_seg[(c, g)])
                s.wait_ge(cx.s_lda[pp].h, cx._ea_ld[(c, g)])
                if c == 0 and g == 0:
                    s.wait_ge(s_ev.h, cx.dt_done_ev)
                for hh in range(2):
                    hgl = 2 * g + hh
                    for st in range(2):
                        ins = s.activation(Lb[:, pp, hh, st, :], segb[:, pp, hh, st, :], AF.Exp,
                                           bias=cx.biasTf[:, 2 * c + st, hgl:hgl + 1])
                p2_L[(c, g)] = s_sc.inc(ins, 1)

            @b.vector
            def _(v, c=c, g=g, pp=pp, gp4=gp4):
                v.wait_ge(s_sc.h, p2_L[(c, g)])
                v.wait_ge(s_pe.h, p2_G[(c, g)])
                v.tensor_tensor(csb[:, pp, :, :],
                                cgb[:, g % 4, :].unsqueeze(1).broadcast_to([128, 2, CL]),
                                expab[:, pp, :, :], ALU.mult)
                p2_gl[(c, 2 * g + 1)] = s_ev.inc(v.tensor_tensor(
                    glb[:, pp, :, :, :],
                    Gb[gp4].unsqueeze(1).broadcast_to([128, 2, 2, CL]),
                    Lb[:, pp, :, :, :], ALU.mult), 1)
                p2_gl2[(c, g)] = p2_gl[(c, 2 * g + 1)]

            for hh in range(2):
                h = 2 * g + hh
                xc0 = (h // 8) * 128 + (h % 2) * 64

                @b.tensor
                def _(t, c=c, g=g, h=h, hh=hh, pp=pp, cpar=cpar, xc0=xc0):
                    t.wait_ge(s_ev.h, p2_gl[(c, 2 * g + 1)])
                    if (c, h - 2) in p2_yev:
                        t.wait_ge(s_ev.h, p2_yev[(c, h - 2)])
                    elif c >= 1 and h < 2:
                        t.wait_ge(s_ev.h, p2_yev[(c - 1, NH - 2 + h)])
                    t.matmul(Yb[hh], xbt[:, 0, 0, xc0:xc0 + 64], glb[:, pp, hh, 0, :],
                             start=True, stop=False)
                    t.matmul(Yb[hh], xbt[:, 0, 1, xc0:xc0 + 64], glb[:, pp, hh, 1, :],
                             start=False, stop=False)
                    p2_Y[(c, h)] = s_pe.inc(
                        t.matmul(Yb[hh], carry_bf[:, h * 64:(h + 1) * 64], csb[:, pp, hh, :],
                                 start=False, stop=True), 1)

                @b.vector
                def _(v, c=c, g=g, h=h, hh=hh, gp=gp, cpar=cpar):
                    v.wait_ge(s_pe.h, p2_Y[(c, h)])
                    godd = g - 1 if g % 2 == 0 else g - 2
                    if (c, godd) in p2_rr:
                        v.wait_ge(s_sc.h, p2_rr[(c, godd)])
                    elif c >= 1 and godd < 0:
                        v.wait_ge(s_sc.h, p2_rr[(c - 1, NG - 1)])
                    po = (h % 2) * 64
                    v.scalar_tensor_tensor(ygc[hh * 64:hh * 64 + 64, gp, :],
                                           xfm[po:po + 64, cpar, h // 8, :],
                                           cx.d_rep[po:po + 64, h:h + 1],
                                           Yb[hh], ALU.mult, ALU.add)
                    ssync(v, s_ev, v.scalar_tensor_tensor(
                        carry[:, h * 64:(h + 1) * 64], carry[:, h * 64:(h + 1) * 64],
                        cx.dbcc[:, c, h:h + 1], states[:, c, h * 64:(h + 1) * 64],
                        ALU.mult, ALU.add))
                    p2_yev[(c, h)] = s_ev.inc(
                        v.tensor_copy(carry_bf[:, h * 64:(h + 1) * 64],
                                      carry[:, h * 64:(h + 1) * 64]), 1)

            @b.scalar
            def _(s, c=c, g=g, gp=gp):
                if (c, g - 2) in p2_sil:
                    s.wait_ge(s_ev.h, p2_y2[(c, g - 2)])
                elif c >= 1 and g < 2:
                    s.wait_ge(s_ev.h, p2_y2[(c - 1, NG - 2 + g)])
                p2_sil[(c, g)] = cx.s_ld2[gp].inc(
                    s.dma_start(out=zgb[:, gp, :],
                                in_=cx.z_raw[g, :, c * CL:(c + 1) * CL]), 16)

            @b.vector
            def _(v, c=c, g=g, gp=gp):
                v.wait_ge(cx.s_ld2[gp].h, p2_sil[(c, g)])
                v.wait_ge(s_ev.h, p2_yev[(c, 2 * g + 1)])
                ssync(v, s_ev, v.tensor_tensor(y2[:, gp, :], ygc[:, gp, :], zgb[:, gp, :], ALU.mult))
                p2_sq[(c, g)] = s_ev.inc(
                    v.tensor_tensor(y2q[:, gp, :], y2[:, gp, :], y2[:, gp, :], ALU.mult), 1)
                p2_y2[(c, g)] = p2_sq[(c, g)]

            if g % 2 == 1:
                @b.tensor
                def _(t, c=c, g=g):
                    t.wait_ge(s_ev.h, p2_sq[(c, g - 1)])
                    t.wait_ge(s_ev.h, p2_sq[(c, g)])
                    if (c, g - 2) in p2_rr:
                        t.wait_ge(s_sc.h, p2_rr[(c, g - 2)])
                        t.wait_ge(s_ev.h, p2_yn[(c, g - 2)])
                    elif c >= 1 and g == 1:
                        t.wait_ge(s_sc.h, p2_rr[(c - 1, NG - 1)])
                        t.wait_ge(s_ev.h, p2_yn[(c - 1, NG - 1)])
                    t.matmul(ps[:, 3072:3328], cx.ones_mat[:, :], y2q[:, 0, :],
                             start=True, stop=True)
                    p2_ms[(c, g)] = s_pe.inc(
                        t.matmul(ps[:, 3584:3840], cx.ones_mat[:, :], y2q[:, 1, :],
                                 start=True, stop=True), 1)

                @b.scalar
                def _(s, c=c, g=g):
                    s.wait_ge(s_pe.h, p2_ms[(c, g)])
                    lnin = ps.rearrange("p (a b) -> p a b", a=8)[:, 6:8, 0:256]
                    s.wait_ge(s_sc.h, s_sc.inc(
                        s.activation(ygc[:, :, :], lnin, AF.Ln,
                                     scale=1.0 / 128.0, bias=cx.eps_col[:, :]), 1))
                    p2_rr[(c, g)] = s_sc.inc(
                        s.activation(rstd[:, :, :], ygc[:, :, :], AF.Exp, scale=-0.5), 1)

                @b.vector
                def _(v, c=c, g=g, cpar=cpar):
                    v.wait_ge(s_sc.h, p2_rr[(c, g)])
                    v.scalar_tensor_tensor(ynormF[:, g - 1, cpar * CL:(cpar + 1) * CL],
                                           rstd[:, 0, :], cx.norm_sb[:, g - 1:g],
                                           y2[:, 0, :], ALU.mult, ALU.mult)
                    p2_yn[(c, g)] = s_ev.inc(
                        v.scalar_tensor_tensor(ynormF[:, g, cpar * CL:(cpar + 1) * CL],
                                               rstd[:, 1, :], cx.norm_sb[:, g:g + 1],
                                               y2[:, 1, :], ALU.mult, ALU.mult), 1)
                    p2_yn[(c, g - 1)] = p2_yn[(c, g)]

        if c % 2 == 1:
            g2 = c // 2
            pallv = ps.rearrange("p (a b) -> p a b", a=2)
            yn_done = p2_yn[(c, NG - 1)]
            pe_op = {}
            ev_op = {}
            wld = {}
            x1_ld = {}
            pe0 = s_pe.v
            kk = 0
            for jp in range(2):
                for hc in range(NG):
                    pe_op[(jp, hc)] = pe0 + kk + 1
                    kk += 1
            if not hasattr(cx, "_op_pe_last"):
                cx._op_pe_last = {}
            cx._op_pe_last[g2] = pe0 + kk

            @b.gpsimd
            def _(g_, g2=g2, c=c):
                g_.wait_ge(s_pe.h, p2_Y[(c, NH - 1)])
                for jp in range(2):
                    for hc in range(NG):
                        wpar = hc % 2
                        if hc >= 2:
                            g_.wait_ge(s_pe.h, pe_op[(jp, hc - 2)])
                        elif jp >= 1:
                            g_.wait_ge(s_pe.h, pe_op[(jp - 1, NG - 2 + hc)])
                        elif g2 >= 1:
                            g_.wait_ge(s_pe.h, cx._op_pe_last[g2 - 1])
                        else:
                            g_.wait_ge(cx.s_wcv[0].h, bf["wout_conv_done"][0])
                            g_.wait_ge(cx.s_wcv[1].h, bf["wout_conv_done"][1])
                        wld[(jp, hc)] = s_ldw[wpar].inc(
                            g_.dma_start(out=woutb[:, wpar, :], in_=cx.wout_bfd[hc]), 16)

            for jp in range(2):
                for tm in (2 * jp, 2 * jp + 1):
                    tpar = tm % 2

                    @b.sync
                    def _(sy, tm=tm, tpar=tpar, g2=g2):
                        if tm >= 2:
                            sy.wait_ge(s_ev.h, ev_op[tm - 2])
                            sy.wait_ge(s_st[tpar].h, op_st[(g2, tm - 2)])
                        elif g2 >= 1:
                            sy.wait_ge(s_ev.h, op_ev[g2 - 1])
                            sy.wait_ge(s_st[tpar].h, op_st[(g2 - 1, 2 + tm)])
                        else:
                            sy.wait_ge(s_ev.h, p2_yev[(1, NH - 1)])
                        row0 = g2 * 512 + tm * 128
                        x1_ld[tm] = s_ld[tpar].inc(
                            sy.dma_start(out=x1t[:, tpar, :], in_=cx.x1[row0:row0 + 128, :]), 16)

                @b.tensor
                def _(t, jp=jp, g2=g2, yn_done=yn_done):
                    if jp == 0:
                        t.wait_ge(s_ev.h, yn_done)
                        if g2 >= 1:
                            t.wait_ge(s_ev.h, op_ev[g2 - 1])
                    else:
                        t.wait_ge(s_ev.h, ev_op[0])
                        t.wait_ge(s_ev.h, ev_op[1])
                    for hc in range(NG):
                        wpar = hc % 2
                        t.wait_ge(s_ldw[wpar].h, wld[(jp, hc)])
                        for tm in (2 * jp, 2 * jp + 1):
                            tpar = tm % 2
                            for hb in range(NHB):
                                ins = t.matmul(pallv[:, tpar, hb * 512:(hb + 1) * 512],
                                               ynormF[:, hc, tm * 128:(tm + 1) * 128],
                                               woutb[:, wpar, hb * 512:(hb + 1) * 512],
                                               start=(hc == 0), stop=(hc == NG - 1))
                        val = s_pe.inc(ins, 1)
                        assert val == pe_op[(jp, hc)]

                for tm in (2 * jp, 2 * jp + 1):
                    tpar = tm % 2

                    @b.vector
                    def _(v, jp=jp, tm=tm, tpar=tpar, g2=g2):
                        v.wait_ge(s_pe.h, pe_op[(jp, NG - 1)])
                        v.wait_ge(s_ld[tpar].h, x1_ld[tm])
                        if tm >= 2:
                            v.wait_ge(s_st[tpar].h, op_st[(g2, tm - 2)])
                        ev_op[tm] = s_ev.inc(
                            v.tensor_tensor(x2t[:, tpar, :], pallv[:, tpar, :],
                                            x1t[:, tpar, :], ALU.add), 1)

                    @b.sync
                    def _(sy, tm=tm, tpar=tpar, g2=g2):
                        sy.wait_ge(s_ev.h, ev_op[tm])
                        row0 = g2 * 512 + tm * 128
                        op_st[(g2, tm)] = s_st[tpar].inc(
                            sy.dma_start(out=cx.x2[row0:row0 + 128, :], in_=x2t[:, tpar, :]), 16)
            op_ev[g2] = s_ev.v

    cx.attn_done_st = [s_st[0].v, s_st[1].v]


def emit_final_wait(cx):
    b = cx.block

    @b.sync
    def _(sy):
        sy.wait_ge(cx.s_st[0].h, cx.s_st[0].v)
        sy.wait_ge(cx.s_st[1].h, cx.s_st[1].v)


def kernel(**inputs):
    per_core = _prep_core_inputs(inputs)
    cxp = build_program()
    res = run_bass_kernel_spmd(cxp.nc, per_core, core_ids=list(range(NCORES)))
    outs = [res.results[c]["out"] for c in range(NCORES)]
    return np.concatenate(outs, axis=0).reshape(B, L, D)
